# version 06
# baseline (speedup 1.0000x reference)
import math
import contextlib
import numpy as np
import ml_dtypes

import concourse.bass as bass
import concourse.mybir as mybir
from concourse.bass_utils import run_bass_kernel_spmd


F32 = mybir.dt.float32
BF16 = mybir.dt.bfloat16
AF = mybir.ActivationFunctionType
ALU = mybir.AluOpType
AXX = mybir.AxisListType.X

E = 2048
CC = 512
NH = 16
DH = 128
LN_EPS = 1e-5
RMS_EPS = float(np.finfo(np.float32).eps)
QEPS = 1e-5
MAGIC = 12582912.0
INV12 = 1.0 / 12.0


class CFG:
    def __init__(self, S=4096, Q=2048, I8=8192, debug=False, stop=None,
                 dump_scratch=False):
        self.S, self.Q, self.I8 = S, Q, I8
        self.debug = debug
        self.stop = stop
        self.dump_scratch = dump_scratch
        self.n_k1 = S // 512
        self.n_k2 = Q // 512
        self.n_kt = S // 512
        self.n_qt = Q // 128
        self.n_grp = Q // 512
        self.OCG = I8 // 128
        self.ICG = I8 // 128
        self.OC16 = E // 128
        self.NSUB = I8 // 2048
        self.NQ = 2 * self.OCG + self.OC16 * self.NSUB


def _hadamard(n):
    h = np.array([[1.0]], np.float32)
    while h.shape[0] < n:
        h = np.block([[h, h], [h, -h]])
    return h


def _rope_perm():
    base = np.arange(E).reshape(NH, DH)
    ev, od = base[:, 0::2], base[:, 1::2]
    out = []
    for j in range(NH // 2):
        out.append(ev[2 * j]); out.append(ev[2 * j + 1])
        out.append(od[2 * j]); out.append(od[2 * j + 1])
    return np.concatenate(out)

ROPE_PERM = _rope_perm()


def _rope_tables(positions):
    freqs = 1.0 / (10000.0 ** (np.arange(0, DH, 2, dtype=np.float64) / DH))
    ang = positions[:, None].astype(np.float64) * freqs[None, :]
    cos, sin = np.cos(ang).T, np.sin(ang).T
    return (np.ascontiguousarray(np.concatenate([cos, cos], 0).astype(np.float32)),
            np.ascontiguousarray(np.concatenate([sin, sin], 0).astype(np.float32)))


def _tile_oc(w):
    O, K = w.shape
    t = w.reshape(O // 128, 128, K // 128, 128).transpose(0, 3, 2, 1)
    return np.ascontiguousarray(t.reshape(O // 128, 128, K))


class B:

    def __init__(self, cfg: CFG):
        self.cfg = cfg
        self.nc = bass.Bass("TRN2", target_bir_lowering=False, debug=False,
                            detect_race_conditions=False)
        self.es = contextlib.ExitStack()
        self.plan = {k: [] for k in ("sync", "scalar", "vector", "tensor", "gpsimd")}
        self.T = {}
        self._cm = {}
        self.bg = {}
        self.ug = {}
        self._haz = {}

    def em(self, eng, fn):
        self.plan[eng].append(fn)

    def wait(self, eng, sem, val):
        if val is None:
            return
        self.em(eng, lambda e, s=sem.h, v=val: e.wait_ge(s, v))

    _HAZ_ENGINES = ("vector", "scalar")

    @staticmethod
    def _ap_tname(a):
        t = getattr(a, "tensor", None)
        return getattr(t, "name", None) if t is not None else None

    def op(self, eng, meth, *args, sem=None, inc=1, **kw):
        if eng in self._HAZ_ENGINES:
            outs = set()
            ins_n = set()
            aps = [a for a in args if self._ap_tname(a)]
            if aps:
                outs.add(self._ap_tname(aps[0]))
                for a in aps[1:]:
                    ins_n.add(self._ap_tname(a))
            ao = kw.get("accum_out")
            if self._ap_tname(ao):
                outs.add(self._ap_tname(ao))
            for k in ("bias", "scale"):
                if self._ap_tname(kw.get(k)):
                    ins_n.add(self._ap_tname(kw.get(k)))
            hist = self._haz.setdefault(eng, [])
            recent = set().union(*hist) if hist else set()
            if ins_n & recent:
                self.em(eng, lambda e: e.drain())
                hist.clear()
            hist.append(outs)
            del hist[:-2]
        tgt = None
        semh = None
        if sem is not None:
            tgt = sem.bump(inc)
            semh = sem.h
        drain_sem = semh is not None and eng in self._HAZ_ENGINES and meth != "dma_start"

        def fn(e, meth=meth, args=args, kw=kw, semh=semh, inc=inc, ds=drain_sem):
            ins = getattr(e, meth)(*args, **kw)
            if semh is not None:
                if ds:
                    e.drain().then_inc(semh, inc)
                else:
                    ins.then_inc(semh, inc)
        self.em(eng, fn)
        if drain_sem and eng in self._HAZ_ENGINES:
            self._haz[eng].clear()
        return tgt

    def dma(self, eng, out, in_, sem):
        return self.op(eng, "dma_start", sem=sem, inc=16, out=out, in_=in_)

    def dma_fm(self, eng, dst, src, k, sem):
        in3 = src.rearrange("(k p) c -> p k c", p=128)
        out3 = dst.rearrange("p (k c) -> p k c", k=k)
        return self.dma(eng, out3, in3, sem)

    _uid = 0

    def sb(self, name, shape, dt):
        B._uid += 1
        cm = self.nc.sbuf_tensor(f"{name}__{B._uid}", list(shape), dt)
        t = cm.__enter__()
        assert name not in self._cm, name
        self._cm[name] = cm
        self.T[name] = t
        return t

    def ps(self, name, shape=(128, 512), dt=F32):
        cm = self.nc.psum_tensor(name, list(shape), dt)
        t = cm.__enter__()
        self._cm[name] = cm
        self.T[name] = t
        return t

    def free(self, *names):
        for n in names:
            self._cm.pop(n).__exit__(None, None, None)
            self.T.pop(n)

    def dram(self, name, shape, dt, kind="Internal"):
        if kind == "Internal" and self.cfg.dump_scratch:
            kind = "ExternalOutput"
        return self.nc.dram_tensor(name, list(shape), dt, kind=kind).ap()

    def sem(self, name):
        class S:
            pass
        s = S()
        s.h = self.es.enter_context(self.nc.semaphore(name))
        s.v = 0
        s.bump = lambda n, s=s: (setattr(s, "v", s.v + n), s.v)[1]
        return s

    def fence(self, tag):
        s = self.s
        vals = [(s[n], s[n].v) for n in
                ("ldg", "lds", "st", "tn", "vc", "sc", "gp", "fin")]
        for eng in ("sync", "scalar", "vector", "tensor", "gpsimd"):
            for sem, v in vals:
                if v:
                    self.wait(eng, sem, v)
        self.bg.clear()
        self.ug.clear()

    def pbb(self, idx):
        return self.pb[idx][:, :].bitcast(BF16)

    def bankw(self, idx):
        g = self.bg.get(idx)
        if g:
            self.wait("tensor", g[0], g[1])

    def bankr(self, idx, sem, val):
        self.bg[idx] = (sem, val)

    def bufw(self, eng, name):
        g = self.ug.get(name)
        if g:
            self.wait(eng, g[0], g[1])

    def bufr(self, name, sem, val):
        self.ug[name] = (sem, val)

    def build(self):
        cfg = self.cfg
        S, Q, I8 = cfg.S, cfg.Q, cfg.I8
        d = self.d = {}
        di = lambda n, sh, dt=F32: d.__setitem__(n, self.dram(n, sh, dt, "ExternalInput"))
        di("x", (S, E)); di("crow", (1, E)); di("g1r", (1, CC)); di("g2r", (1, CC))
        di("g3r", (1, E)); di("identf", (128, 128)); di("identb", (128, 128), BF16)
        di("h128", (128, 128), BF16); di("cosd", (128, S)); di("sind", (128, S))
        di("wmod", (E, 6 * E)); di("wdkv", (E, CC)); di("wdq", (E, CC))
        di("wqr", (CC, E)); di("wkr", (E, E)); di("wuq", (E, CC)); di("wuk", (E, CC))
        di("wuv", (E, CC)); di("wot", (E, E))
        di("gtiled", (cfg.OCG, 128, E)); di("utiled", (cfg.OCG, 128, E))
        di("dtiled", (cfg.OC16, 128, I8))
        d["y"] = self.dram("y", (Q, E), F32, "ExternalOutput")
        d["h_sc"] = self.dram("h_sc", (cfg.n_k1, 128, 16 * 512), BF16)
        d["kr_sc"] = self.dram("kr_sc", (cfg.n_k1, 128, 16 * 512), BF16)
        d["ckv_sc"] = self.dram("ckv_sc", (cfg.n_k1, 128, 4 * 512), BF16)
        d["xmid_sc"] = self.dram("xmid_sc", (Q // 128, 128, E), F32)
        d["xq1_sc"] = self.dram("xq1_sc", (cfg.n_grp, 128, 16 * 512), BF16)
        d["wo_sc"] = self.dram("wo_sc", (128, 4 * E), BF16)
        d["qr_sc"] = self.dram("qr_sc", (cfg.n_k2, 128, 16 * 512), BF16)
        d["aq_sc"] = self.dram("aq_sc", (cfg.n_k2, 128, 4 * 512), BF16)
        d["oa_sc"] = self.dram("oa_sc", (128, cfg.n_qt * 512), F32)
        d["l_sc"] = self.dram("l_sc", (128, cfg.n_qt), F32)
        if cfg.debug:
            d["dbg_p0"] = self.dram("dbg_p0", (128, 256), F32)
            d["dbg_m512"] = self.dram("dbg_m512", (128, 4 * CC), BF16)
            d["dbg_y"] = self.dram("dbg_y", (cfg.n_grp, 128, cfg.OCG * 512), BF16)
            d["dbg_x2"] = self.dram("dbg_x2", (cfg.n_grp, 128, cfg.OCG * 512), BF16)
            d["dbg_dn"] = self.dram("dbg_dn", (cfg.n_grp, 128, cfg.OC16 * 512), BF16)
            d["dbg_r"] = self.dram("dbg_r", (cfg.n_grp, 2, 512), F32)
        d["tg_sc"] = self.dram("tg_sc", (cfg.OCG, 128, E), BF16)
        d["tu_sc"] = self.dram("tu_sc", (cfg.OCG, 128, E), BF16)
        d["td_sc"] = self.dram("td_sc", (cfg.OC16, 128, I8), BF16)

        self.sb("s_idf", [128, 128], F32)
        self.sb("s_idb", [128, 128], BF16)
        self.sb("h128s", [128, 128], BF16)
        self.sb("mod_sb", [128, 96], F32)
        self.sb("gvec", [128, 8], F32)
        self.sb("ca1", [128, 32], F32)
        self.sb("gag3r", [1, E], F32)
        self.sb("wq", [128, 8], F32)
        self.sb("gbpf", [128, 16], F32)
        self.sb("wsum", [128, cfg.NQ], F32)
        self.sb("m512", [128, 4 * CC], BF16)
        self.sb("epsc", [128, 1], F32)
        self.sb("onesr", [1, 128], F32)
        self.sb("onesc", [128, 1], F32)
        self.sb("wrow", [1, 8], F32)
        self.sb("ropeA", [128, 512], F32)
        self.sb("ropeB", [128, 512], F32)
        self.sb("inv1", [1, Q], F32)
        for i in range(8):
            self.ps(f"pb{i}")
        self.pb = [self.T[f"pb{i}"] for i in range(8)]

        self.s = {n: self.sem(n) for n in
                  ("ldg", "lds", "st", "tn", "vc", "sc", "gp", "fin")}

        phases = [("p0", self.p0), ("k1a", self.k1a), ("k1b", self.k1b),
                  ("k2", self.k2), ("attn", self.attn), ("epi", self.epi),
                  ("ffn", self.ffn)]
        for name, fn in phases:
            fn()
            if cfg.stop == name:
                break

        with self.nc.Block() as blk:
            blk.sync(lambda e: [f(e) for f in self.plan["sync"]])
            blk.scalar(lambda e: [f(e) for f in self.plan["scalar"]])
            blk.vector(lambda e: [f(e) for f in self.plan["vector"]])
            blk.tensor(lambda e: [f(e) for f in self.plan["tensor"]])
            blk.gpsimd(lambda e: [f(e) for f in self.plan["gpsimd"]])
        self.es.close()
        return self.nc

    def p0(self):
        cfg, d, s, pb, T = self.cfg, self.d, self.s, self.pb, self.T

        self.dma("sync", T["s_idf"][:, :], d["identf"][:, :], s["lds"])
        self.dma("sync", T["s_idb"][:, :], d["identb"][:, :], s["lds"])
        self.dma("sync", T["h128s"][:, :], d["h128"][:, :], s["lds"])
        self.op("vector", "memset", self.T["epsc"][:, :], RMS_EPS)
        self.op("vector", "memset", self.T["onesr"][:, :], 1.0)
        self.op("vector", "memset", self.T["onesc"][:, :], 1.0)
        g123 = self.sb("g123", [1, 2 * CC + E], F32)
        csf = self.sb("csf", [128, 16], BF16)
        crow = self.sb("s_crow", [1, E], F32)
        csrow = self.sb("csrow", [1, E], F32)
        self.dma("sync", crow[:, :], d["crow"][:, :], s["lds"])
        self.dma("sync", g123[:, 0:CC], d["g1r"][:, :], s["lds"])
        self.dma("sync", g123[:, CC:2 * CC], d["g2r"][:, :], s["lds"])
        w_const = self.dma("sync", g123[:, 2 * CC:], d["g3r"][:, :], s["lds"])

        self.wait("scalar", s["lds"], w_const)
        sc0 = self.op("scalar", "activation", csrow[:, :], crow[:, :], AF.Silu, sem=s["sc"])
        self.wait("tensor", s["sc"], sc0)
        for j in range(16):
            self.op("tensor", "transpose", pb[0][:, j:j + 1],
                    csrow[0:1, j * 128:(j + 1) * 128], T["s_idf"][0:1, 0:1],
                    sem=(s["tn"] if j == 15 else None))
        t_csf = s["tn"].v
        self.wait("vector", s["tn"], t_csf)
        v_csf = self.op("vector", "tensor_copy", csf[:, :], pb[0][:, 0:16], sem=s["vc"])
        self.bankr(0, s["vc"], v_csf)

        self.wait("tensor", s["vc"], v_csf)
        for j in range(4):
            self.op("tensor", "transpose", pb[1][:, j:j + 1],
                    g123[0:1, j * 128:(j + 1) * 128], T["s_idf"][0:1, 0:1])
        for j in range(4):
            self.op("tensor", "transpose", pb[1][:, 4 + j:5 + j],
                    g123[0:1, CC + j * 128:CC + (j + 1) * 128], T["s_idf"][0:1, 0:1],
                    sem=(s["tn"] if j == 3 else None))
        t_g = s["tn"].v
        self.wait("vector", s["tn"], t_g)
        self.op("vector", "tensor_copy", T["gvec"][:, 0:4], pb[1][:, 0:4])
        v_g = self.op("vector", "tensor_scalar_mul", T["gvec"][:, 4:8], pb[1][:, 4:8],
                      INV12, sem=s["vc"])
        self.bankr(1, s["vc"], v_g)

        self.free("csrow", "s_crow")
        wuq = self.sb("s_wuq", [128, 16 * CC], BF16)
        wuk = self.sb("s_wuk", [128, 16 * CC], BF16)
        self.wait("gpsimd", s["tn"], t_g)
        self.dma_fm("gpsimd", wuq[:, :], d["wuq"][:, :], 16, s["ldg"])
        g_uk = self.dma_fm("gpsimd", wuk[:, :], d["wuk"][:, :], 16, s["ldg"])
        m512 = T["m512"]
        self.wait("tensor", s["ldg"], g_uk)
        for m in range(4):
            self.bankw(4)
            for k in range(16):
                self.op("tensor", "matmul", pb[4][:, :],
                        wuq[:, k * CC + m * 128:k * CC + (m + 1) * 128],
                        wuk[:, k * CC:(k + 1) * CC],
                        start=(k == 0), stop=(k == 15),
                        sem=(s["tn"] if k == 15 else None))
            self.wait("scalar", s["tn"], s["tn"].v)
            scm = self.op("scalar", "copy", m512[:, m * CC:(m + 1) * CC], pb[4][:, :],
                          sem=s["sc"])
            self.bankr(4, s["sc"], scm)
            t_m512_done = s["tn"].v
        self.free("s_wuk", "s_wuq")

        wuv = self.sb("s_wuv", [128, 16 * CC], BF16)
        wot = self.sb("s_wot", [128, 16 * E], BF16)
        self.wait("gpsimd", s["tn"], t_m512_done)
        self.dma_fm("gpsimd", wuv[:, :], d["wuv"][:, :], 16, s["ldg"])
        g_wo = self.dma_fm("gpsimd", wot[:, :], d["wot"][:, :], 16, s["ldg"])
        g2raw = self.sb("g2raw", [128, 4], F32)
        v_g2 = self.op("vector", "tensor_scalar_mul", g2raw[:, :], T["gvec"][:, 4:8], 12.0,
                       sem=s["vc"])
        self.wait("scalar", s["vc"], v_g2)
        wos = self.sb("wos", [128, 2 * 512], BF16)
        self.wait("tensor", s["ldg"], g_wo)
        st_wo = None
        for mn in range(16):
            m, n = divmod(mn, 4)
            bidx = 5 + mn % 2
            self.bankw(bidx)
            for k in range(16):
                self.op("tensor", "matmul", pb[bidx][:, :],
                        wuv[:, k * CC + m * 128:k * CC + (m + 1) * 128],
                        wot[:, k * E + n * 512:k * E + (n + 1) * 512],
                        start=(k == 0), stop=(k == 15),
                        sem=(s["tn"] if k == 15 else None))
            self.wait("scalar", s["tn"], s["tn"].v)
            self.bufw("scalar", f"wos{mn % 2}")
            scv = self.op("scalar", "activation", wos[:, (mn % 2) * 512:(mn % 2 + 1) * 512],
                          pb[bidx][:, :], AF.Copy, bias=0.0, scale=g2raw[:, m:m + 1],
                          sem=s["sc"])
            self.bankr(bidx, s["sc"], scv)
            self.wait("scalar", s["sc"], scv)
            st_wo = self.dma("scalar", d["wo_sc"][:, mn * 512:(mn + 1) * 512],
                             wos[:, (mn % 2) * 512:(mn % 2 + 1) * 512], s["st"])
            self.bufr(f"wos{mn % 2}", s["st"], st_wo)
        self.st_wo = st_wo
        t_wo_done = s["tn"].v
        self.free("wos", "g2raw", "s_wot", "s_wuv")

        wmodb = [self.sb(f"wmodb{i}", [128, 6 * E], BF16) for i in range(2)]
        self.wait("gpsimd", s["tn"], t_wo_done)
        self.wait("gpsimd", s["st"], st_wo)
        self.op("vector", "memset", T["mod_sb"][:, :], 0.0)
        for k in range(16):
            self.bufw("gpsimd", f"wmodb{k % 2}")
            wv = self.dma("gpsimd", wmodb[k % 2][:, :], d["wmod"][k * 128:(k + 1) * 128, :],
                          s["ldg"])
            self.wait("tensor", s["ldg"], wv)
            bidx = 2 + k % 2
            self.bankw(bidx)
            for j in range(96):
                self.op("tensor", "matmul", pb[bidx][:, j:j + 1],
                        wmodb[k % 2][:, j * 128:(j + 1) * 128], csf[:, k:k + 1],
                        start=True, stop=True,
                        sem=(s["tn"] if j == 95 else None))
            tk = s["tn"].v
            self.bufr(f"wmodb{k % 2}", s["tn"], tk)
            self.wait("vector", s["tn"], tk)
            v_acc = self.op("vector", "tensor_add", T["mod_sb"][:, :], T["mod_sb"][:, :],
                            pb[bidx][:, 0:96], sem=s["vc"])
            self.bankr(bidx, s["vc"], v_acc)
        v_mod = s["vc"].v

        self.op("vector", "tensor_scalar_add", T["ca1"][:, 0:16], T["mod_sb"][:, 16:32], 1.0)
        self.op("vector", "tensor_scalar_add", T["ca1"][:, 16:32], T["mod_sb"][:, 64:80], 1.0)
        gsc = self.sb("gsc", [128, 16], F32)
        self.op("vector", "tensor_single_scalar", gsc[:, :], T["mod_sb"][:, 80:96], 0.0, ALU.is_ge)
        v_sg = self.op("vector", "tensor_scalar", gsc[:, :], gsc[:, :], 2.0, -1.0,
                       ALU.mult, ALU.add, sem=s["vc"])
        self.wait("scalar", s["vc"], v_sg)
        sc_ab = self.op("scalar", "activation", T["gbpf"][:, :], T["mod_sb"][:, 80:96],
                        AF.Abs, sem=s["sc"])
        self.wait("vector", s["sc"], sc_ab)
        self.op("vector", "tensor_scalar_max", T["gbpf"][:, :], T["gbpf"][:, :], 0.25)
        self.op("vector", "tensor_mul", T["gbpf"][:, :], T["gbpf"][:, :], gsc[:, :])

        gab = self.sb("gab", [16, 128], F32)
        rowb = self.sb("rowb", [1, 2 * E], F32)
        self.wait("tensor", s["vc"], v_mod)
        self.bankw(3)
        t_ga = self.op("tensor", "transpose", pb[3][0:16, 0:128], T["mod_sb"][:, 32:48],
                       T["s_idf"][:, :], sem=s["tn"])
        self.wait("vector", s["tn"], t_ga)
        v_gab = self.op("vector", "tensor_copy", gab[:, :], pb[3][0:16, 0:128], sem=s["vc"])
        self.bankr(3, s["vc"], v_gab)
        self.wait("sync", s["vc"], v_gab)
        for j in range(16):
            w_gar = self.dma("sync", rowb[0:1, j * 128:(j + 1) * 128],
                             gab[j:j + 1, :], s["lds"])
        scr = rowb[0:1, E:2 * E]
        self.wait("vector", s["lds"], w_gar)
        self.op("vector", "tensor_single_scalar", scr, rowb[0:1, 0:E], 0.0, ALU.is_ge)
        v_s2 = self.op("vector", "tensor_scalar", scr, scr, 2.0, -1.0, ALU.mult, ALU.add,
                       sem=s["vc"])
        self.wait("scalar", s["vc"], v_s2)
        sc_a2 = self.op("scalar", "activation", rowb[0:1, 0:E], rowb[0:1, 0:E], AF.Abs,
                        sem=s["sc"])
        self.wait("vector", s["sc"], sc_a2)
        self.op("vector", "tensor_scalar_max", rowb[0:1, 0:E], rowb[0:1, 0:E], 0.25)
        self.op("vector", "tensor_mul", rowb[0:1, 0:E], rowb[0:1, 0:E], scr)
        self.op("vector", "tensor_mul", rowb[0:1, 0:E], rowb[0:1, 0:E], g123[:, 2 * CC:])
        self.v_gag3 = self.op("vector", "tensor_copy", T["gag3r"][0:1, :], rowb[0:1, 0:E],
                              sem=s["vc"])
        self.free("rowb", "gab", "gsc", "wmodb1", "wmodb0", "csf", "g123")

        if cfg.debug:
            self.wait("scalar", s["vc"], s["vc"].v)
            self.wait("scalar", s["sc"], s["sc"].v)
            self.dma("scalar", d["dbg_p0"][:, 0:96], T["mod_sb"][:, :], s["st"])
            self.dma("scalar", d["dbg_p0"][:, 96:128], T["ca1"][:, :], s["st"])
            self.dma("scalar", d["dbg_p0"][:, 128:136], T["gvec"][:, :], s["st"])
            self.dma("scalar", d["dbg_p0"][:, 140:156], T["gbpf"][:, :], s["st"])
            self.dma("scalar", d["dbg_m512"][:, :], T["m512"][:, :], s["st"])
        self.fence("p0")

        chunks = []
        for oc in range(cfg.OCG):
            chunks.append((d["gtiled"][oc], d["tg_sc"][oc], 0, oc))
        for oc in range(cfg.OCG):
            chunks.append((d["utiled"][oc], d["tu_sc"][oc], 1, cfg.OCG + oc))
        for oc in range(cfg.OC16):
            for j in range(cfg.NSUB):
                chunks.append((d["dtiled"][oc][:, j * 2048:(j + 1) * 2048],
                               d["td_sc"][oc][:, j * 2048:(j + 1) * 2048],
                               2, 2 * cfg.OCG + oc * cfg.NSUB + j))
        self.q_chunks = chunks
        mchunks = []
        for oc in range(0, cfg.OCG, 2):
            mchunks.append((d["gtiled"][oc:oc + 2].rearrange("a p e -> p a e"), len(mchunks)))
        for oc in range(0, cfg.OCG, 2):
            mchunks.append((d["utiled"][oc:oc + 2].rearrange("a p e -> p a e"), len(mchunks)))
        for oc in range(cfg.OC16):
            if cfg.NSUB >= 2:
                for j in range(0, cfg.NSUB, 2):
                    mchunks.append((d["dtiled"][oc][:, j * 2048:(j + 2) * 2048], len(mchunks)))
            elif oc % 2 == 0:
                mchunks.append((d["dtiled"][oc:oc + 2].rearrange("a p e -> p a e"),
                                len(mchunks)))
        self.m_chunks = mchunks
        self.m_cols = {"g": (0, cfg.OCG // 2),
                       "u": (cfg.OCG // 2, cfg.OCG),
                       "d": (cfg.OCG, len(mchunks))}
        self.m_idx = 0

    def _mean(self, n):
        cfg, d, s, T = self.cfg, self.d, self.s, self.T
        for _ in range(n):
            if self.m_idx >= len(self.m_chunks):
                return
            idx = self.m_idx
            self.m_idx += 1
            ap, col = self.m_chunks[idx]
            b = idx % 2
            self.bufw("gpsimd", f"wmb{b}")
            dst = self.T[f"wmb{b}"][:, :]
            if len(ap.shape) == 3:
                dst = dst.rearrange("p (a e) -> p a e", a=2)
            wv = self.dma("gpsimd", dst, ap, s["ldg"])
            if idx % 2 == 0:
                self.wait("vector", s["ldg"], wv)
                gv = self.op("vector", "tensor_reduce", T["wsum"][:, col:col + 1],
                             self.T[f"wmb{b}"][:, :], AXX, ALU.add, True, sem=s["vc"])
                self.bufr(f"wmb{b}", s["vc"], gv)
            else:
                self.wait("scalar", s["ldg"], wv)
                sv = self.op("scalar", "activation", T["mabs"][:, :],
                             self.T[f"wmb{b}"][:, :], AF.Abs,
                             accum_out=T["wsum"][:, col:col + 1], sem=s["sc"])
                self.bufr(f"wmb{b}", s["sc"], sv)

    def _ln(self, xs, st_, sqs, xw_scalar=None, last=False):
        s = self.s
        self.op("vector", "reduce_sum", st_[:, 0:1], xs, AXX)
        if xw_scalar is not None:
            self.wait("scalar", s["lds"], xw_scalar)
        for bq in range(4):
            self.op("scalar", "activation", sqs[:, :], xs[:, bq * 512:(bq + 1) * 512],
                    AF.Square, accum_out=st_[:, 1 + bq:2 + bq],
                    sem=(s["sc"] if bq == 3 else None))
        sq_done = s["sc"].v
        self.wait("vector", s["sc"], sq_done)
        self.op("vector", "tensor_scalar_mul", st_[:, 5:6], st_[:, 0:1], 1.0 / E)
        self.op("vector", "tensor_add", st_[:, 6:7], st_[:, 1:2], st_[:, 2:3])
        self.op("vector", "tensor_add", st_[:, 7:8], st_[:, 3:4], st_[:, 4:5])
        self.op("vector", "tensor_add", st_[:, 6:7], st_[:, 6:7], st_[:, 7:8])
        self.op("vector", "tensor_mul", st_[:, 7:8], st_[:, 5:6], st_[:, 5:6])
        self.op("vector", "tensor_scalar", st_[:, 6:7], st_[:, 6:7], 1.0 / E,
                None, ALU.mult)
        self.op("vector", "tensor_sub", st_[:, 6:7], st_[:, 6:7], st_[:, 7:8])
        self.op("vector", "tensor_scalar_add", st_[:, 6:7], st_[:, 6:7], LN_EPS)
        v_rc = self.op("vector", "reciprocal", st_[:, 7:8], st_[:, 6:7], sem=s["vc"])
        self.wait("scalar", s["vc"], v_rc)
        s_sq = self.op("scalar", "activation", st_[:, 8:9], st_[:, 7:8], AF.Sqrt,
                       sem=s["sc"])
        self.wait("vector", s["sc"], s_sq)
        return self.op("vector", "tensor_scalar", xs, xs, st_[:, 5:6], st_[:, 8:9],
                       ALU.subtract, ALU.mult, sem=(s["vc"] if last else None))

    def k1a(self):
        cfg, d, s, pb, T = self.cfg, self.d, self.s, self.pb, self.T
        self.sb("wmb0", [128, 4096], BF16)
        self.sb("wmb1", [128, 4096], BF16)
        self.sb("mabs", [128, 4096], BF16)
        wdkv = self.sb("s_wdkv", [128, 16 * CC], BF16)
        g_kv = self.dma_fm("gpsimd", wdkv[:, :], d["wdkv"][:, :], 16, s["ldg"])
        xa = [self.sb(f"xa{i}", [128, 4 * E], F32) for i in range(2)]
        hf = [self.sb(f"hf{i}", [128, 16 * 512], BF16) for i in range(2)]
        cvsb = self.sb("cvsb", [128, 4 * 512], BF16)
        ckvo = self.sb("ckvo", [128, 4 * 512], BF16)
        st_ = self.sb("k1st", [128, 32], F32)
        sqs = self.sb("sqs", [128, 512], F32)

        for i in range(cfg.n_k1):
            bi = i % 2
            self.bufw("sync", f"xa{bi}")
            xw = self.dma("sync", xa[bi][:, :].rearrange("p (s e) -> p s e", s=4),
                          d["x"][i * 512:(i + 1) * 512, :].rearrange("(s p) e -> p s e", p=128),
                          s["lds"])
            self.wait("vector", s["lds"], xw)
            if i == 0:
                self.wait("vector", s["vc"], self.v_gag3)
            for sub in range(4):
                xs = xa[bi][:, sub * E:(sub + 1) * E]
                self._ln(xs, st_, sqs, xw_scalar=(xw if sub == 0 else None),
                         last=(sub == 3))
            v_ln = s["vc"].v
            self.wait("tensor", s["vc"], v_ln)
            if i == 0:
                self.wait("tensor", s["ldg"], g_kv)
            self.bufw("tensor", f"hf{bi}")
            for c in range(16):
                bidx = c % 2
                self.bankw(bidx)
                for sub in range(4):
                    self.op("tensor", "transpose", pb[bidx][:, sub * 128:(sub + 1) * 128],
                            xa[bi][:, sub * E + c * 128:sub * E + (c + 1) * 128],
                            T["s_idf"][:, :], sem=(s["tn"] if sub == 3 else None))
                self.wait("vector", s["tn"], s["tn"].v)
                scv = self.op("vector", "tensor_scalar", hf[bi][:, c * 512:(c + 1) * 512],
                              pb[bidx][:, :], T["ca1"][:, c:c + 1], T["mod_sb"][:, c:c + 1],
                              ALU.mult, ALU.add, sem=s["vc"])
                self.bankr(bidx, s["vc"], scv)
            sc_h = s["vc"].v
            self.wait("scalar", s["vc"], sc_h)
            sth = self.dma("scalar", d["h_sc"][i], hf[bi][:, :], s["st"])
            self.wait("tensor", s["vc"], sc_h)
            for m in range(4):
                bidx = 2 + m % 2
                self.bankw(bidx)
                for k in range(16):
                    self.op("tensor", "matmul", pb[bidx][:, :],
                            wdkv[:, k * CC + m * 128:k * CC + (m + 1) * 128],
                            hf[bi][:, k * 512:(k + 1) * 512],
                            start=(k == 0), stop=(k == 15),
                            sem=(s["tn"] if k == 15 else None))
                self.wait("scalar", s["tn"], s["tn"].v)
                scv = self.op("scalar", "copy", cvsb[:, m * 512:(m + 1) * 512],
                              pb[bidx][:, :], sem=s["sc"])
                self.bankr(bidx, s["sc"], scv)
            sc_cv = s["sc"].v
            self.wait("tensor", s["sc"], sc_cv)
            self.bufw("vector", "ckvo")
            for sub in range(4):
                bidx = 4 + sub % 2
                self.bankw(bidx)
                for m in range(4):
                    self.op("tensor", "transpose", self.pbb(bidx)[:, m * 128:(m + 1) * 128],
                            cvsb[:, m * 512 + sub * 128:m * 512 + (sub + 1) * 128],
                            T["s_idb"][:, :], sem=(s["tn"] if m == 3 else None))
                t_tm = s["tn"].v
                self.wait("scalar", s["tn"], t_tm)
                s1 = self.op("scalar", "activation", sqs[:, :], self.pbb(bidx)[:, 0:512],
                             AF.Square, accum_out=st_[:, 29:30], sem=s["sc"])
                self.wait("scalar", s["sc"], s1)
                s2 = self.op("scalar", "activation", st_[:, 30:31], st_[:, 29:30], AF.Sqrt,
                             bias=self.T["epsc"][:, 0:1], scale=1.0 / CC, sem=s["sc"])
                self.wait("vector", s["sc"], s2)
                self.op("vector", "reciprocal", st_[:, 31:32], st_[:, 30:31])
                vv = self.op("vector", "tensor_scalar_mul", ckvo[:, sub * 512:(sub + 1) * 512],
                             self.pbb(bidx)[:, 0:512], st_[:, 31:32], sem=s["vc"])
                self.bankr(bidx, s["vc"], vv)
            v_cv = s["vc"].v
            self.wait("scalar", s["vc"], v_cv)
            stc = self.dma("scalar", d["ckv_sc"][i], ckvo[:, :], s["st"])
            tm = s["tn"].v
            self.bufr(f"xa{bi}", s["tn"], tm)
            self.bufr(f"hf{bi}", s["st"], sth)
            self.bufr("ckvo", s["st"], stc)
            self._mean(-(-len(self.m_chunks) // cfg.n_k1))
        self._mean(len(self.m_chunks))
        if cfg.debug:
            self.wait("scalar", s["gp"], s["gp"].v)
            self.wait("scalar", s["sc"], s["sc"].v)
            self.dma("scalar", d["dbg_p0"][:, 156:156 + min(64, len(self.m_chunks))],
                     T["wsum"][:, 0:min(64, len(self.m_chunks))], s["st"])
        self.fence("k1a")
        self.free("sqs", "k1st", "ckvo", "cvsb", "hf1", "hf0", "xa1", "xa0",
                  "s_wdkv", "mabs", "wmb1", "wmb0")

    def k1b(self):
        cfg, d, s, pb, T = self.cfg, self.d, self.s, self.pb, self.T
        wq = T["wq"]
        for j, mat in enumerate(("g", "u", "d")):
            lo, hi = self.m_cols[mat]
            v_r = self.op("vector", "tensor_reduce", wq[:, 3 + j:4 + j],
                          T["wsum"][:, lo:hi], AXX, ALU.add,
                          sem=(s["vc"] if j == 2 else None))
        wrow = T["wrow"]
        self.wait("tensor", s["vc"], v_r)
        self.bankw(4)
        t_ws = self.op("tensor", "matmul", pb[4][0:1, 0:3], T["onesc"][:, 0:1],
                       wq[:, 3:6], start=True, stop=True, sem=s["tn"])
        self.wait("scalar", s["tn"], t_ws)
        sc_w = self.op("scalar", "copy", wrow[0:1, 0:3], pb[4][0:1, 0:3], sem=s["sc"])
        self.bankr(4, s["sc"], sc_w)
        self.wait("vector", s["sc"], sc_w)
        nel = float(cfg.I8 * E)
        self.op("vector", "tensor_scalar", wrow[0:1, 0:3], wrow[0:1, 0:3],
                1.0 / nel, QEPS, ALU.mult, ALU.max)
        v_w2 = self.op("vector", "reciprocal", wrow[0:1, 3:6], wrow[0:1, 0:3], sem=s["vc"])
        self.wait("tensor", s["vc"], v_w2)
        self.bankw(4)
        t_wb = self.op("tensor", "matmul", pb[4][:, 0:6], T["onesr"][0:1, :],
                       wrow[0:1, 0:6], start=True, stop=True, sem=s["tn"])
        self.wait("scalar", s["tn"], t_wb)
        self.op("scalar", "copy", wq[:, 3:6], pb[4][:, 0:3])
        sc_wb = self.op("scalar", "copy", wq[:, 0:3], pb[4][:, 3:6], sem=s["sc"])
        self.bankr(4, s["sc"], sc_wb)
        self.wq_ready = sc_wb
        self.q_idx = 0

        cosd = self.sb("s_cos", [128, cfg.S], F32)
        sind = self.sb("s_sin", [128, cfg.S], F32)
        hb = self.sb("hb0", [128, 16 * 512], BF16)
        wkr = self.sb("s_wkr", [128, 16 * E], BF16)
        krf = [self.sb(f"krf{i}", [128, 16 * 512], BF16) for i in range(2)]
        self.qf = [self.sb(f"qf{i}", [128, 2048], F32) for i in range(2)]
        self.qt = [self.sb(f"qt{i}", [128, 2048], BF16) for i in range(2)]

        self.dma("sync", cosd[:, :], d["cosd"][:, :], s["lds"])
        w_tb = self.dma("sync", sind[:, :], d["sind"][:, :], s["lds"])
        g_kr = self.dma_fm("gpsimd", wkr[:, :], d["wkr"][:, :], 16, s["ldg"])

        nq_per = max(1, cfg.NQ // (2 * cfg.n_k1))
        for i in range(cfg.n_k1):
            bi = i % 2
            self.bufw("sync", "hb0")
            hw = self.dma("sync", hb[:, :], d["h_sc"][i], s["lds"])
            self.wait("tensor", s["lds"], hw)
            if i == 0:
                self.wait("tensor", s["ldg"], g_kr)
                self.wait("vector", s["lds"], w_tb)
            self.bufw("vector", f"krf{bi}")
            for grp in range(4):
                for u in range(4):
                    m = grp * 4 + u
                    bidx = m % 4
                    self.bankw(bidx)
                    for k in range(16):
                        self.op("tensor", "matmul", pb[bidx][:, :],
                                wkr[:, k * E + m * 128:k * E + (m + 1) * 128],
                                hb[:, k * 512:(k + 1) * 512],
                                start=(k == 0), stop=(k == 15),
                                sem=(s["tn"] if k == 15 else None))
                t_grp = s["tn"].v
                self.wait("vector", s["tn"], t_grp)
                v_rp = self._rope4(grp * 4, [pb[0], pb[1], pb[2], pb[3]], krf[bi], 512,
                                   cosd[:, i * 512:(i + 1) * 512],
                                   sind[:, i * 512:(i + 1) * 512])
                for bidx in range(4):
                    self.bankr(bidx, s["vc"], v_rp)
            v_kr = s["vc"].v
            self.bufr("hb0", s["tn"], s["tn"].v)
            self.wait("scalar", s["vc"], v_kr)
            stk = self.dma("scalar", d["kr_sc"][i], krf[bi][:, :], s["st"])
            self.bufr(f"krf{bi}", s["st"], stk)
            self._quant(nq_per)
        self._quant((cfg.NQ + 1) // 2 - self.q_idx)
        self.fence("k1b")
        self.free("qt1", "qt0", "qf1", "qf0", "krf1", "krf0", "s_wkr", "hb0",
                  "s_sin", "s_cos")

    def _rope4(self, base, banks, out, outw, cosap, sinap):
        T = self.T
        ret = None
        for u in range(2):
            x1 = banks[2 * u][:, :]
            x2 = banks[2 * u + 1][:, :]
            o1 = out[:, (base + 2 * u) * outw:(base + 2 * u + 1) * outw]
            o2 = out[:, (base + 2 * u + 1) * outw:(base + 2 * u + 2) * outw]
            t1 = T["ropeA"][:, :]
            t2 = T["ropeB"][:, :]
            self.op("vector", "tensor_mul", t1, x1, cosap)
            self.op("vector", "tensor_mul", t2, x2, sinap)
            self.op("vector", "tensor_sub", o1, t1, t2)
            self.op("vector", "tensor_mul", t1, x1, sinap)
            self.op("vector", "tensor_mul", t2, x2, cosap)
            ret = self.op("vector", "tensor_add", o2, t1, t2,
                          sem=(self.s["vc"] if u == 1 else None))
        return ret

    def bcast(self, row, out_sb, n, bank, wait_vc=None):
        s, pb, T = self.s, self.pb, self.T
        if wait_vc is not None:
            self.wait("tensor", s["vc"], wait_vc)
        sc = None
        for j0 in range(0, n, 512):
            w = min(512, n - j0)
            self.bankw(bank)
            t = self.op("tensor", "matmul", pb[bank][:, 0:w], T["onesr"][0:1, :],
                        row[0:1, j0:j0 + w], start=True, stop=True, sem=s["tn"])
            self.wait("scalar", s["tn"], t)
            sc = self.op("scalar", "copy", out_sb[:, j0:j0 + w], pb[bank][:, 0:w],
                         sem=s["sc"])
            self.bankr(bank, s["sc"], sc)
        return sc

    def _rowmax(self, mx, mxc, tmp4, outrow, bank=7):
        s, pb, T = self.s, self.pb, self.T
        self.bankw(bank)
        for j in range(4):
            self.op("tensor", "transpose", pb[bank][:, j * 128:(j + 1) * 128],
                    mx[:, j * 128:(j + 1) * 128], T["s_idf"][:, :],
                    sem=(s["tn"] if j == 3 else None))
        t_tr = s["tn"].v
        self.wait("vector", s["tn"], t_tr)
        for j in range(4):
            self.op("vector", "reduce_max", mxc[:, j:j + 1],
                    pb[bank][:, j * 128:(j + 1) * 128], AXX,
                    sem=(s["vc"] if j == 3 else None))
        v_mc = s["vc"].v
        self.bankr(bank, s["vc"], v_mc)
        self.wait("tensor", s["vc"], v_mc)
        self.bankw(bank)
        t2 = self.op("tensor", "transpose", pb[bank][0:4, 0:128], mxc[:, 0:4],
                     T["s_idf"][:, :], sem=s["tn"])
        self.wait("vector", s["tn"], t2)
        v_t4 = self.op("vector", "tensor_copy", tmp4[:, :], pb[bank][0:4, 0:128],
                       sem=s["vc"])
        self.bankr(bank, s["vc"], v_t4)
        self.wait("sync", s["vc"], v_t4)
        w = None
        for j in range(4):
            w = self.dma("sync", outrow[0:1, j * 128:(j + 1) * 128], tmp4[j:j + 1, :],
                         s["lds"])
        return w

    def _quant(self, n):
        cfg, d, s = self.cfg, self.d, self.s
        wq = self.T["wq"]
        for _ in range(n):
            if self.q_idx >= len(self.q_chunks):
                return
            idx = self.q_idx
            self.q_idx += 1
            src, dst, j, _ = self.q_chunks[idx]
            b = idx % 2
            if idx == 0:
                self.wait("vector", s["sc"], self.wq_ready)
            self.bufw("sync", f"qf{b}")
            wv = self.dma("sync", self.qf[b][:, :], src, s["lds"])
            self.wait("vector", s["lds"], wv)
            self.bufw("vector", f"qt{b}")
            f, t = self.qf[b], self.qt[b]
            self.op("vector", "tensor_scalar", f[:, :], f[:, :], wq[:, j:j + 1], MAGIC,
                    ALU.mult, ALU.add)
            self.op("vector", "tensor_scalar", f[:, :], f[:, :], MAGIC, 1.0,
                    ALU.subtract, ALU.min)
            qv = self.op("vector", "tensor_scalar_max", t[:, :], f[:, :], -1.0, sem=s["vc"])
            self.bufr(f"qf{b}", s["vc"], qv)
            self.wait("scalar", s["vc"], qv)
            stq = self.dma("scalar", dst, t[:, :], s["st"])
            self.bufr(f"qt{b}", s["st"], stq)

    def k2(self):
        cfg, d, s, pb, T = self.cfg, self.d, self.s, self.pb, self.T
        cosd = self.sb("s_cos", [128, cfg.Q], F32)
        sind = self.sb("s_sin", [128, cfg.Q], F32)
        hb = self.sb("hb0", [128, 16 * 512], BF16)
        wdq = self.sb("s_wdq", [128, 16 * CC], BF16)
        wqr = self.sb("s_wqr", [128, 4 * E], BF16)
        qrf = self.sb("qrf", [128, 16 * 512], BF16)
        aqf = self.sb("aqf", [128, 4 * 512], BF16)
        tbf = self.sb("tbf", [128, 4 * 512], BF16)
        cqb = self.sb("cqb", [128, 4 * 512], BF16)
        rrep = self.sb("rrep", [128, 512], F32)
        sqs2 = self.sb("sqs2", [128, 512], F32)
        rows = self.sb("rows", [1, 1024], F32)
        ones = self.sb("ones", [128, 1], F32)

        self.dma("sync", cosd[:, :], d["cosd"][:, 0:cfg.Q], s["lds"])
        w_tb = self.dma("sync", sind[:, :], d["sind"][:, 0:cfg.Q], s["lds"])
        g_dq = self.dma_fm("gpsimd", wdq[:, :], d["wdq"][:, :], 16, s["ldg"])
        g_qr = self.dma_fm("gpsimd", wqr[:, :], d["wqr"][:, :], 4, s["ldg"])
        self.wait("vector", s["ldg"], g_qr)
        for k in range(4):
            self.op("vector", "tensor_scalar_mul", wqr[:, k * E:(k + 1) * E],
                    wqr[:, k * E:(k + 1) * E], T["gvec"][:, k:k + 1],
                    sem=(s["vc"] if k == 3 else None))
        v_wqr = s["vc"].v
        gp_o = None
        v_ones = self.op("vector", "memset", ones[:, :], 1.0, sem=s["vc"])
        m512 = T["m512"]

        for i in range(cfg.n_k2):
            self.bufw("sync", "hb0")
            hw = self.dma("sync", hb[:, :], d["h_sc"][i], s["lds"])
            self.wait("tensor", s["lds"], hw)
            if i == 0:
                self.wait("tensor", s["ldg"], g_dq)
                self.wait("tensor", s["vc"], v_ones)
                self.wait("tensor", s["vc"], v_wqr)
                self.wait("vector", s["lds"], w_tb)
            for m in range(4):
                self.bankw(m)
                for k in range(16):
                    self.op("tensor", "matmul", pb[m][:, :],
                            wdq[:, k * CC + m * 128:k * CC + (m + 1) * 128],
                            hb[:, k * 512:(k + 1) * 512],
                            start=(k == 0), stop=(k == 15),
                            sem=(s["tn"] if (k == 15 and m == 3) else None))
            t_t = s["tn"].v
            self.wait("scalar", s["tn"], t_t)
            self.bufw("scalar", "tbf")
            for m in range(4):
                self.op("scalar", "copy", tbf[:, m * 512:(m + 1) * 512], pb[m][:, :],
                        sem=(s["sc"] if m == 3 else None))
            sc_t = s["sc"].v
            for m in range(4):
                if m > 0:
                    self.wait("scalar", s["tn"], t_sqm)
                sq = self.op("scalar", "activation", sqs2[:, :], pb[m][:, :], AF.Square,
                             sem=s["sc"])
                self.wait("tensor", s["sc"], sq)
                if m == 0:
                    self.bankw(4)
                t_sqm = self.op("tensor", "matmul", pb[4][0:1, :], ones[:, 0:1], sqs2[:, :],
                                start=(m == 0), stop=(m == 3), sem=s["tn"])
            t_sq = s["tn"].v
            self.wait("scalar", s["tn"], t_sq)
            s_den = self.op("scalar", "activation", rows[0:1, 0:512], pb[4][0:1, :], AF.Sqrt,
                            bias=self.T["epsc"][0:1, 0:1], scale=1.0 / CC, sem=s["sc"])
            self.bankr(4, s["sc"], s_den)
            self.wait("vector", s["sc"], s_den)
            self.op("vector", "reciprocal", rows[0:1, 512:1024], rows[0:1, 0:512])
            v_rr = self.op("vector", "tensor_scalar_mul", rows[0:1, 512:1024],
                           rows[0:1, 512:1024], INV12, sem=s["vc"])
            sc_b = self.bcast(rows[0:1, 512:1024], rrep, 512, 4, wait_vc=v_rr)
            self.wait("vector", s["sc"], sc_b)
            self.bufw("vector", "cqb")
            for m in range(4):
                self.op("vector", "tensor_mul", cqb[:, m * 512:(m + 1) * 512],
                        pb[m][:, :], rrep[:, :], sem=(s["vc"] if m == 3 else None))
            v_cq = s["vc"].v
            for m in range(4):
                self.bankr(m, s["vc"], v_cq)
            self.wait("tensor", s["sc"], sc_t)
            self.bufw("scalar", "aqf")
            for m in range(4):
                self.bankw(5)
                for k in range(4):
                    self.op("tensor", "matmul", pb[5][:, :],
                            m512[:, k * CC + m * 128:k * CC + (m + 1) * 128],
                            tbf[:, k * 512:(k + 1) * 512],
                            start=(k == 0), stop=(k == 3),
                            sem=(s["tn"] if k == 3 else None))
                self.wait("scalar", s["tn"], s["tn"].v)
                sca = self.op("scalar", "activation", aqf[:, m * 512:(m + 1) * 512],
                              pb[5][:, :], AF.Copy, bias=0.0, scale=T["gvec"][:, 4 + m:5 + m],
                              sem=s["sc"])
                self.bankr(5, s["sc"], sca)
            sc_aq = s["sc"].v
            self.wait("scalar", s["sc"], sc_aq)
            st_aq = self.dma("scalar", d["aq_sc"][i], aqf[:, :], s["st"])
            self.bufr("aqf", s["st"], st_aq)
            self.bufr("tbf", s["tn"], s["tn"].v)
            self.wait("tensor", s["vc"], v_cq)
            self.bufw("vector", "qrf")
            for grp in range(4):
                for u in range(4):
                    m = grp * 4 + u
                    bidx = m % 4
                    self.bankw(bidx)
                    for k in range(4):
                        self.op("tensor", "matmul", pb[bidx][:, :],
                                wqr[:, k * E + m * 128:k * E + (m + 1) * 128],
                                cqb[:, k * 512:(k + 1) * 512],
                                start=(k == 0), stop=(k == 3),
                                sem=(s["tn"] if k == 3 else None))
                t_q = s["tn"].v
                self.wait("vector", s["tn"], t_q)
                v_rp = self._rope4(grp * 4, [pb[0], pb[1], pb[2], pb[3]], qrf, 512,
                                   cosd[:, i * 512:(i + 1) * 512],
                                   sind[:, i * 512:(i + 1) * 512])
                for bb in range(4):
                    self.bankr(bb, s["vc"], v_rp)
            v_qr = s["vc"].v
            self.wait("scalar", s["vc"], v_qr)
            st_qr = self.dma("scalar", d["qr_sc"][i], qrf[:, :], s["st"])
            self.bufr("qrf", s["st"], st_qr)
            self.bufr("hb0", s["tn"], s["tn"].v)
        self.fence("k2")
        self.free("ones", "rows", "sqs2", "rrep", "cqb", "tbf", "aqf", "qrf",
                  "s_wqr", "s_wdq", "hb0", "s_sin", "s_cos")

    def attn(self):
        cfg, d, s, pb, T = self.cfg, self.d, self.s, self.pb, self.T
        Q, n_qt = cfg.Q, cfg.n_qt
        qr = self.sb("qr", [128, 16 * Q], BF16)
        aq = self.sb("aq", [128, 4 * Q], BF16)
        oa = self.sb("oa", [128, n_qt * 512], F32)
        ml = self.sb("ml", [128, 4 * n_qt], F32)
        krb = self.sb("krb0", [128, 16 * 512], BF16)
        ckvb = [self.sb(f"ckvb{i}", [128, 4 * 512], BF16) for i in range(2)]
        cfm = self.sb("cfm", [128, 4 * 512], BF16)
        pbf = self.sb("pbf", [128, 512], BF16)
        ptb = self.sb("ptb", [128, 512], BF16)
        prow = self.sb("prow", [128, 2], F32)
        self.qf = [self.sb(f"qf{i}", [128, 2048], F32) for i in range(2)]
        self.qt = [self.sb(f"qt{i}", [128, 2048], BF16) for i in range(2)]
        identb = T["s_idb"]

        qr3 = qr[:, :].rearrange("p (m q) -> p m q", m=16)
        for i in range(cfg.n_k2):
            self.dma("sync", qr3[:, :, i * 512:(i + 1) * 512],
                     d["qr_sc"][i].rearrange("p (m t) -> p m t", m=16), s["lds"])
        aq3 = aq[:, :].rearrange("p (m q) -> p m q", m=4)
        for i in range(cfg.n_k2):
            self.dma("sync", aq3[:, :, i * 512:(i + 1) * 512],
                     d["aq_sc"][i].rearrange("p (m t) -> p m t", m=4), s["lds"])
        w_qa = s["lds"].v

        self.op("vector", "memset", ml[:, 0:n_qt], -30000.0)
        self.op("vector", "memset", ml[:, n_qt:2 * n_qt], 0.0)
        v_i = self.op("vector", "memset", oa[:, :], 0.0, sem=s["vc"])
        self.wait("tensor", s["vc"], v_i)
        self.wait("tensor", s["lds"], w_qa)

        for kt in range(cfg.n_kt):
            b = kt % 2
            self.bufw("sync", "krb0")
            self.bufw("sync", f"ckvb{b}")
            krw = self.dma("sync", krb[:, :], d["kr_sc"][kt], s["lds"])
            cvw = self.dma("sync", ckvb[b][:, :], d["ckv_sc"][kt], s["lds"])
            self.wait("tensor", s["lds"], cvw)
            self.bufw("tensor", "cfm")
            for m in range(4):
                bidx = 6 + m % 2
                self.bankw(bidx)
                for sub in range(4):
                    self.op("tensor", "transpose", self.pbb(bidx)[:, sub * 128:(sub + 1) * 128],
                            ckvb[b][:, sub * 512 + m * 128:sub * 512 + (m + 1) * 128],
                            identb[:, :], sem=(s["tn"] if sub == 3 else None))
                self.wait("scalar", s["tn"], s["tn"].v)
                scv = self.op("scalar", "copy", cfm[:, m * 512:(m + 1) * 512],
                              self.pbb(bidx)[:, 0:512], sem=s["sc"])
                self.bankr(bidx, s["sc"], scv)
            sc_cfm = s["sc"].v
            self.wait("tensor", s["sc"], sc_cfm)
            self.wait("tensor", s["lds"], krw)

            for qt in range(n_qt):
                sb_i = qt % 2
                self.bankw(sb_i)
                for m in range(4):
                    self.op("tensor", "matmul", pb[sb_i][:, :],
                            aq[:, m * Q + qt * 128:m * Q + (qt + 1) * 128],
                            cfm[:, m * 512:(m + 1) * 512],
                            start=(m == 0), stop=False)
                for m in range(16):
                    self.op("tensor", "matmul", pb[sb_i][:, :],
                            qr[:, m * Q + qt * 128:m * Q + (qt + 1) * 128],
                            krb[:, m * 512:(m + 1) * 512],
                            start=False, stop=(m == 15),
                            sem=(s["tn"] if m == 15 else None))
                t_s = s["tn"].v
                mo = ml[:, qt:qt + 1]
                lo = ml[:, n_qt + qt:n_qt + qt + 1]
                mn = ml[:, 2 * n_qt + qt:2 * n_qt + qt + 1]
                co = ml[:, 3 * n_qt + qt:3 * n_qt + qt + 1]
                self.wait("vector", s["tn"], t_s)
                self.bufw("vector", "prow")
                self.op("vector", "reduce_max", prow[:, 0:1], pb[sb_i][:, :], AXX)
                self.op("vector", "tensor_max", mn, prow[:, 0:1], mo)
                v_m = self.op("vector", "tensor_scalar_mul", prow[:, 1:2], mn, -1.0,
                              sem=s["vc"])
                self.wait("scalar", s["vc"], v_m)
                self.op("scalar", "activation", co, mo, AF.Exp, bias=prow[:, 1:2], scale=1.0)
                self.bufw("scalar", "pbf")
                sc_p = self.op("scalar", "activation", pbf[:, :], pb[sb_i][:, :], AF.Exp,
                               bias=prow[:, 1:2], scale=1.0, accum_out=prow[:, 0:1],
                               sem=s["sc"])
                self.bankr(sb_i, s["sc"], sc_p)
                self.bufr("prow", s["sc"], sc_p)
                self.wait("vector", s["sc"], sc_p)
                self.op("vector", "tensor_copy", mo, mn)
                v_l = self.op("vector", "scalar_tensor_tensor", lo, lo, co, prow[:, 0:1],
                              ALU.mult, ALU.add, sem=s["vc"])
                pt_i = 2 + qt % 2
                self.wait("tensor", s["sc"], sc_p)
                self.bankw(pt_i)
                for sub in range(4):
                    self.op("tensor", "transpose", self.pbb(pt_i)[:, sub * 128:(sub + 1) * 128],
                            pbf[:, sub * 128:(sub + 1) * 128], identb[:, :],
                            sem=(s["tn"] if sub == 3 else None))
                t_pt = s["tn"].v
                self.wait("scalar", s["tn"], t_pt)
                self.bufw("scalar", "ptb")
                sc_pt = self.op("scalar", "copy", ptb[:, :], self.pbb(pt_i)[:, 0:512], sem=s["sc"])
                self.bankr(pt_i, s["sc"], sc_pt)
                self.bufr("pbf", s["tn"], t_pt)
                o_i = 4 + qt % 2
                self.wait("tensor", s["sc"], sc_pt)
                self.bankw(o_i)
                for sub in range(4):
                    self.op("tensor", "matmul", pb[o_i][:, :], ptb[:, sub * 128:(sub + 1) * 128],
                            ckvb[b][:, sub * 512:(sub + 1) * 512],
                            start=(sub == 0), stop=(sub == 3),
                            sem=(s["tn"] if sub == 3 else None))
                t_o = s["tn"].v
                self.bufr("ptb", s["tn"], t_o)
                self.wait("vector", s["tn"], t_o)
                self.wait("vector", s["vc"], v_l)
                v_o = self.op("vector", "scalar_tensor_tensor", oa[:, qt * 512:(qt + 1) * 512],
                              oa[:, qt * 512:(qt + 1) * 512], co, pb[o_i][:, :],
                              ALU.mult, ALU.add, sem=s["vc"])
                self.bankr(o_i, s["vc"], v_o)
            t_kt = s["tn"].v
            self.bufr("krb0", s["tn"], t_kt)
            self.bufr(f"ckvb{b}", s["tn"], t_kt)
            self.bufr("cfm", s["tn"], t_kt)
            left = cfg.NQ - self.q_idx
            self._quant(max(1, left // max(1, cfg.n_kt - kt)))
        self._quant(cfg.NQ - self.q_idx)
        v_last = s["vc"].v
        self.wait("scalar", s["vc"], v_last)
        self.dma("scalar", d["oa_sc"][:, :], oa[:, :], s["st"])
        self.dma("scalar", d["l_sc"][:, :], ml[:, n_qt:2 * n_qt], s["st"])
        self.fence("attn")
        self.free("qt1", "qt0", "qf1", "qf0", "prow", "ptb", "pbf", "cfm",
                  "ckvb1", "ckvb0", "krb0", "ml", "oa", "aq", "qr")

    def epi(self):
        cfg, d, s, pb, T = self.cfg, self.d, self.s, self.pb, self.T
        Q, n_qt = cfg.Q, cfg.n_qt
        identb, identf = T["s_idb"], T["s_idf"]

        oa = self.sb("oa", [128, n_qt * 512], F32)
        lv = self.sb("lv", [128, 2 * n_qt], F32)
        gag3 = self.sb("gag3", [128, E], F32)
        wob = self.sb("wob", [128, 4 * E], BF16)
        obf = self.sb("obf", [128, 512], BF16)
        ofm = self.sb("ofm", [128, 512], BF16)
        usb = self.sb("usb", [128, E], F32)
        sqs = self.sb("sqs3", [128, 512], F32)
        ust = self.sb("ust", [128, 8], F32)
        xg = self.sb("xg", [128, 4 * E], F32)
        st_ = self.sb("epst", [128, 32], F32)
        h2f = self.sb("h2f", [128, 16 * 512], BF16)
        zb = self.sb("zb", [128, 16 * 512], F32)
        mx = self.sb("mx", [128, 512], F32)
        mnb = self.sb("mnb", [128, 512], F32)
        qrow = self.sb("qrow", [1, 3 * 512], F32)
        mxc = self.sb("mxc", [128, 4], F32)
        tmp4 = self.sb("tmp4", [4, 128], F32)
        screp = self.sb("screp", [128, 512], F32)
        xq1 = self.sb("xq1", [128, 16 * 512], BF16)

        w_oa = self.dma("sync", oa[:, :], d["oa_sc"][:, :], s["lds"])
        w_l = self.dma("sync", lv[:, 0:n_qt], d["l_sc"][:, :], s["lds"])
        sc_g = self.bcast(T["gag3r"][0:1, :], gag3, E, 7, wait_vc=self.v_gag3)
        self.wait("sync", s["st"], self.st_wo)
        wo_w = self.dma("sync", wob[:, :], d["wo_sc"][:, :], s["lds"])
        self.wait("vector", s["lds"], w_l)
        self.wait("vector", s["lds"], w_oa)
        for qt in range(n_qt):
            self.op("vector", "reciprocal", lv[:, n_qt + qt:n_qt + qt + 1],
                    lv[:, qt:qt + 1], sem=(s["vc"] if qt == n_qt - 1 else None))

        for g in range(cfg.n_grp):
            self.bufw("sync", "xg")
            xw = self.dma("sync", xg[:, :].rearrange("p (s e) -> p s e", s=4),
                          d["x"][g * 512:(g + 1) * 512, :].rearrange("(s p) e -> p s e", p=128),
                          s["lds"])
            for j in range(4):
                qt = g * 4 + j
                self.bufw("vector", "obf")
                v_ob = self.op("vector", "tensor_scalar_mul", obf[:, :],
                               oa[:, qt * 512:(qt + 1) * 512],
                               lv[:, n_qt + qt:n_qt + qt + 1], sem=s["vc"])
                self.wait("tensor", s["vc"], v_ob)
                self.bankw(0)
                for m in range(4):
                    self.op("tensor", "transpose", self.pbb(0)[:, m * 128:(m + 1) * 128],
                            obf[:, m * 128:(m + 1) * 128], identb[:, :],
                            sem=(s["tn"] if m == 3 else None))
                t_of = s["tn"].v
                self.bufr("obf", s["tn"], t_of)
                self.wait("scalar", s["tn"], t_of)
                self.bufw("scalar", "ofm")
                sc_of = self.op("scalar", "copy", ofm[:, :], self.pbb(0)[:, 0:512], sem=s["sc"])
                self.bankr(0, s["sc"], sc_of)
                self.wait("tensor", s["sc"], sc_of)
                if g == 0 and j == 0:
                    self.wait("tensor", s["lds"], wo_w)
                for n in range(4):
                    bidx = 1 + n % 2
                    self.bankw(bidx)
                    for m in range(4):
                        self.op("tensor", "matmul", pb[bidx][:, :],
                                ofm[:, m * 128:(m + 1) * 128],
                                wob[:, m * E + n * 512:m * E + (n + 1) * 512],
                                start=(m == 0), stop=(m == 3),
                                sem=(s["tn"] if m == 3 else None))
                    t_u = s["tn"].v
                    self.wait("scalar", s["tn"], t_u)
                    self.op("scalar", "copy", usb[:, n * 512:(n + 1) * 512], pb[bidx][:, :])
                    sc_u = self.op("scalar", "activation", sqs[:, :], pb[bidx][:, :],
                                   AF.Square, accum_out=ust[:, n:n + 1], sem=s["sc"])
                    self.bankr(bidx, s["sc"], sc_u)
                sc_un = s["sc"].v
                self.bufr("ofm", s["tn"], s["tn"].v)
                self.wait("vector", s["sc"], sc_un)
                self.op("vector", "tensor_add", ust[:, 4:5], ust[:, 0:1], ust[:, 1:2])
                self.op("vector", "tensor_add", ust[:, 5:6], ust[:, 2:3], ust[:, 3:4])
                v_sq = self.op("vector", "tensor_add", ust[:, 4:5], ust[:, 4:5], ust[:, 5:6],
                               sem=s["vc"])
                self.wait("scalar", s["vc"], v_sq)
                sden = self.op("scalar", "activation", ust[:, 6:7], ust[:, 4:5], AF.Sqrt,
                               bias=self.T["epsc"][:, 0:1], scale=1.0 / E, sem=s["sc"])
                self.wait("vector", s["sc"], sden)
                self.op("vector", "reciprocal", ust[:, 7:8], ust[:, 6:7])
                self.wait("vector", s["lds"], xw)
                if g == 0 and j == 0:
                    self.wait("vector", s["sc"], sc_g)
                xs = xg[:, j * E:(j + 1) * E]
                self.op("vector", "tensor_scalar_mul", usb[:, :], usb[:, :], ust[:, 7:8])
                self.op("vector", "tensor_mul", usb[:, :], usb[:, :], gag3[:, :])
                v_xm = self.op("vector", "tensor_add", xs, xs, usb[:, :], sem=s["vc"])
                self.wait("scalar", s["vc"], v_xm)
                st_xm = self.dma("scalar", d["xmid_sc"][g * 4 + j], xs, s["st"])
                self.wait("vector", s["st"], st_xm)
                self.wait("scalar", s["st"], st_xm)
                self._ln(xs, st_, sqs, last=(j == 3))
            v_ln = s["vc"].v
            self.wait("tensor", s["vc"], v_ln)
            for c in range(16):
                bidx = 3 + c % 2
                self.bankw(bidx)
                for sub in range(4):
                    self.op("tensor", "transpose", pb[bidx][:, sub * 128:(sub + 1) * 128],
                            xg[:, sub * E + c * 128:sub * E + (c + 1) * 128],
                            identf[:, :], sem=(s["tn"] if sub == 3 else None))
                self.wait("vector", s["tn"], s["tn"].v)
                scv = self.op("vector", "tensor_scalar", h2f[:, c * 512:(c + 1) * 512],
                              pb[bidx][:, :], T["ca1"][:, 16 + c:17 + c],
                              T["mod_sb"][:, 48 + c:49 + c],
                              ALU.mult, ALU.add, sem=s["vc"])
                self.bankr(bidx, s["vc"], scv)
            sc_h2 = s["vc"].v
            self.bufr("xg", s["tn"], s["tn"].v)
            self.wait("tensor", s["vc"], sc_h2)
            for c in range(16):
                bidx = 5 + c % 2
                self.bankw(bidx)
                self.op("tensor", "matmul", pb[bidx][:, :], T["h128s"][:, :],
                        h2f[:, c * 512:(c + 1) * 512], start=True, stop=True, sem=s["tn"])
                self.wait("scalar", s["tn"], s["tn"].v)
                scz = self.op("scalar", "copy", zb[:, c * 512:(c + 1) * 512], pb[bidx][:, :],
                              sem=s["sc"])
                self.bankr(bidx, s["sc"], scz)
            sc_z = s["sc"].v
            self.wait("vector", s["sc"], sc_z)
            h = 1
            while h < 16:
                for base in range(0, 16, 2 * h):
                    for u in range(h):
                        a = zb[:, (base + u) * 512:(base + u + 1) * 512]
                        bt = zb[:, (base + u + h) * 512:(base + u + h + 1) * 512]
                        self.op("vector", "tensor_add", a, a, bt)
                        self.op("vector", "scalar_tensor_tensor", bt, bt, -2.0, a,
                                ALU.mult, ALU.add)
                h *= 2
            self.op("vector", "tensor_max", mx[:, :], zb[:, 0:512], zb[:, 512:1024])
            self.op("vector", "tensor_tensor", mnb[:, :], zb[:, 0:512], zb[:, 512:1024],
                    ALU.min)
            for c in range(2, 16):
                self.op("vector", "tensor_max", mx[:, :], mx[:, :],
                        zb[:, c * 512:(c + 1) * 512])
                self.op("vector", "tensor_tensor", mnb[:, :], mnb[:, :],
                        zb[:, c * 512:(c + 1) * 512], ALU.min)
            self.op("vector", "tensor_scalar_mul", mnb[:, :], mnb[:, :], -1.0)
            self.op("vector", "tensor_max", mx[:, :], mx[:, :], mnb[:, :], sem=s["vc"])
            self.wait("tensor", s["vc"], s["vc"].v)
            w_mr = self._rowmax(mx, mxc, tmp4, qrow[0:1, 1024:1536], bank=7)
            self.wait("vector", s["lds"], w_mr)
            isq = 1.0 / math.sqrt(E)
            self.op("vector", "tensor_scalar", qrow[0:1, 0:512], qrow[0:1, 1024:1536],
                    isq, QEPS, ALU.mult, ALU.max)
            self.op("vector", "tensor_scalar_mul", T["inv1"][0:1, g * 512:(g + 1) * 512],
                    qrow[0:1, 0:512], 1.0 / 127.0)
            self.op("vector", "reciprocal", qrow[0:1, 512:1024],
                    T["inv1"][0:1, g * 512:(g + 1) * 512])
            v_qs = self.op("vector", "tensor_scalar_mul", qrow[0:1, 512:1024],
                           qrow[0:1, 512:1024], isq, sem=s["vc"])
            self.bufw("tensor", "screp")
            sc_sc = self.bcast(qrow[0:1, 512:1024], screp, 512, 7, wait_vc=v_qs)
            self.wait("vector", s["sc"], sc_sc)
            self.bufw("vector", "xq1")
            for c in range(16):
                zc = zb[:, c * 512:(c + 1) * 512]
                self.op("vector", "tensor_mul", zc, zc, screp[:, :])
                self.op("vector", "tensor_scalar", zc, zc, MAGIC, MAGIC, ALU.add, ALU.subtract)
                self.op("vector", "tensor_scalar", xq1[:, c * 512:(c + 1) * 512], zc,
                        127.0, -127.0, ALU.min, ALU.max,
                        sem=(s["vc"] if c == 15 else None))
            v_x1 = s["vc"].v
            self.bufr("screp", s["vc"], v_x1)
            self.wait("scalar", s["vc"], v_x1)
            st_x1 = self.dma("scalar", d["xq1_sc"][g], xq1[:, :], s["st"])
            self.bufr("xq1", s["st"], st_x1)
        self.fence("epi")
        self.free("xq1", "screp", "tmp4", "mxc", "qrow", "mnb", "mx", "zb", "h2f", "epst",
                  "xg", "ust", "sqs3", "usb", "ofm", "obf", "wob", "gag3", "lv", "oa")

    def ffn(self):
        cfg, d, s, pb, T = self.cfg, self.d, self.s, self.pb, self.T
        Q, I8 = cfg.Q, cfg.I8
        OCG, OC16, ICG = cfg.OCG, cfg.OC16, cfg.ICG
        xq1s = self.sb("xq1s", [128, 16 * 512], BF16)
        yb = self.sb("yb", [128, OCG * 512], BF16)
        tg = [self.sb(f"tg{i}", [128, E], BF16) for i in range(2)]
        tu = [self.sb(f"tu{i}", [128, E], BF16) for i in range(2)]
        i1r = self.sb("i1r", [128, 512], F32)
        gf = self.sb("gf", [128, 512], F32)
        sf = self.sb("sf", [128, 512], F32)
        uf = self.sb("uf", [128, 512], F32)
        mx2 = self.sb("mx2", [128, 512], F32)
        mxc2 = self.sb("mxc2", [128, 4], F32)
        tmp42 = self.sb("tmp42", [4, 128], F32)
        q2row = self.sb("q2row", [1, 3 * 512], F32)
        scr2 = self.sb("scr2", [128, 512], F32)
        i2r = self.sb("i2r", [128, 512], F32)
        gbsd = self.sb("gbsd", [128, 16], F32)
        wqT = T["wq"]

        v_gb = self.op("vector", "tensor_scalar_mul", gbsd[:, :], T["gbpf"][:, :],
                       wqT[:, 5:6], sem=s["vc"])

        for fs in range(cfg.n_grp):
            self.bufw("sync", "xq1s")
            x1w = self.dma("sync", xq1s[:, :], d["xq1_sc"][fs], s["lds"])
            self.bufw("tensor", "i1r")
            sc_i1 = self.bcast(T["inv1"][0:1, fs * 512:(fs + 1) * 512], i1r, 512, 7)
            self.wait("vector", s["sc"], sc_i1)
            self.wait("tensor", s["lds"], x1w)
            self.bufw("vector", "yb")
            for oc in range(OCG):
                b = oc % 2
                self.bufw("sync", f"tg{b}")
                self.bufw("sync", f"tu{b}")
                tgw = self.dma("sync", tg[b][:, :], d["tg_sc"][oc], s["lds"])
                tuw = self.dma("sync", tu[b][:, :], d["tu_sc"][oc], s["lds"])
                self.wait("tensor", s["lds"], tgw)
                g_i = oc % 2
                self.bankw(g_i)
                for k in range(16):
                    self.op("tensor", "matmul", pb[g_i][:, :],
                            tg[b][:, k * 128:(k + 1) * 128],
                            xq1s[:, k * 512:(k + 1) * 512],
                            start=(k == 0), stop=(k == 15),
                            sem=(s["tn"] if k == 15 else None))
                t_gm = s["tn"].v
                self.wait("tensor", s["lds"], tuw)
                u_i = 2 + oc % 2
                self.bankw(u_i)
                for k in range(16):
                    self.op("tensor", "matmul", pb[u_i][:, :],
                            tu[b][:, k * 128:(k + 1) * 128],
                            xq1s[:, k * 512:(k + 1) * 512],
                            start=(k == 0), stop=(k == 15),
                            sem=(s["tn"] if k == 15 else None))
                t_um = s["tn"].v
                self.bufr(f"tg{b}", s["tn"], t_gm)
                self.bufr(f"tu{b}", s["tn"], t_um)
                self.wait("vector", s["tn"], t_um)
                self.bufw("vector", "gf")
                self.op("vector", "scalar_tensor_tensor", gf[:, :], pb[g_i][:, :],
                        wqT[:, 3:4], i1r[:, :], ALU.mult, ALU.mult, sem=s["vc"])
                v_gf = s["vc"].v
                self.bankr(g_i, s["vc"], v_gf)
                self.wait("scalar", s["vc"], v_gf)
                sc_sf = self.op("scalar", "activation", sf[:, :], gf[:, :], AF.Silu,
                                sem=s["sc"])
                self.bufw("vector", "uf")
                v_uf = self.op("vector", "scalar_tensor_tensor", uf[:, :], pb[u_i][:, :],
                               wqT[:, 4:5], i1r[:, :], ALU.mult, ALU.mult, sem=s["vc"])
                self.bankr(u_i, s["vc"], v_uf)
                self.wait("vector", s["sc"], sc_sf)
                v_y = self.op("vector", "tensor_mul", yb[:, oc * 512:(oc + 1) * 512],
                              sf[:, :], uf[:, :], sem=s["vc"])
                self.bufr("gf", s["sc"], sc_sf)
                self.bufr("uf", s["vc"], v_y)
                last_v_y = v_y
                last_t_um = t_um
            self.bufr("xq1s", s["tn"], last_t_um)
            self.bufr("i1r", s["vc"], last_v_y)
            if cfg.debug:
                self.wait("scalar", s["vc"], last_v_y)
                self.dma("scalar", d["dbg_y"][fs], yb[:, :], s["st"])
                self.wait("tensor", s["st"], s["st"].v)
            for oc in range(OCG):
                bidx = 4 + oc % 2
                self.bankw(bidx)
                self.op("tensor", "matmul", pb[bidx][:, :], T["h128s"][:, :],
                        yb[:, oc * 512:(oc + 1) * 512], start=True, stop=True, sem=s["tn"])
                self.wait("scalar", s["tn"], s["tn"].v)
                scz = self.op("scalar", "copy", yb[:, oc * 512:(oc + 1) * 512],
                              pb[bidx][:, :], sem=s["sc"])
                self.bankr(bidx, s["sc"], scz)
            sc_za = s["sc"].v
            self.wait("vector", s["sc"], sc_za)
            h = 1
            while h < OCG:
                for base in range(0, OCG, 2 * h):
                    for u in range(h):
                        a = yb[:, (base + u) * 512:(base + u + 1) * 512]
                        bt = yb[:, (base + u + h) * 512:(base + u + h + 1) * 512]
                        self.op("vector", "tensor_add", a, a, bt)
                        self.op("vector", "scalar_tensor_tensor", bt, bt, -2.0, a,
                                ALU.mult, ALU.add)
                h *= 2
            self.op("vector", "tensor_max", mx2[:, :], yb[:, 0:512], yb[:, 512:1024])
            self.op("vector", "tensor_tensor", sf[:, :], yb[:, 0:512], yb[:, 512:1024],
                    ALU.min)
            for c in range(2, OCG):
                self.op("vector", "tensor_max", mx2[:, :], mx2[:, :],
                        yb[:, c * 512:(c + 1) * 512])
                self.op("vector", "tensor_tensor", sf[:, :], sf[:, :],
                        yb[:, c * 512:(c + 1) * 512], ALU.min)
            self.op("vector", "tensor_scalar_mul", sf[:, :], sf[:, :], -1.0)
            self.op("vector", "tensor_max", mx2[:, :], mx2[:, :], sf[:, :], sem=s["vc"])
            self.wait("tensor", s["vc"], s["vc"].v)
            w_mr = self._rowmax(mx2, mxc2, tmp42, q2row[0:1, 1024:1536], bank=7)
            self.wait("vector", s["lds"], w_mr)
            isq = 1.0 / math.sqrt(I8)
            self.op("vector", "tensor_scalar", q2row[0:1, 0:512], q2row[0:1, 1024:1536],
                    isq, QEPS, ALU.mult, ALU.max)
            self.op("vector", "tensor_scalar_mul", q2row[0:1, 512:1024],
                    q2row[0:1, 0:512], 1.0 / 127.0)
            self.op("vector", "reciprocal", q2row[0:1, 0:512], q2row[0:1, 512:1024])
            v_q2 = self.op("vector", "tensor_scalar_mul", q2row[0:1, 0:512],
                           q2row[0:1, 0:512], isq, sem=s["vc"])
            self.bufw("tensor", "scr2")
            self.bufw("tensor", "i2r")
            self.bcast(q2row[0:1, 0:512], scr2, 512, 6, wait_vc=v_q2)
            sc_q2 = self.bcast(q2row[0:1, 512:1024], i2r, 512, 7)
            self.wait("vector", s["sc"], sc_q2)
            for c in range(OCG):
                yc = yb[:, c * 512:(c + 1) * 512]
                self.op("vector", "tensor_mul", gf[:, :], yc, scr2[:, :])
                self.op("vector", "tensor_scalar", gf[:, :], gf[:, :], MAGIC, MAGIC,
                        ALU.add, ALU.subtract)
                self.op("vector", "tensor_scalar", yc, gf[:, :], 127.0, -127.0,
                        ALU.min, ALU.max, sem=(s["vc"] if c == OCG - 1 else None))
            v_xq2 = s["vc"].v
            self.bufr("scr2", s["vc"], v_xq2)
            if cfg.debug:
                self.wait("scalar", s["vc"], v_xq2)
                self.dma("scalar", d["dbg_x2"][fs], yb[:, :], s["st"])
                self.dma("scalar", d["dbg_r"][fs][0:1, :], q2row[0:1, 0:512], s["st"])
                self.dma("scalar", d["dbg_r"][fs][1:2, :], q2row[0:1, 512:1024], s["st"])
                self.wait("tensor", s["st"], s["st"].v)
            self.wait("tensor", s["vc"], v_xq2)
            td = [self.T.get("td0") or self.sb("td0", [128, I8], BF16),
                  self.T.get("td1") or self.sb("td1", [128, I8], BF16)]
            dn = self.T.get("dn") or self.sb("dn", [128, OC16 * 512], BF16)
            self.bufw("vector", "dn")
            for oc in range(OC16):
                b = oc % 2
                self.bufw("sync", f"td{b}")
                tdw = self.dma("sync", td[b][:, :], d["td_sc"][oc], s["lds"])
                self.wait("tensor", s["lds"], tdw)
                d_i = 6 + oc % 2
                self.bankw(d_i)
                for k in range(ICG):
                    self.op("tensor", "matmul", pb[d_i][:, :],
                            td[b][:, k * 128:(k + 1) * 128],
                            yb[:, k * 512:(k + 1) * 512],
                            start=(k == 0), stop=(k == ICG - 1),
                            sem=(s["tn"] if k == ICG - 1 else None))
                t_d = s["tn"].v
                self.bufr(f"td{b}", s["tn"], t_d)
                self.wait("vector", s["tn"], t_d)
                if fs == 0 and oc == 0:
                    self.wait("vector", s["vc"], v_gb)
                v_d = self.op("vector", "scalar_tensor_tensor", dn[:, oc * 512:(oc + 1) * 512],
                              pb[d_i][:, :], gbsd[:, oc:oc + 1], i2r[:, :],
                              ALU.mult, ALU.mult, sem=s["vc"])
                self.bankr(d_i, s["vc"], v_d)
                last_t_d = t_d
            v_dn = s["vc"].v
            self.bufr("i2r", s["vc"], v_dn)
            self.bufr("yb", s["tn"], last_t_d)
            if cfg.debug:
                self.wait("scalar", s["vc"], v_dn)
                self.dma("scalar", d["dbg_dn"][fs], dn[:, :], s["st"])
                self.wait("tensor", s["st"], s["st"].v)
            xm = self.T.get("xm") or self.sb("xm", [128, E], F32)
            ob = self.T.get("ob") or self.sb("ob", [128, E], F32)
            self.wait("tensor", s["vc"], v_dn)
            for tgp in range(4):
                self.bufw("sync", "xm")
                xmw = self.dma("sync", xm[:, :], d["xmid_sc"][fs * 4 + tgp], s["lds"])
                for m in range(4):
                    bidx = m % 4
                    self.bankw(bidx)
                    for u in range(4):
                        oc = m * 4 + u
                        self.op("tensor", "transpose", self.pbb(bidx)[:, u * 128:(u + 1) * 128],
                                dn[:, oc * 512 + tgp * 128:oc * 512 + (tgp + 1) * 128],
                                T["s_idb"][:, :],
                                sem=(s["tn"] if u == 3 else None))
                t_tr = s["tn"].v
                self.bufr("dn", s["tn"], t_tr)
                self.wait("vector", s["tn"], t_tr)
                self.wait("vector", s["lds"], xmw)
                self.bufw("vector", "ob")
                for m in range(4):
                    self.op("vector", "tensor_add", ob[:, m * 512:(m + 1) * 512],
                            xm[:, m * 512:(m + 1) * 512], self.pbb(m)[:, 0:512],
                            sem=(s["vc"] if m == 3 else None))
                v_ob = s["vc"].v
                for m in range(4):
                    self.bankr(m, s["vc"], v_ob)
                self.bufr("xm", s["vc"], v_ob)
                self.wait("scalar", s["vc"], v_ob)
                st_o = self.dma("scalar", d["y"][(fs * 4 + tgp) * 128:(fs * 4 + tgp + 1) * 128, :],
                                ob[:, :], s["fin"])
                self.bufr("ob", s["fin"], st_o)
        self.wait("scalar", s["fin"], s["fin"].v)



def _prep_core_inputs(inputs, cfg, core):
    b, hf = core // 2, core % 2
    S, Q = cfg.S, cfg.Q
    x_full = np.asarray(inputs["L0_attn_h"][b], np.float32)
    own = x_full[hf * Q:(hf + 1) * Q]
    oth = x_full[(1 - hf) * Q:(2 - hf) * Q]
    x = np.ascontiguousarray(np.concatenate([own, oth], 0))
    pos = np.concatenate([np.arange(hf * Q, (hf + 1) * Q),
                          np.arange((1 - hf) * Q, (2 - hf) * Q)])
    cosd, sind = _rope_tables(pos)
    return x, cosd, sind


def _shared_arrays(inputs, cfg):
    f32 = lambda a: np.ascontiguousarray(np.asarray(a, np.float32))
    bf = lambda a: np.ascontiguousarray(np.asarray(a).astype(ml_dtypes.bfloat16))
    sh = {}
    sh["crow_all"] = f32(inputs["L0_c"])
    sh["g1r"] = f32(inputs["L0_attn_g1"])[None, :]
    sh["g2r"] = f32(inputs["L0_attn_g2"])[None, :]
    sh["g3r"] = f32(inputs["L0_attn_g3"])[None, :]
    sh["identf"] = np.eye(128, dtype=np.float32)
    sh["identb"] = np.eye(128).astype(ml_dtypes.bfloat16)
    sh["h128"] = bf(_hadamard(128))
    sh["wmod"] = f32(inputs["L0_wmod"])
    sh["wdkv"] = f32(inputs["L0_attn_w_d_kv"].T)
    sh["wdq"] = f32(inputs["L0_attn_w_d_q"].T)
    sh["wqr"] = f32(np.asarray(inputs["L0_attn_w_qr"])[ROPE_PERM].T)
    sh["wkr"] = f32(np.asarray(inputs["L0_attn_w_kr"])[ROPE_PERM].T)
    sh["wuq"] = f32(inputs["L0_attn_w_u_q"])
    sh["wuk"] = f32(inputs["L0_attn_w_u_k"])
    sh["wuv"] = f32(inputs["L0_attn_w_u_v"])
    sh["wot"] = f32(inputs["L0_attn_w_o"].T)
    sh["gtiled"] = _tile_oc(f32(inputs["L0_ffn_w_gate"]))
    sh["utiled"] = _tile_oc(f32(inputs["L0_ffn_w_up"]))
    sh["dtiled"] = _tile_oc(f32(inputs["L0_ffn_w_down"]))
    return sh


def kernel(**inputs):
    cfg = CFG()
    sh = _shared_arrays(inputs, cfg)
    nc = B(cfg).build()
    in_maps = []
    for core in range(8):
        b = core // 2
        x, cosd, sind = _prep_core_inputs(inputs, cfg, core)
        m = {k: sh[k] for k in sh if k != "crow_all"}
        m["crow"] = sh["crow_all"][b][None, :]
        m["x"] = x
        m["cosd"] = cosd
        m["sind"] = sind
        in_maps.append(m)
    res = run_bass_kernel_spmd(nc, in_maps, core_ids=list(range(8)))
    out = np.empty((4, 2 * cfg.Q, E), np.float32)
    for core in range(8):
        b, hf = core // 2, core % 2
        out[b, hf * cfg.Q:(hf + 1) * cfg.Q] = np.asarray(res.results[core]["y"])
    return out


# version 12
# speedup vs baseline: 1.1846x; 1.1846x over previous
import math
import contextlib
import numpy as np
import ml_dtypes

import concourse.bass as bass
import concourse.mybir as mybir
from concourse.bass_utils import run_bass_kernel_spmd


F32 = mybir.dt.float32
BF16 = mybir.dt.bfloat16
AF = mybir.ActivationFunctionType
ALU = mybir.AluOpType
AXX = mybir.AxisListType.X

E = 2048
CC = 512
NH = 16
DH = 128
LN_EPS = 1e-5
RMS_EPS = float(np.finfo(np.float32).eps)
QEPS = 1e-5
MAGIC = 12582912.0
INV12 = 1.0 / 12.0


class CFG:
    def __init__(self, S=4096, Q=2048, I8=8192, debug=False, stop=None,
                 dump_scratch=False):
        self.S, self.Q, self.I8 = S, Q, I8
        self.debug = debug
        self.stop = stop
        self.dump_scratch = dump_scratch
        self.n_k1 = S // 512
        self.n_k2 = Q // 512
        self.n_kt = S // 512
        self.n_qt = Q // 128
        self.n_grp = Q // 512
        self.OCG = I8 // 128
        self.ICG = I8 // 128
        self.OC16 = E // 128
        self.NSUB = I8 // 2048
        self.NQ = 2 * self.OCG + self.OC16 * self.NSUB


def _hadamard(n):
    h = np.array([[1.0]], np.float32)
    while h.shape[0] < n:
        h = np.block([[h, h], [h, -h]])
    return h


def _rope_perm():
    base = np.arange(E).reshape(NH, DH)
    ev, od = base[:, 0::2], base[:, 1::2]
    out = []
    for j in range(NH // 2):
        out.append(ev[2 * j]); out.append(ev[2 * j + 1])
        out.append(od[2 * j]); out.append(od[2 * j + 1])
    return np.concatenate(out)

ROPE_PERM = _rope_perm()


def _rope_tables(positions):
    freqs = 1.0 / (10000.0 ** (np.arange(0, DH, 2, dtype=np.float64) / DH))
    ang = positions[:, None].astype(np.float64) * freqs[None, :]
    cos, sin = np.cos(ang).T, np.sin(ang).T
    return (np.ascontiguousarray(np.concatenate([cos, cos], 0).astype(np.float32)),
            np.ascontiguousarray(np.concatenate([sin, sin], 0).astype(np.float32)))


def _tile_oc(w):
    O, K = w.shape
    t = w.reshape(O // 128, 128, K // 128, 128).transpose(0, 3, 2, 1)
    return np.ascontiguousarray(t.reshape(O // 128, 128, K))


class B:

    def __init__(self, cfg: CFG):
        self.cfg = cfg
        self.nc = bass.Bass("TRN2", target_bir_lowering=False, debug=False,
                            detect_race_conditions=False)
        self.es = contextlib.ExitStack()
        self.plan = {k: [] for k in ("sync", "scalar", "vector", "tensor", "gpsimd")}
        self.T = {}
        self._cm = {}
        self.bg = {}
        self.ug = {}
        self._haz = {}

    def em(self, eng, fn):
        self.plan[eng].append(fn)

    def wait(self, eng, sem, val):
        if val is None:
            return
        self.em(eng, lambda e, s=sem.h, v=val: e.wait_ge(s, v))

    _HAZ_ENGINES = ("vector", "scalar")

    @staticmethod
    def _ap_tname(a):
        t = getattr(a, "tensor", None)
        return getattr(t, "name", None) if t is not None else None

    @staticmethod
    def _ap_elems(a):
        try:
            sh = a.shape
            n = 1
            for d in list(sh)[1:]:
                n *= int(d)
            return n
        except Exception:
            return 1

    def op(self, eng, meth, *args, sem=None, inc=1, **kw):
        if eng in self._HAZ_ENGINES:
            outs = set()
            ins_n = set()
            aps = [a for a in args if self._ap_tname(a)]
            elems = 1
            if aps:
                outs.add(self._ap_tname(aps[0]))
                elems = self._ap_elems(aps[0])
                for a in aps[1:]:
                    ins_n.add(self._ap_tname(a))
            ao = kw.get("accum_out")
            if self._ap_tname(ao):
                outs.add(self._ap_tname(ao))
            for k in ("bias", "scale"):
                if self._ap_tname(kw.get(k)):
                    ins_n.add(self._ap_tname(kw.get(k)))
            hist = self._haz.setdefault(eng, [])
            need = False
            if hist and (ins_n & hist[-1][0]):
                need = True
            elif len(hist) >= 2 and (ins_n & hist[-2][0]) and hist[-1][1] < 256:
                need = True
            if need:
                self.em(eng, lambda e: e.drain())
                hist.clear()
            hist.append((outs, elems))
            del hist[:-2]
        tgt = None
        semh = None
        if sem is not None:
            tgt = sem.bump(inc)
            semh = sem.h
        drain_sem = semh is not None and eng in self._HAZ_ENGINES and meth != "dma_start"

        def fn(e, meth=meth, args=args, kw=kw, semh=semh, inc=inc, ds=drain_sem):
            ins = getattr(e, meth)(*args, **kw)
            if semh is not None:
                if ds:
                    e.drain().then_inc(semh, inc)
                else:
                    ins.then_inc(semh, inc)
        self.em(eng, fn)
        if drain_sem:
            self._haz[eng].clear()
        return tgt

    def dma(self, eng, out, in_, sem):
        return self.op(eng, "dma_start", sem=sem, inc=16, out=out, in_=in_)

    def dma_fm(self, eng, dst, src, k, sem):
        in3 = src.rearrange("(k p) c -> p k c", p=128)
        out3 = dst.rearrange("p (k c) -> p k c", k=k)
        return self.dma(eng, out3, in3, sem)

    _uid = 0

    def sb(self, name, shape, dt):
        B._uid += 1
        cm = self.nc.sbuf_tensor(f"{name}__{B._uid}", list(shape), dt)
        t = cm.__enter__()
        assert name not in self._cm, name
        self._cm[name] = cm
        self.T[name] = t
        return t

    def ps(self, name, shape=(128, 512), dt=F32):
        cm = self.nc.psum_tensor(name, list(shape), dt)
        t = cm.__enter__()
        self._cm[name] = cm
        self.T[name] = t
        return t

    def free(self, *names):
        for n in names:
            self._cm.pop(n).__exit__(None, None, None)
            self.T.pop(n)

    def dram(self, name, shape, dt, kind="Internal"):
        if kind == "Internal" and self.cfg.dump_scratch:
            kind = "ExternalOutput"
        return self.nc.dram_tensor(name, list(shape), dt, kind=kind).ap()

    def sem(self, name):
        class S:
            pass
        s = S()
        s.h = self.es.enter_context(self.nc.semaphore(name))
        s.v = 0
        s.bump = lambda n, s=s: (setattr(s, "v", s.v + n), s.v)[1]
        return s

    def fence(self, tag):
        s = self.s
        vals = [(s[n], s[n].v) for n in
                ("ldg", "lds", "st", "tn", "vc", "sc", "gp", "fin")]
        for eng in ("sync", "scalar", "vector", "tensor", "gpsimd"):
            for sem, v in vals:
                if v:
                    self.wait(eng, sem, v)
        self.bg.clear()
        self.ug.clear()

    def pbb(self, idx):
        return self.pb[idx][:, :].bitcast(BF16)

    def bankw(self, idx):
        g = self.bg.get(idx)
        if g:
            self.wait("tensor", g[0], g[1])

    def bankr(self, idx, sem, val):
        self.bg[idx] = (sem, val)

    def bufw(self, eng, name):
        g = self.ug.get(name)
        if g:
            self.wait(eng, g[0], g[1])

    def bufr(self, name, sem, val):
        self.ug[name] = (sem, val)

    def build(self):
        cfg = self.cfg
        S, Q, I8 = cfg.S, cfg.Q, cfg.I8
        d = self.d = {}
        di = lambda n, sh, dt=F32: d.__setitem__(n, self.dram(n, sh, dt, "ExternalInput"))
        di("x", (S, E)); di("crow", (1, E)); di("g1r", (1, CC)); di("g2r", (1, CC))
        di("g3r", (1, E)); di("identf", (128, 128)); di("identb", (128, 128), BF16)
        di("h128", (128, 128), BF16); di("cosd", (128, S)); di("sind", (128, S))
        di("wmod", (E, 6 * E)); di("wdkv", (E, CC)); di("wdq", (E, CC))
        di("wqr", (CC, E)); di("wkr", (E, E)); di("wuq", (E, CC)); di("wuk", (E, CC))
        di("wuv", (E, CC)); di("wot", (E, E))
        di("gtiled", (cfg.OCG, 128, E)); di("utiled", (cfg.OCG, 128, E))
        di("dtiled", (cfg.OC16, 128, I8))
        d["y"] = self.dram("y", (Q, E), F32, "ExternalOutput")
        d["h_sc"] = self.dram("h_sc", (cfg.n_k1, 128, 16 * 512), BF16)
        d["kr_sc"] = self.dram("kr_sc", (cfg.n_k1, 128, 16 * 512), BF16)
        d["ckv_sc"] = self.dram("ckv_sc", (cfg.n_k1, 128, 4 * 512), BF16)
        d["xmid_sc"] = self.dram("xmid_sc", (Q // 128, 128, E), F32)
        d["xq1_sc"] = self.dram("xq1_sc", (cfg.n_grp, 128, 16 * 512), BF16)
        d["wo_sc"] = self.dram("wo_sc", (128, 4 * E), BF16)
        d["qr_sc"] = self.dram("qr_sc", (cfg.n_k2, 128, 16 * 512), BF16)
        d["aq_sc"] = self.dram("aq_sc", (cfg.n_k2, 128, 4 * 512), BF16)
        d["oa_sc"] = self.dram("oa_sc", (128, cfg.n_qt * 512), F32)
        d["l_sc"] = self.dram("l_sc", (128, cfg.n_qt), F32)
        if cfg.debug:
            d["dbg_p0"] = self.dram("dbg_p0", (128, 256), F32)
            d["dbg_m512"] = self.dram("dbg_m512", (128, 4 * CC), BF16)
            d["dbg_y"] = self.dram("dbg_y", (cfg.n_grp, 128, cfg.OCG * 512), BF16)
            d["dbg_x2"] = self.dram("dbg_x2", (cfg.n_grp, 128, cfg.OCG * 512), BF16)
            d["dbg_dn"] = self.dram("dbg_dn", (cfg.n_grp, 128, cfg.OC16 * 512), BF16)
            d["dbg_r"] = self.dram("dbg_r", (cfg.n_grp, 2, 512), F32)
        d["tg_sc"] = self.dram("tg_sc", (cfg.OCG, 128, E), BF16)
        d["tu_sc"] = self.dram("tu_sc", (cfg.OCG, 128, E), BF16)
        d["td_sc"] = self.dram("td_sc", (cfg.OC16, 128, I8), BF16)

        self.sb("s_idf", [128, 128], F32)
        self.sb("s_idb", [128, 128], BF16)
        self.sb("h128s", [128, 128], BF16)
        self.sb("mod_sb", [128, 96], F32)
        self.sb("gvec", [128, 8], F32)
        self.sb("ca1", [128, 32], F32)
        self.sb("gag3r", [1, E], F32)
        self.sb("wq", [128, 8], F32)
        self.sb("gbpf", [128, 16], F32)
        self.sb("wsum", [128, cfg.NQ], F32)
        self.sb("m512", [128, 4 * CC], BF16)
        self.sb("epsc", [128, 1], F32)
        self.sb("onesr", [1, 128], F32)
        self.sb("onesc", [128, 1], F32)
        self.sb("wrow", [1, 8], F32)
        self.sb("ropeA", [128, 512], F32)
        self.sb("ropeB", [128, 512], F32)
        self.sb("inv1", [1, Q], F32)
        for i in range(8):
            self.ps(f"pb{i}")
        self.pb = [self.T[f"pb{i}"] for i in range(8)]

        self.s = {n: self.sem(n) for n in
                  ("ldg", "lds", "st", "tn", "vc", "sc", "gp", "fin")}

        phases = [("p0", self.p0), ("k1a", self.k1a), ("k1b", self.k1b),
                  ("k2", self.k2), ("attn", self.attn), ("epi", self.epi),
                  ("ffn", self.ffn)]
        for name, fn in phases:
            fn()
            if cfg.stop == name:
                break

        with self.nc.Block() as blk:
            blk.sync(lambda e: [f(e) for f in self.plan["sync"]])
            blk.scalar(lambda e: [f(e) for f in self.plan["scalar"]])
            blk.vector(lambda e: [f(e) for f in self.plan["vector"]])
            blk.tensor(lambda e: [f(e) for f in self.plan["tensor"]])
            blk.gpsimd(lambda e: [f(e) for f in self.plan["gpsimd"]])
        self.es.close()
        return self.nc

    def p0(self):
        cfg, d, s, pb, T = self.cfg, self.d, self.s, self.pb, self.T

        self.dma("sync", T["s_idf"][:, :], d["identf"][:, :], s["lds"])
        self.dma("sync", T["s_idb"][:, :], d["identb"][:, :], s["lds"])
        self.dma("sync", T["h128s"][:, :], d["h128"][:, :], s["lds"])
        self.op("vector", "memset", self.T["epsc"][:, :], RMS_EPS)
        self.op("vector", "memset", self.T["onesr"][:, :], 1.0)
        self.op("vector", "memset", self.T["onesc"][:, :], 1.0)
        g123 = self.sb("g123", [1, 2 * CC + E], F32)
        csf = self.sb("csf", [128, 16], BF16)
        crow = self.sb("s_crow", [1, E], F32)
        csrow = self.sb("csrow", [1, E], F32)
        self.dma("sync", crow[:, :], d["crow"][:, :], s["lds"])
        self.dma("sync", g123[:, 0:CC], d["g1r"][:, :], s["lds"])
        self.dma("sync", g123[:, CC:2 * CC], d["g2r"][:, :], s["lds"])
        w_const = self.dma("sync", g123[:, 2 * CC:], d["g3r"][:, :], s["lds"])

        self.wait("scalar", s["lds"], w_const)
        sc0 = self.op("scalar", "activation", csrow[:, :], crow[:, :], AF.Silu, sem=s["sc"])
        self.wait("tensor", s["sc"], sc0)
        for j in range(16):
            self.op("tensor", "transpose", pb[0][:, j:j + 1],
                    csrow[0:1, j * 128:(j + 1) * 128], T["s_idf"][0:1, 0:1],
                    sem=(s["tn"] if j == 15 else None))
        t_csf = s["tn"].v
        self.wait("vector", s["tn"], t_csf)
        v_csf = self.op("vector", "tensor_copy", csf[:, :], pb[0][:, 0:16], sem=s["vc"])
        self.bankr(0, s["vc"], v_csf)

        self.wait("tensor", s["vc"], v_csf)
        for j in range(4):
            self.op("tensor", "transpose", pb[1][:, j:j + 1],
                    g123[0:1, j * 128:(j + 1) * 128], T["s_idf"][0:1, 0:1])
        for j in range(4):
            self.op("tensor", "transpose", pb[1][:, 4 + j:5 + j],
                    g123[0:1, CC + j * 128:CC + (j + 1) * 128], T["s_idf"][0:1, 0:1],
                    sem=(s["tn"] if j == 3 else None))
        t_g = s["tn"].v
        self.wait("vector", s["tn"], t_g)
        self.op("vector", "tensor_copy", T["gvec"][:, 0:4], pb[1][:, 0:4])
        v_g = self.op("vector", "tensor_scalar_mul", T["gvec"][:, 4:8], pb[1][:, 4:8],
                      INV12, sem=s["vc"])
        self.bankr(1, s["vc"], v_g)

        self.free("csrow", "s_crow")
        wuq = self.sb("s_wuq", [128, 16 * CC], BF16)
        wuk = self.sb("s_wuk", [128, 16 * CC], BF16)
        self.wait("gpsimd", s["tn"], t_g)
        self.dma_fm("gpsimd", wuq[:, :], d["wuq"][:, :], 16, s["ldg"])
        g_uk = self.dma_fm("gpsimd", wuk[:, :], d["wuk"][:, :], 16, s["ldg"])
        m512 = T["m512"]
        self.wait("tensor", s["ldg"], g_uk)
        for m in range(4):
            self.bankw(4)
            for k in range(16):
                self.op("tensor", "matmul", pb[4][:, :],
                        wuq[:, k * CC + m * 128:k * CC + (m + 1) * 128],
                        wuk[:, k * CC:(k + 1) * CC],
                        start=(k == 0), stop=(k == 15),
                        sem=(s["tn"] if k == 15 else None))
            self.wait("scalar", s["tn"], s["tn"].v)
            scm = self.op("scalar", "copy", m512[:, m * CC:(m + 1) * CC], pb[4][:, :],
                          sem=s["sc"])
            self.bankr(4, s["sc"], scm)
            t_m512_done = s["tn"].v
        self.free("s_wuk", "s_wuq")

        wuv = self.sb("s_wuv", [128, 16 * CC], BF16)
        wot = self.sb("s_wot", [128, 16 * E], BF16)
        self.wait("gpsimd", s["tn"], t_m512_done)
        self.dma_fm("gpsimd", wuv[:, :], d["wuv"][:, :], 16, s["ldg"])
        g_wo = self.dma_fm("gpsimd", wot[:, :], d["wot"][:, :], 16, s["ldg"])
        g2raw = self.sb("g2raw", [128, 4], F32)
        v_g2 = self.op("vector", "tensor_scalar_mul", g2raw[:, :], T["gvec"][:, 4:8], 12.0,
                       sem=s["vc"])
        self.wait("scalar", s["vc"], v_g2)
        wos = self.sb("wos", [128, 2 * 512], BF16)
        self.wait("tensor", s["ldg"], g_wo)
        st_wo = None
        for mn in range(16):
            m, n = divmod(mn, 4)
            bidx = 5 + mn % 2
            self.bankw(bidx)
            for k in range(16):
                self.op("tensor", "matmul", pb[bidx][:, :],
                        wuv[:, k * CC + m * 128:k * CC + (m + 1) * 128],
                        wot[:, k * E + n * 512:k * E + (n + 1) * 512],
                        start=(k == 0), stop=(k == 15),
                        sem=(s["tn"] if k == 15 else None))
            self.wait("scalar", s["tn"], s["tn"].v)
            self.bufw("scalar", f"wos{mn % 2}")
            scv = self.op("scalar", "activation", wos[:, (mn % 2) * 512:(mn % 2 + 1) * 512],
                          pb[bidx][:, :], AF.Copy, bias=0.0, scale=g2raw[:, m:m + 1],
                          sem=s["sc"])
            self.bankr(bidx, s["sc"], scv)
            self.wait("scalar", s["sc"], scv)
            st_wo = self.dma("scalar", d["wo_sc"][:, mn * 512:(mn + 1) * 512],
                             wos[:, (mn % 2) * 512:(mn % 2 + 1) * 512], s["st"])
            self.bufr(f"wos{mn % 2}", s["st"], st_wo)
        self.st_wo = st_wo
        t_wo_done = s["tn"].v
        self.free("wos", "g2raw", "s_wot", "s_wuv")

        wmodb = [self.sb(f"wmodb{i}", [128, 6 * E], BF16) for i in range(2)]
        self.wait("gpsimd", s["tn"], t_wo_done)
        self.wait("gpsimd", s["st"], st_wo)
        self.op("vector", "memset", T["mod_sb"][:, :], 0.0)
        for k in range(16):
            self.bufw("gpsimd", f"wmodb{k % 2}")
            wv = self.dma("gpsimd", wmodb[k % 2][:, :], d["wmod"][k * 128:(k + 1) * 128, :],
                          s["ldg"])
            self.wait("tensor", s["ldg"], wv)
            bidx = 2 + k % 2
            self.bankw(bidx)
            for j in range(96):
                self.op("tensor", "matmul", pb[bidx][:, j:j + 1],
                        wmodb[k % 2][:, j * 128:(j + 1) * 128], csf[:, k:k + 1],
                        start=True, stop=True,
                        sem=(s["tn"] if j == 95 else None))
            tk = s["tn"].v
            self.bufr(f"wmodb{k % 2}", s["tn"], tk)
            self.wait("vector", s["tn"], tk)
            v_acc = self.op("vector", "tensor_add", T["mod_sb"][:, :], T["mod_sb"][:, :],
                            pb[bidx][:, 0:96], sem=s["vc"])
            self.bankr(bidx, s["vc"], v_acc)
        v_mod = s["vc"].v

        self.op("vector", "tensor_scalar_add", T["ca1"][:, 0:16], T["mod_sb"][:, 16:32], 1.0)
        self.op("vector", "tensor_scalar_add", T["ca1"][:, 16:32], T["mod_sb"][:, 64:80], 1.0)
        gsc = self.sb("gsc", [128, 16], F32)
        self.op("vector", "tensor_single_scalar", gsc[:, :], T["mod_sb"][:, 80:96], 0.0, ALU.is_ge)
        v_sg = self.op("vector", "tensor_scalar", gsc[:, :], gsc[:, :], 2.0, -1.0,
                       ALU.mult, ALU.add, sem=s["vc"])
        self.wait("scalar", s["vc"], v_sg)
        sc_ab = self.op("scalar", "activation", T["gbpf"][:, :], T["mod_sb"][:, 80:96],
                        AF.Abs, sem=s["sc"])
        self.wait("vector", s["sc"], sc_ab)
        self.op("vector", "tensor_scalar_max", T["gbpf"][:, :], T["gbpf"][:, :], 0.25)
        self.op("vector", "tensor_mul", T["gbpf"][:, :], T["gbpf"][:, :], gsc[:, :])

        gab = self.sb("gab", [16, 128], F32)
        rowb = self.sb("rowb", [1, 2 * E], F32)
        self.wait("tensor", s["vc"], v_mod)
        self.bankw(3)
        t_ga = self.op("tensor", "transpose", pb[3][0:16, 0:128], T["mod_sb"][:, 32:48],
                       T["s_idf"][:, :], sem=s["tn"])
        self.wait("vector", s["tn"], t_ga)
        v_gab = self.op("vector", "tensor_copy", gab[:, :], pb[3][0:16, 0:128], sem=s["vc"])
        self.bankr(3, s["vc"], v_gab)
        self.wait("sync", s["vc"], v_gab)
        for j in range(16):
            w_gar = self.dma("sync", rowb[0:1, j * 128:(j + 1) * 128],
                             gab[j:j + 1, :], s["lds"])
        scr = rowb[0:1, E:2 * E]
        self.wait("vector", s["lds"], w_gar)
        self.op("vector", "tensor_single_scalar", scr, rowb[0:1, 0:E], 0.0, ALU.is_ge)
        v_s2 = self.op("vector", "tensor_scalar", scr, scr, 2.0, -1.0, ALU.mult, ALU.add,
                       sem=s["vc"])
        self.wait("scalar", s["vc"], v_s2)
        sc_a2 = self.op("scalar", "activation", rowb[0:1, 0:E], rowb[0:1, 0:E], AF.Abs,
                        sem=s["sc"])
        self.wait("vector", s["sc"], sc_a2)
        self.op("vector", "tensor_scalar_max", rowb[0:1, 0:E], rowb[0:1, 0:E], 0.25)
        self.op("vector", "tensor_mul", rowb[0:1, 0:E], rowb[0:1, 0:E], scr)
        self.op("vector", "tensor_mul", rowb[0:1, 0:E], rowb[0:1, 0:E], g123[:, 2 * CC:])
        self.v_gag3 = self.op("vector", "tensor_copy", T["gag3r"][0:1, :], rowb[0:1, 0:E],
                              sem=s["vc"])
        self.free("rowb", "gab", "gsc", "wmodb1", "wmodb0", "csf", "g123")

        if cfg.debug:
            self.wait("scalar", s["vc"], s["vc"].v)
            self.wait("scalar", s["sc"], s["sc"].v)
            self.dma("scalar", d["dbg_p0"][:, 0:96], T["mod_sb"][:, :], s["st"])
            self.dma("scalar", d["dbg_p0"][:, 96:128], T["ca1"][:, :], s["st"])
            self.dma("scalar", d["dbg_p0"][:, 128:136], T["gvec"][:, :], s["st"])
            self.dma("scalar", d["dbg_p0"][:, 140:156], T["gbpf"][:, :], s["st"])
            self.dma("scalar", d["dbg_m512"][:, :], T["m512"][:, :], s["st"])
        self.fence("p0")

        chunks = []
        for oc in range(cfg.OCG):
            chunks.append((d["gtiled"][oc], d["tg_sc"][oc], 0, oc))
        for oc in range(cfg.OCG):
            chunks.append((d["utiled"][oc], d["tu_sc"][oc], 1, cfg.OCG + oc))
        for oc in range(cfg.OC16):
            for j in range(cfg.NSUB):
                chunks.append((d["dtiled"][oc][:, j * 2048:(j + 1) * 2048],
                               d["td_sc"][oc][:, j * 2048:(j + 1) * 2048],
                               2, 2 * cfg.OCG + oc * cfg.NSUB + j))
        self.q_chunks = chunks
        mchunks = []
        for oc in range(0, cfg.OCG, 2):
            mchunks.append((d["gtiled"][oc:oc + 2].rearrange("a p e -> p a e"), len(mchunks)))
        for oc in range(0, cfg.OCG, 2):
            mchunks.append((d["utiled"][oc:oc + 2].rearrange("a p e -> p a e"), len(mchunks)))
        for oc in range(cfg.OC16):
            if cfg.NSUB >= 2:
                for j in range(0, cfg.NSUB, 2):
                    mchunks.append((d["dtiled"][oc][:, j * 2048:(j + 2) * 2048], len(mchunks)))
            elif oc % 2 == 0:
                mchunks.append((d["dtiled"][oc:oc + 2].rearrange("a p e -> p a e"),
                                len(mchunks)))
        self.m_chunks = mchunks
        self.m_cols = {"g": (0, cfg.OCG // 2),
                       "u": (cfg.OCG // 2, cfg.OCG),
                       "d": (cfg.OCG, len(mchunks))}
        self.m_idx = 0

    def _mean(self, n):
        cfg, d, s, T = self.cfg, self.d, self.s, self.T
        for _ in range(n):
            if self.m_idx >= len(self.m_chunks):
                return
            idx = self.m_idx
            self.m_idx += 1
            ap, col = self.m_chunks[idx]
            b = idx % 2
            self.bufw("gpsimd", f"wmb{b}")
            dst = self.T[f"wmb{b}"][:, :]
            if len(ap.shape) == 3:
                dst = dst.rearrange("p (a e) -> p a e", a=2)
            wv = self.dma("gpsimd", dst, ap, s["ldg"])
            if idx % 2 == 0:
                self.wait("vector", s["ldg"], wv)
                gv = self.op("vector", "tensor_reduce", T["wsum"][:, col:col + 1],
                             self.T[f"wmb{b}"][:, :], AXX, ALU.add, True, sem=s["vc"])
                self.bufr(f"wmb{b}", s["vc"], gv)
            else:
                self.wait("scalar", s["ldg"], wv)
                sv = self.op("scalar", "activation", T["mabs"][:, :],
                             self.T[f"wmb{b}"][:, :], AF.Abs,
                             accum_out=T["wsum"][:, col:col + 1], sem=s["sc"])
                self.bufr(f"wmb{b}", s["sc"], sv)

    def _ln4(self, xt, st_, sqs, xw_scalar=None):
        s = self.s
        for sub in range(4):
            self.op("vector", "reduce_sum", st_[:, sub:sub + 1],
                    xt[:, sub * E:(sub + 1) * E], AXX)
        if xw_scalar is not None:
            self.wait("scalar", s["lds"], xw_scalar)
        for sub in range(4):
            for bq in range(4):
                self.op("scalar", "activation", sqs[:, :],
                        xt[:, sub * E + bq * 512:sub * E + (bq + 1) * 512],
                        AF.Square, accum_out=st_[:, 4 + sub * 4 + bq:5 + sub * 4 + bq],
                        sem=(s["sc"] if sub == 3 and bq == 3 else None))
        sq_done = s["sc"].v
        self.wait("vector", s["sc"], sq_done)
        qv = st_[:, 4:20].rearrange("p (a b) -> p a b", b=4)
        self.op("vector", "tensor_reduce", st_[:, 20:24], qv, AXX, ALU.add)
        self.op("vector", "tensor_scalar_mul", st_[:, 24:28], st_[:, 0:4], 1.0 / E)
        self.op("vector", "tensor_scalar_mul", st_[:, 28:32], st_[:, 20:24], 1.0 / E)
        self.op("vector", "tensor_mul", st_[:, 32:36], st_[:, 24:28], st_[:, 24:28])
        self.op("vector", "tensor_sub", st_[:, 28:32], st_[:, 28:32], st_[:, 32:36])
        self.op("vector", "tensor_scalar_add", st_[:, 28:32], st_[:, 28:32], LN_EPS)
        v_rc = self.op("vector", "reciprocal", st_[:, 28:32], st_[:, 28:32], sem=s["vc"])
        self.wait("scalar", s["vc"], v_rc)
        s_sq = self.op("scalar", "activation", st_[:, 32:36], st_[:, 28:32], AF.Sqrt,
                       sem=s["sc"])
        self.wait("vector", s["sc"], s_sq)
        for sub in range(4):
            v = self.op("vector", "tensor_scalar", xt[:, sub * E:(sub + 1) * E],
                        xt[:, sub * E:(sub + 1) * E], st_[:, 24 + sub:25 + sub],
                        st_[:, 32 + sub:33 + sub], ALU.subtract, ALU.mult,
                        sem=(s["vc"] if sub == 3 else None))
        return v

    def k1a(self):
        cfg, d, s, pb, T = self.cfg, self.d, self.s, self.pb, self.T
        self.sb("wmb0", [128, 4096], BF16)
        self.sb("wmb1", [128, 4096], BF16)
        self.sb("mabs", [128, 4096], BF16)
        wdkv = self.sb("s_wdkv", [128, 16 * CC], BF16)
        g_kv = self.dma_fm("gpsimd", wdkv[:, :], d["wdkv"][:, :], 16, s["ldg"])
        xa = [self.sb(f"xa{i}", [128, 4 * E], F32) for i in range(2)]
        hf = [self.sb(f"hf{i}", [128, 16 * 512], BF16) for i in range(2)]
        cvsb = self.sb("cvsb", [128, 4 * 512], BF16)
        ckvo = self.sb("ckvo", [128, 4 * 512], BF16)
        st_ = self.sb("k1st", [128, 40], F32)
        sqs = self.sb("sqs", [128, 512], F32)

        for i in range(cfg.n_k1):
            bi = i % 2
            self.bufw("sync", f"xa{bi}")
            xw = self.dma("sync", xa[bi][:, :].rearrange("p (s e) -> p s e", s=4),
                          d["x"][i * 512:(i + 1) * 512, :].rearrange("(s p) e -> p s e", p=128),
                          s["lds"])
            self.wait("vector", s["lds"], xw)
            if i == 0:
                self.wait("vector", s["vc"], self.v_gag3)
            v_ln = self._ln4(xa[bi][:, :], st_, sqs, xw_scalar=xw)
            self.wait("tensor", s["vc"], v_ln)
            if i == 0:
                self.wait("tensor", s["ldg"], g_kv)
            self.bufw("tensor", f"hf{bi}")
            for c in range(16):
                bidx = c % 2
                self.bankw(bidx)
                for sub in range(4):
                    self.op("tensor", "transpose", pb[bidx][:, sub * 128:(sub + 1) * 128],
                            xa[bi][:, sub * E + c * 128:sub * E + (c + 1) * 128],
                            T["s_idf"][:, :], sem=(s["tn"] if sub == 3 else None))
                self.wait("vector", s["tn"], s["tn"].v)
                scv = self.op("vector", "tensor_scalar", hf[bi][:, c * 512:(c + 1) * 512],
                              pb[bidx][:, :], T["ca1"][:, c:c + 1], T["mod_sb"][:, c:c + 1],
                              ALU.mult, ALU.add, sem=s["vc"])
                self.bankr(bidx, s["vc"], scv)
            sc_h = s["vc"].v
            self.wait("scalar", s["vc"], sc_h)
            sth = self.dma("scalar", d["h_sc"][i], hf[bi][:, :], s["st"])
            self.wait("tensor", s["vc"], sc_h)
            for m in range(4):
                bidx = 2 + m % 2
                self.bankw(bidx)
                for k in range(16):
                    self.op("tensor", "matmul", pb[bidx][:, :],
                            wdkv[:, k * CC + m * 128:k * CC + (m + 1) * 128],
                            hf[bi][:, k * 512:(k + 1) * 512],
                            start=(k == 0), stop=(k == 15),
                            sem=(s["tn"] if k == 15 else None))
                self.wait("scalar", s["tn"], s["tn"].v)
                scv = self.op("scalar", "copy", cvsb[:, m * 512:(m + 1) * 512],
                              pb[bidx][:, :], sem=s["sc"])
                self.bankr(bidx, s["sc"], scv)
            sc_cv = s["sc"].v
            self.wait("tensor", s["sc"], sc_cv)
            self.bufw("vector", "ckvo")
            for sub in range(4):
                bidx = 4 + sub % 2
                self.bankw(bidx)
                for m in range(4):
                    self.op("tensor", "transpose", self.pbb(bidx)[:, m * 128:(m + 1) * 128],
                            cvsb[:, m * 512 + sub * 128:m * 512 + (sub + 1) * 128],
                            T["s_idb"][:, :], sem=(s["tn"] if m == 3 else None))
                t_tm = s["tn"].v
                self.wait("scalar", s["tn"], t_tm)
                s1 = self.op("scalar", "activation", sqs[:, :], self.pbb(bidx)[:, 0:512],
                             AF.Square, accum_out=st_[:, 29:30], sem=s["sc"])
                self.wait("scalar", s["sc"], s1)
                s2 = self.op("scalar", "activation", st_[:, 30:31], st_[:, 29:30], AF.Sqrt,
                             bias=self.T["epsc"][:, 0:1], scale=1.0 / CC, sem=s["sc"])
                self.wait("vector", s["sc"], s2)
                self.op("vector", "reciprocal", st_[:, 31:32], st_[:, 30:31])
                vv = self.op("vector", "tensor_scalar_mul", ckvo[:, sub * 512:(sub + 1) * 512],
                             self.pbb(bidx)[:, 0:512], st_[:, 31:32], sem=s["vc"])
                self.bankr(bidx, s["vc"], vv)
            v_cv = s["vc"].v
            self.wait("scalar", s["vc"], v_cv)
            stc = self.dma("scalar", d["ckv_sc"][i], ckvo[:, :], s["st"])
            tm = s["tn"].v
            self.bufr(f"xa{bi}", s["tn"], tm)
            self.bufr(f"hf{bi}", s["st"], sth)
            self.bufr("ckvo", s["st"], stc)
            self._mean(-(-len(self.m_chunks) // cfg.n_k1))
        self._mean(len(self.m_chunks))
        if cfg.debug:
            self.wait("scalar", s["gp"], s["gp"].v)
            self.wait("scalar", s["sc"], s["sc"].v)
            self.dma("scalar", d["dbg_p0"][:, 156:156 + min(64, len(self.m_chunks))],
                     T["wsum"][:, 0:min(64, len(self.m_chunks))], s["st"])
        self.fence("k1a")
        self.free("sqs", "k1st", "ckvo", "cvsb", "hf1", "hf0", "xa1", "xa0",
                  "s_wdkv", "mabs", "wmb1", "wmb0")

    def k1b(self):
        cfg, d, s, pb, T = self.cfg, self.d, self.s, self.pb, self.T
        wq = T["wq"]
        for j, mat in enumerate(("g", "u", "d")):
            lo, hi = self.m_cols[mat]
            v_r = self.op("vector", "tensor_reduce", wq[:, 3 + j:4 + j],
                          T["wsum"][:, lo:hi], AXX, ALU.add,
                          sem=(s["vc"] if j == 2 else None))
        wrow = T["wrow"]
        self.wait("tensor", s["vc"], v_r)
        self.bankw(4)
        t_ws = self.op("tensor", "matmul", pb[4][0:1, 0:3], T["onesc"][:, 0:1],
                       wq[:, 3:6], start=True, stop=True, sem=s["tn"])
        self.wait("scalar", s["tn"], t_ws)
        sc_w = self.op("scalar", "copy", wrow[0:1, 0:3], pb[4][0:1, 0:3], sem=s["sc"])
        self.bankr(4, s["sc"], sc_w)
        self.wait("vector", s["sc"], sc_w)
        nel = float(cfg.I8 * E)
        self.op("vector", "tensor_scalar", wrow[0:1, 0:3], wrow[0:1, 0:3],
                1.0 / nel, QEPS, ALU.mult, ALU.max)
        v_w2 = self.op("vector", "reciprocal", wrow[0:1, 3:6], wrow[0:1, 0:3], sem=s["vc"])
        self.wait("tensor", s["vc"], v_w2)
        self.bankw(4)
        t_wb = self.op("tensor", "matmul", pb[4][:, 0:6], T["onesr"][0:1, :],
                       wrow[0:1, 0:6], start=True, stop=True, sem=s["tn"])
        self.wait("scalar", s["tn"], t_wb)
        self.op("scalar", "copy", wq[:, 3:6], pb[4][:, 0:3])
        sc_wb = self.op("scalar", "copy", wq[:, 0:3], pb[4][:, 3:6], sem=s["sc"])
        self.bankr(4, s["sc"], sc_wb)
        self.wq_ready = sc_wb
        self.q_idx = 0

        cosd = self.sb("s_cos", [128, cfg.S], F32)
        sind = self.sb("s_sin", [128, cfg.S], F32)
        hb = self.sb("hb0", [128, 16 * 512], BF16)
        wkr = self.sb("s_wkr", [128, 16 * E], BF16)
        krf = [self.sb(f"krf{i}", [128, 16 * 512], BF16) for i in range(2)]
        self.qf = [self.sb(f"qf{i}", [128, 2048], F32) for i in range(2)]
        self.qt = [self.sb(f"qt{i}", [128, 2048], BF16) for i in range(2)]

        self.dma("sync", cosd[:, :], d["cosd"][:, :], s["lds"])
        w_tb = self.dma("sync", sind[:, :], d["sind"][:, :], s["lds"])
        g_kr = self.dma_fm("gpsimd", wkr[:, :], d["wkr"][:, :], 16, s["ldg"])

        nq_per = max(1, cfg.NQ // (2 * cfg.n_k1))
        for i in range(cfg.n_k1):
            bi = i % 2
            self.bufw("sync", "hb0")
            hw = self.dma("sync", hb[:, :], d["h_sc"][i], s["lds"])
            self.wait("tensor", s["lds"], hw)
            if i == 0:
                self.wait("tensor", s["ldg"], g_kr)
                self.wait("vector", s["lds"], w_tb)
            self.bufw("vector", f"krf{bi}")
            for grp in range(4):
                for u in range(4):
                    m = grp * 4 + u
                    bidx = m % 4
                    self.bankw(bidx)
                    for k in range(16):
                        self.op("tensor", "matmul", pb[bidx][:, :],
                                wkr[:, k * E + m * 128:k * E + (m + 1) * 128],
                                hb[:, k * 512:(k + 1) * 512],
                                start=(k == 0), stop=(k == 15),
                                sem=(s["tn"] if k == 15 else None))
                t_grp = s["tn"].v
                self.wait("vector", s["tn"], t_grp)
                v_rp = self._rope4(grp * 4, [pb[0], pb[1], pb[2], pb[3]], krf[bi], 512,
                                   cosd[:, i * 512:(i + 1) * 512],
                                   sind[:, i * 512:(i + 1) * 512])
                for bidx in range(4):
                    self.bankr(bidx, s["vc"], v_rp)
            v_kr = s["vc"].v
            self.bufr("hb0", s["tn"], s["tn"].v)
            self.wait("scalar", s["vc"], v_kr)
            stk = self.dma("scalar", d["kr_sc"][i], krf[bi][:, :], s["st"])
            self.bufr(f"krf{bi}", s["st"], stk)
            self._quant(nq_per)
        self._quant((cfg.NQ + 1) // 2 - self.q_idx)
        self.fence("k1b")
        self.free("qt1", "qt0", "qf1", "qf0", "krf1", "krf0", "s_wkr", "hb0",
                  "s_sin", "s_cos")

    def _rope4(self, base, banks, out, outw, cosap, sinap):
        T = self.T
        ret = None
        for u in range(2):
            x1 = banks[2 * u][:, :]
            x2 = banks[2 * u + 1][:, :]
            o1 = out[:, (base + 2 * u) * outw:(base + 2 * u + 1) * outw]
            o2 = out[:, (base + 2 * u + 1) * outw:(base + 2 * u + 2) * outw]
            t1 = T["ropeA"][:, :]
            t2 = T["ropeB"][:, :]
            self.op("vector", "tensor_mul", t1, x1, cosap)
            self.op("vector", "tensor_mul", t2, x2, sinap)
            self.op("vector", "tensor_sub", o1, t1, t2)
            self.op("vector", "tensor_mul", t1, x1, sinap)
            self.op("vector", "tensor_mul", t2, x2, cosap)
            ret = self.op("vector", "tensor_add", o2, t1, t2,
                          sem=(self.s["vc"] if u == 1 else None))
        return ret

    def bcast(self, row, out_sb, n, bank, wait_vc=None):
        s, pb, T = self.s, self.pb, self.T
        if wait_vc is not None:
            self.wait("tensor", s["vc"], wait_vc)
        sc = None
        for j0 in range(0, n, 512):
            w = min(512, n - j0)
            self.bankw(bank)
            t = self.op("tensor", "matmul", pb[bank][:, 0:w], T["onesr"][0:1, :],
                        row[0:1, j0:j0 + w], start=True, stop=True, sem=s["tn"])
            self.wait("scalar", s["tn"], t)
            sc = self.op("scalar", "copy", out_sb[:, j0:j0 + w], pb[bank][:, 0:w],
                         sem=s["sc"])
            self.bankr(bank, s["sc"], sc)
        return sc

    def _rowmax(self, mx, mxc, tmp4, outrow, bank=7):
        s, pb, T = self.s, self.pb, self.T
        self.bankw(bank)
        for j in range(4):
            self.op("tensor", "transpose", pb[bank][:, j * 128:(j + 1) * 128],
                    mx[:, j * 128:(j + 1) * 128], T["s_idf"][:, :],
                    sem=(s["tn"] if j == 3 else None))
        t_tr = s["tn"].v
        self.wait("vector", s["tn"], t_tr)
        for j in range(4):
            self.op("vector", "reduce_max", mxc[:, j:j + 1],
                    pb[bank][:, j * 128:(j + 1) * 128], AXX,
                    sem=(s["vc"] if j == 3 else None))
        v_mc = s["vc"].v
        self.bankr(bank, s["vc"], v_mc)
        self.wait("tensor", s["vc"], v_mc)
        self.bankw(bank)
        t2 = self.op("tensor", "transpose", pb[bank][0:4, 0:128], mxc[:, 0:4],
                     T["s_idf"][:, :], sem=s["tn"])
        self.wait("vector", s["tn"], t2)
        v_t4 = self.op("vector", "tensor_copy", tmp4[:, :], pb[bank][0:4, 0:128],
                       sem=s["vc"])
        self.bankr(bank, s["vc"], v_t4)
        self.wait("sync", s["vc"], v_t4)
        w = None
        for j in range(4):
            w = self.dma("sync", outrow[0:1, j * 128:(j + 1) * 128], tmp4[j:j + 1, :],
                         s["lds"])
        return w

    def _quant(self, n):
        cfg, d, s = self.cfg, self.d, self.s
        wq = self.T["wq"]
        while n > 0:
            if self.q_idx >= len(self.q_chunks):
                return
            pair = []
            for _ in range(min(2, n, len(self.q_chunks) - self.q_idx)):
                idx = self.q_idx
                self.q_idx += 1
                n -= 1
                srcap, dst, j, _ = self.q_chunks[idx]
                b = idx % 2
                if idx == 0:
                    self.wait("scalar", s["sc"], self.wq_ready)
                self.bufw("sync", f"qf{b}")
                wv = self.dma("sync", self.qf[b][:, :], srcap, s["lds"])
                pair.append((idx, b, dst, j, wv))
            for _, b, _, j, wv in pair:
                self.wait("scalar", s["lds"], wv)
                sv = self.op("scalar", "activation", self.qf[b][:, :],
                             self.qf[b][:, :], AF.Copy, scale=wq[:, j:j + 1],
                             sem=s["sc"])
                pairsc = sv
                self.wait("vector", s["sc"], sv)
                self.bufw("vector", f"qt{b}")
            for _, b, _, _, _ in pair:
                f = self.qf[b]
                self.op("vector", "tensor_scalar", f[:, :], f[:, :], MAGIC,
                        MAGIC, ALU.add, ALU.subtract)
            for _, b, _, _, _ in pair:
                f, t = self.qf[b], self.qt[b]
                qv = self.op("vector", "tensor_scalar", t[:, :], f[:, :], 1.0,
                             -1.0, ALU.min, ALU.max, sem=s["vc"])
                self.bufr(f"qf{b}", s["vc"], qv)
                self.wait("scalar", s["vc"], qv)
                stq = self.dma("scalar",
                               [p for p in pair if p[1] == b][0][2], t[:, :],
                               s["st"])
                self.bufr(f"qt{b}", s["st"], stq)

    def k2(self):
        cfg, d, s, pb, T = self.cfg, self.d, self.s, self.pb, self.T
        cosd = self.sb("s_cos", [128, cfg.Q], F32)
        sind = self.sb("s_sin", [128, cfg.Q], F32)
        hb = self.sb("hb0", [128, 16 * 512], BF16)
        wdq = self.sb("s_wdq", [128, 16 * CC], BF16)
        wqr = self.sb("s_wqr", [128, 4 * E], BF16)
        qrf = self.sb("qrf", [128, 16 * 512], BF16)
        aqf = self.sb("aqf", [128, 4 * 512], BF16)
        tbf = self.sb("tbf", [128, 4 * 512], BF16)
        cqb = self.sb("cqb", [128, 4 * 512], BF16)
        rrep = self.sb("rrep", [128, 512], F32)
        sqs2 = self.sb("sqs2", [128, 512], F32)
        rows = self.sb("rows", [1, 1024], F32)
        ones = self.sb("ones", [128, 1], F32)

        self.dma("sync", cosd[:, :], d["cosd"][:, 0:cfg.Q], s["lds"])
        w_tb = self.dma("sync", sind[:, :], d["sind"][:, 0:cfg.Q], s["lds"])
        g_dq = self.dma_fm("gpsimd", wdq[:, :], d["wdq"][:, :], 16, s["ldg"])
        g_qr = self.dma_fm("gpsimd", wqr[:, :], d["wqr"][:, :], 4, s["ldg"])
        self.wait("vector", s["ldg"], g_qr)
        for k in range(4):
            self.op("vector", "tensor_scalar_mul", wqr[:, k * E:(k + 1) * E],
                    wqr[:, k * E:(k + 1) * E], T["gvec"][:, k:k + 1],
                    sem=(s["vc"] if k == 3 else None))
        v_wqr = s["vc"].v
        gp_o = None
        v_ones = self.op("vector", "memset", ones[:, :], 1.0, sem=s["vc"])
        m512 = T["m512"]

        for i in range(cfg.n_k2):
            self.bufw("sync", "hb0")
            hw = self.dma("sync", hb[:, :], d["h_sc"][i], s["lds"])
            self.wait("tensor", s["lds"], hw)
            if i == 0:
                self.wait("tensor", s["ldg"], g_dq)
                self.wait("tensor", s["vc"], v_ones)
                self.wait("tensor", s["vc"], v_wqr)
                self.wait("vector", s["lds"], w_tb)
            for m in range(4):
                self.bankw(m)
                for k in range(16):
                    self.op("tensor", "matmul", pb[m][:, :],
                            wdq[:, k * CC + m * 128:k * CC + (m + 1) * 128],
                            hb[:, k * 512:(k + 1) * 512],
                            start=(k == 0), stop=(k == 15),
                            sem=(s["tn"] if (k == 15 and m == 3) else None))
            t_t = s["tn"].v
            self.wait("scalar", s["tn"], t_t)
            self.bufw("scalar", "tbf")
            for m in range(4):
                self.op("scalar", "copy", tbf[:, m * 512:(m + 1) * 512], pb[m][:, :],
                        sem=(s["sc"] if m == 3 else None))
            sc_t = s["sc"].v
            for m in range(4):
                if m > 0:
                    self.wait("scalar", s["tn"], t_sqm)
                sq = self.op("scalar", "activation", sqs2[:, :], pb[m][:, :], AF.Square,
                             sem=s["sc"])
                self.wait("tensor", s["sc"], sq)
                if m == 0:
                    self.bankw(4)
                t_sqm = self.op("tensor", "matmul", pb[4][0:1, :], ones[:, 0:1], sqs2[:, :],
                                start=(m == 0), stop=(m == 3), sem=s["tn"])
            t_sq = s["tn"].v
            self.wait("scalar", s["tn"], t_sq)
            s_den = self.op("scalar", "activation", rows[0:1, 0:512], pb[4][0:1, :], AF.Sqrt,
                            bias=self.T["epsc"][0:1, 0:1], scale=1.0 / CC, sem=s["sc"])
            self.bankr(4, s["sc"], s_den)
            self.wait("vector", s["sc"], s_den)
            self.op("vector", "reciprocal", rows[0:1, 512:1024], rows[0:1, 0:512])
            v_rr = self.op("vector", "tensor_scalar_mul", rows[0:1, 512:1024],
                           rows[0:1, 512:1024], INV12, sem=s["vc"])
            sc_b = self.bcast(rows[0:1, 512:1024], rrep, 512, 4, wait_vc=v_rr)
            self.wait("vector", s["sc"], sc_b)
            self.bufw("vector", "cqb")
            for m in range(4):
                self.op("vector", "tensor_mul", cqb[:, m * 512:(m + 1) * 512],
                        pb[m][:, :], rrep[:, :], sem=(s["vc"] if m == 3 else None))
            v_cq = s["vc"].v
            for m in range(4):
                self.bankr(m, s["vc"], v_cq)
            self.wait("tensor", s["sc"], sc_t)
            self.bufw("scalar", "aqf")
            for m in range(4):
                self.bankw(5)
                for k in range(4):
                    self.op("tensor", "matmul", pb[5][:, :],
                            m512[:, k * CC + m * 128:k * CC + (m + 1) * 128],
                            tbf[:, k * 512:(k + 1) * 512],
                            start=(k == 0), stop=(k == 3),
                            sem=(s["tn"] if k == 3 else None))
                self.wait("scalar", s["tn"], s["tn"].v)
                sca = self.op("scalar", "activation", aqf[:, m * 512:(m + 1) * 512],
                              pb[5][:, :], AF.Copy, bias=0.0, scale=T["gvec"][:, 4 + m:5 + m],
                              sem=s["sc"])
                self.bankr(5, s["sc"], sca)
            sc_aq = s["sc"].v
            self.wait("scalar", s["sc"], sc_aq)
            st_aq = self.dma("scalar", d["aq_sc"][i], aqf[:, :], s["st"])
            self.bufr("aqf", s["st"], st_aq)
            self.bufr("tbf", s["tn"], s["tn"].v)
            self.wait("tensor", s["vc"], v_cq)
            self.bufw("vector", "qrf")
            for grp in range(4):
                for u in range(4):
                    m = grp * 4 + u
                    bidx = m % 4
                    self.bankw(bidx)
                    for k in range(4):
                        self.op("tensor", "matmul", pb[bidx][:, :],
                                wqr[:, k * E + m * 128:k * E + (m + 1) * 128],
                                cqb[:, k * 512:(k + 1) * 512],
                                start=(k == 0), stop=(k == 3),
                                sem=(s["tn"] if k == 3 else None))
                t_q = s["tn"].v
                self.wait("vector", s["tn"], t_q)
                v_rp = self._rope4(grp * 4, [pb[0], pb[1], pb[2], pb[3]], qrf, 512,
                                   cosd[:, i * 512:(i + 1) * 512],
                                   sind[:, i * 512:(i + 1) * 512])
                for bb in range(4):
                    self.bankr(bb, s["vc"], v_rp)
            v_qr = s["vc"].v
            self.wait("scalar", s["vc"], v_qr)
            st_qr = self.dma("scalar", d["qr_sc"][i], qrf[:, :], s["st"])
            self.bufr("qrf", s["st"], st_qr)
            self.bufr("hb0", s["tn"], s["tn"].v)
        self.fence("k2")
        self.free("ones", "rows", "sqs2", "rrep", "cqb", "tbf", "aqf", "qrf",
                  "s_wqr", "s_wdq", "hb0", "s_sin", "s_cos")

    def attn(self):
        cfg, d, s, pb, T = self.cfg, self.d, self.s, self.pb, self.T
        Q, n_qt = cfg.Q, cfg.n_qt
        qr = self.sb("qr", [128, 16 * Q], BF16)
        aq = self.sb("aq", [128, 4 * Q], BF16)
        oa = self.sb("oa", [128, n_qt * 512], F32)
        ml = self.sb("ml", [128, 4 * n_qt], F32)
        krb = self.sb("krb0", [128, 16 * 512], BF16)
        ckvb = [self.sb(f"ckvb{i}", [128, 4 * 512], BF16) for i in range(2)]
        cfm = self.sb("cfm", [128, 4 * 512], BF16)
        pbf = self.sb("pbf", [128, 1024], BF16)
        ptb = self.sb("ptb", [128, 1024], BF16)
        prow = self.sb("prow", [128, 4], F32)
        self.qf = [self.sb(f"qf{i}", [128, 2048], F32) for i in range(2)]
        self.qt = [self.sb(f"qt{i}", [128, 2048], BF16) for i in range(2)]
        identb = T["s_idb"]

        qr3 = qr[:, :].rearrange("p (m q) -> p m q", m=16)
        for i in range(cfg.n_k2):
            self.dma("sync", qr3[:, :, i * 512:(i + 1) * 512],
                     d["qr_sc"][i].rearrange("p (m t) -> p m t", m=16), s["lds"])
        aq3 = aq[:, :].rearrange("p (m q) -> p m q", m=4)
        for i in range(cfg.n_k2):
            self.dma("sync", aq3[:, :, i * 512:(i + 1) * 512],
                     d["aq_sc"][i].rearrange("p (m t) -> p m t", m=4), s["lds"])
        w_qa = s["lds"].v

        self.op("vector", "memset", ml[:, 0:n_qt], -30000.0)
        self.op("vector", "memset", ml[:, n_qt:2 * n_qt], 0.0)
        v_i = self.op("vector", "memset", oa[:, :], 0.0, sem=s["vc"])
        self.wait("tensor", s["vc"], v_i)
        self.wait("tensor", s["lds"], w_qa)

        def emit_S(qts, sbk):
            t_s = [None, None]
            for ci, qt in enumerate(qts):
                self.bankw(sbk[ci])
                for m in range(4):
                    self.op("tensor", "matmul", pb[sbk[ci]][:, :],
                            aq[:, m * Q + qt * 128:m * Q + (qt + 1) * 128],
                            cfm[:, m * 512:(m + 1) * 512],
                            start=(m == 0), stop=False)
                for m in range(16):
                    self.op("tensor", "matmul", pb[sbk[ci]][:, :],
                            qr[:, m * Q + qt * 128:m * Q + (qt + 1) * 128],
                            krb[:, m * 512:(m + 1) * 512],
                            start=False, stop=(m == 15),
                            sem=(s["tn"] if m == 15 else None))
                t_s[ci] = s["tn"].v
            return t_s

        def emit_epi(qts, sbk, t_s, bb):
            v_m = [None, None]
            sc_p = [None, None]
            v_l = [None, None]
            t_pt = [None, None]
            sc_pt = [None, None]
            t_o = [None, None]
            for ci, qt in enumerate(qts):
                mo = ml[:, qt:qt + 1]
                mn = ml[:, 2 * n_qt + qt:2 * n_qt + qt + 1]
                self.wait("vector", s["tn"], t_s[ci])
                self.bufw("vector", f"prow{ci}")
                self.op("vector", "reduce_max", prow[:, 2 * ci:2 * ci + 1],
                        pb[sbk[ci]][:, :], AXX)
                self.op("vector", "tensor_max", mn, prow[:, 2 * ci:2 * ci + 1], mo)
                v_m[ci] = self.op("vector", "tensor_scalar_mul",
                                  prow[:, 2 * ci + 1:2 * ci + 2], mn, -1.0,
                                  sem=s["vc"])
            for ci, qt in enumerate(qts):
                mo = ml[:, qt:qt + 1]
                co = ml[:, 3 * n_qt + qt:3 * n_qt + qt + 1]
                self.wait("scalar", s["vc"], v_m[ci])
                self.op("scalar", "activation", co, mo, AF.Exp,
                        bias=prow[:, 2 * ci + 1:2 * ci + 2], scale=1.0)
                self.bufw("scalar", f"pbf{ci}")
                sc_p[ci] = self.op("scalar", "activation",
                                   pbf[:, ci * 512:(ci + 1) * 512], pb[sbk[ci]][:, :],
                                   AF.Exp, bias=prow[:, 2 * ci + 1:2 * ci + 2],
                                   scale=1.0, accum_out=prow[:, 2 * ci:2 * ci + 1],
                                   sem=s["sc"])
                self.bankr(sbk[ci], s["sc"], sc_p[ci])
                self.bufr(f"prow{ci}", s["sc"], sc_p[ci])
            for ci, qt in enumerate(qts):
                mo = ml[:, qt:qt + 1]
                lo = ml[:, n_qt + qt:n_qt + qt + 1]
                mn = ml[:, 2 * n_qt + qt:2 * n_qt + qt + 1]
                co = ml[:, 3 * n_qt + qt:3 * n_qt + qt + 1]
                self.wait("vector", s["sc"], sc_p[ci])
                self.op("vector", "tensor_copy", mo, mn)
                v_l[ci] = self.op("vector", "scalar_tensor_tensor", lo, lo, co,
                                  prow[:, 2 * ci:2 * ci + 1], ALU.mult, ALU.add,
                                  sem=s["vc"])
            for ci, qt in enumerate(qts):
                pt_i = 2 + ci
                self.wait("tensor", s["sc"], sc_p[ci])
                self.bankw(pt_i)
                for sub in range(4):
                    self.op("tensor", "transpose",
                            self.pbb(pt_i)[:, sub * 128:(sub + 1) * 128],
                            pbf[:, ci * 512 + sub * 128:ci * 512 + (sub + 1) * 128],
                            identb[:, :], sem=(s["tn"] if sub == 3 else None))
                t_pt[ci] = s["tn"].v
                self.bufr(f"pbf{ci}", s["tn"], t_pt[ci])
            for ci, qt in enumerate(qts):
                pt_i = 2 + ci
                self.wait("scalar", s["tn"], t_pt[ci])
                self.bufw("scalar", f"ptb{ci}")
                sc_pt[ci] = self.op("scalar", "copy", ptb[:, ci * 512:(ci + 1) * 512],
                                    self.pbb(pt_i)[:, 0:512], sem=s["sc"])
                self.bankr(pt_i, s["sc"], sc_pt[ci])
            for ci, qt in enumerate(qts):
                o_i = 4 + ci
                self.wait("tensor", s["sc"], sc_pt[ci])
                self.bankw(o_i)
                for sub in range(4):
                    self.op("tensor", "matmul", pb[o_i][:, :],
                            ptb[:, ci * 512 + sub * 128:ci * 512 + (sub + 1) * 128],
                            ckvb[bb][:, sub * 512:(sub + 1) * 512],
                            start=(sub == 0), stop=(sub == 3),
                            sem=(s["tn"] if sub == 3 else None))
                t_o[ci] = s["tn"].v
                self.bufr(f"ptb{ci}", s["tn"], t_o[ci])
            self.bufr(f"ckvb{bb}", s["tn"], t_o[1])
            for ci, qt in enumerate(qts):
                co = ml[:, 3 * n_qt + qt:3 * n_qt + qt + 1]
                self.wait("vector", s["tn"], t_o[ci])
                self.wait("vector", s["vc"], v_l[ci])
                v_o = self.op("vector", "scalar_tensor_tensor",
                              oa[:, qt * 512:(qt + 1) * 512],
                              oa[:, qt * 512:(qt + 1) * 512], co,
                              pb[4 + ci][:, :], ALU.mult, ALU.add, sem=s["vc"])
                self.bankr(4 + ci, s["vc"], v_o)

        pend = None
        gp = 0
        q_start = self.q_idx
        q_left0 = cfg.NQ - q_start
        pairs_total = cfg.n_kt * (n_qt // 2)
        for kt in range(cfg.n_kt):
            b = kt % 2
            self.bufw("sync", "krb0")
            self.bufw("sync", f"ckvb{b}")
            krw = self.dma("sync", krb[:, :], d["kr_sc"][kt], s["lds"])
            cvw = self.dma("sync", ckvb[b][:, :], d["ckv_sc"][kt], s["lds"])
            self.wait("tensor", s["lds"], cvw)
            self.bufw("tensor", "cfm")
            for m in range(4):
                bidx = 2 + m % 2
                self.bankw(bidx)
                for sub in range(4):
                    self.op("tensor", "transpose", self.pbb(bidx)[:, sub * 128:(sub + 1) * 128],
                            ckvb[b][:, sub * 512 + m * 128:sub * 512 + (m + 1) * 128],
                            identb[:, :], sem=(s["tn"] if sub == 3 else None))
                self.wait("scalar", s["tn"], s["tn"].v)
                scv = self.op("scalar", "copy", cfm[:, m * 512:(m + 1) * 512],
                              self.pbb(bidx)[:, 0:512], sem=s["sc"])
                self.bankr(bidx, s["sc"], scv)
            sc_cfm = s["sc"].v
            self.wait("tensor", s["sc"], sc_cfm)
            self.wait("tensor", s["lds"], krw)

            for qp in range(n_qt // 2):
                qts = (2 * qp, 2 * qp + 1)
                sbk = (0, 1) if gp % 2 == 0 else (6, 7)
                gp += 1
                t_s = emit_S(qts, sbk)
                if pend is not None:
                    emit_epi(*pend)
                pend = (qts, sbk, t_s, b)
                tgt = q_start + q_left0 * gp // pairs_total
                self._quant(tgt - self.q_idx)
            self.bufr("krb0", s["tn"], t_s[1])
            self.bufr("cfm", s["tn"], t_s[1])
        emit_epi(*pend)
        self._quant(cfg.NQ - self.q_idx)
        v_last = s["vc"].v
        self.wait("scalar", s["vc"], v_last)
        self.dma("scalar", d["oa_sc"][:, :], oa[:, :], s["st"])
        self.dma("scalar", d["l_sc"][:, :], ml[:, n_qt:2 * n_qt], s["st"])
        self.fence("attn")
        self.free("qt1", "qt0", "qf1", "qf0", "prow", "ptb", "pbf", "cfm",
                  "ckvb1", "ckvb0", "krb0", "ml", "oa", "aq", "qr")

    def epi(self):
        cfg, d, s, pb, T = self.cfg, self.d, self.s, self.pb, self.T
        Q, n_qt = cfg.Q, cfg.n_qt
        identb, identf = T["s_idb"], T["s_idf"]

        oa = self.sb("oa", [128, n_qt * 512], F32)
        lv = self.sb("lv", [128, 2 * n_qt], F32)
        gag3 = self.sb("gag3", [128, E], F32)
        wob = self.sb("wob", [128, 4 * E], BF16)
        obf = self.sb("obf", [128, 512], BF16)
        ofm = self.sb("ofm", [128, 512], BF16)
        usb = self.sb("usb", [128, E], F32)
        sqs = self.sb("sqs3", [128, 512], F32)
        ust = self.sb("ust", [128, 8], F32)
        xg = self.sb("xg", [128, 4 * E], F32)
        st_ = self.sb("epst", [128, 40], F32)
        h2f = self.sb("h2f", [128, 16 * 512], BF16)
        zb = self.sb("zb", [128, 16 * 512], F32)
        mx = self.sb("mx", [128, 512], F32)
        mnb = self.sb("mnb", [128, 512], F32)
        qrow = self.sb("qrow", [1, 3 * 512], F32)
        mxc = self.sb("mxc", [128, 4], F32)
        tmp4 = self.sb("tmp4", [4, 128], F32)
        screp = self.sb("screp", [128, 512], F32)
        xq1 = self.sb("xq1", [128, 16 * 512], BF16)

        w_oa = self.dma("sync", oa[:, :], d["oa_sc"][:, :], s["lds"])
        w_l = self.dma("sync", lv[:, 0:n_qt], d["l_sc"][:, :], s["lds"])
        sc_g = self.bcast(T["gag3r"][0:1, :], gag3, E, 7, wait_vc=self.v_gag3)
        self.wait("sync", s["st"], self.st_wo)
        wo_w = self.dma("sync", wob[:, :], d["wo_sc"][:, :], s["lds"])
        self.wait("vector", s["lds"], w_l)
        self.wait("vector", s["lds"], w_oa)
        for qt in range(n_qt):
            self.op("vector", "reciprocal", lv[:, n_qt + qt:n_qt + qt + 1],
                    lv[:, qt:qt + 1], sem=(s["vc"] if qt == n_qt - 1 else None))

        for g in range(cfg.n_grp):
            self.bufw("sync", "xg")
            xw = self.dma("sync", xg[:, :].rearrange("p (s e) -> p s e", s=4),
                          d["x"][g * 512:(g + 1) * 512, :].rearrange("(s p) e -> p s e", p=128),
                          s["lds"])
            for j in range(4):
                qt = g * 4 + j
                self.bufw("vector", "obf")
                v_ob = self.op("vector", "tensor_scalar_mul", obf[:, :],
                               oa[:, qt * 512:(qt + 1) * 512],
                               lv[:, n_qt + qt:n_qt + qt + 1], sem=s["vc"])
                self.wait("tensor", s["vc"], v_ob)
                self.bankw(0)
                for m in range(4):
                    self.op("tensor", "transpose", self.pbb(0)[:, m * 128:(m + 1) * 128],
                            obf[:, m * 128:(m + 1) * 128], identb[:, :],
                            sem=(s["tn"] if m == 3 else None))
                t_of = s["tn"].v
                self.bufr("obf", s["tn"], t_of)
                self.wait("scalar", s["tn"], t_of)
                self.bufw("scalar", "ofm")
                sc_of = self.op("scalar", "copy", ofm[:, :], self.pbb(0)[:, 0:512], sem=s["sc"])
                self.bankr(0, s["sc"], sc_of)
                self.wait("tensor", s["sc"], sc_of)
                if g == 0 and j == 0:
                    self.wait("tensor", s["lds"], wo_w)
                for n in range(4):
                    bidx = 1 + n % 2
                    self.bankw(bidx)
                    for m in range(4):
                        self.op("tensor", "matmul", pb[bidx][:, :],
                                ofm[:, m * 128:(m + 1) * 128],
                                wob[:, m * E + n * 512:m * E + (n + 1) * 512],
                                start=(m == 0), stop=(m == 3),
                                sem=(s["tn"] if m == 3 else None))
                    t_u = s["tn"].v
                    self.wait("scalar", s["tn"], t_u)
                    self.op("scalar", "copy", usb[:, n * 512:(n + 1) * 512], pb[bidx][:, :])
                    sc_u = self.op("scalar", "activation", sqs[:, :], pb[bidx][:, :],
                                   AF.Square, accum_out=ust[:, n:n + 1], sem=s["sc"])
                    self.bankr(bidx, s["sc"], sc_u)
                sc_un = s["sc"].v
                self.bufr("ofm", s["tn"], s["tn"].v)
                self.wait("vector", s["sc"], sc_un)
                self.op("vector", "tensor_add", ust[:, 4:5], ust[:, 0:1], ust[:, 1:2])
                self.op("vector", "tensor_add", ust[:, 5:6], ust[:, 2:3], ust[:, 3:4])
                v_sq = self.op("vector", "tensor_add", ust[:, 4:5], ust[:, 4:5], ust[:, 5:6],
                               sem=s["vc"])
                self.wait("scalar", s["vc"], v_sq)
                sden = self.op("scalar", "activation", ust[:, 6:7], ust[:, 4:5], AF.Sqrt,
                               bias=self.T["epsc"][:, 0:1], scale=1.0 / E, sem=s["sc"])
                self.wait("vector", s["sc"], sden)
                self.op("vector", "reciprocal", ust[:, 7:8], ust[:, 6:7])
                self.wait("vector", s["lds"], xw)
                if g == 0 and j == 0:
                    self.wait("vector", s["sc"], sc_g)
                xs = xg[:, j * E:(j + 1) * E]
                self.op("vector", "tensor_scalar_mul", usb[:, :], usb[:, :], ust[:, 7:8])
                self.op("vector", "tensor_mul", usb[:, :], usb[:, :], gag3[:, :])
                v_xm = self.op("vector", "tensor_add", xs, xs, usb[:, :], sem=s["vc"])
                self.wait("scalar", s["vc"], v_xm)
                st_xm = self.dma("scalar", d["xmid_sc"][g * 4 + j], xs, s["st"])
            self.wait("vector", s["st"], st_xm)
            self.wait("scalar", s["st"], st_xm)
            v_ln = self._ln4(xg[:, :], st_, sqs)
            self.wait("tensor", s["vc"], v_ln)
            for c in range(16):
                bidx = 3 + c % 2
                self.bankw(bidx)
                for sub in range(4):
                    self.op("tensor", "transpose", pb[bidx][:, sub * 128:(sub + 1) * 128],
                            xg[:, sub * E + c * 128:sub * E + (c + 1) * 128],
                            identf[:, :], sem=(s["tn"] if sub == 3 else None))
                self.wait("vector", s["tn"], s["tn"].v)
                scv = self.op("vector", "tensor_scalar", h2f[:, c * 512:(c + 1) * 512],
                              pb[bidx][:, :], T["ca1"][:, 16 + c:17 + c],
                              T["mod_sb"][:, 48 + c:49 + c],
                              ALU.mult, ALU.add, sem=s["vc"])
                self.bankr(bidx, s["vc"], scv)
            sc_h2 = s["vc"].v
            self.bufr("xg", s["tn"], s["tn"].v)
            self.wait("tensor", s["vc"], sc_h2)
            for c in range(16):
                bidx = 5 + c % 2
                self.bankw(bidx)
                self.op("tensor", "matmul", pb[bidx][:, :], T["h128s"][:, :],
                        h2f[:, c * 512:(c + 1) * 512], start=True, stop=True, sem=s["tn"])
                self.wait("scalar", s["tn"], s["tn"].v)
                scz = self.op("scalar", "copy", zb[:, c * 512:(c + 1) * 512], pb[bidx][:, :],
                              sem=s["sc"])
                self.bankr(bidx, s["sc"], scz)
            sc_z = s["sc"].v
            self.wait("vector", s["sc"], sc_z)
            h = 1
            while h < 16:
                v4 = zb[:, 0:16 * 512].rearrange("p (g two u t) -> p g two u t",
                                                 two=2, u=h, t=512)
                av = v4[:, :, 0]
                bv = v4[:, :, 1]
                self.op("vector", "tensor_add", av, av, bv)
                self.op("vector", "scalar_tensor_tensor", bv, bv, -2.0, av,
                        ALU.mult, ALU.add)
                h *= 2
            self.op("vector", "tensor_max", mx[:, :], zb[:, 0:512], zb[:, 512:1024])
            self.op("vector", "tensor_tensor", mnb[:, :], zb[:, 0:512], zb[:, 512:1024],
                    ALU.min)
            for c in range(2, 16):
                self.op("vector", "tensor_max", mx[:, :], mx[:, :],
                        zb[:, c * 512:(c + 1) * 512])
                self.op("vector", "tensor_tensor", mnb[:, :], mnb[:, :],
                        zb[:, c * 512:(c + 1) * 512], ALU.min)
            self.op("vector", "tensor_scalar_mul", mnb[:, :], mnb[:, :], -1.0)
            self.op("vector", "tensor_max", mx[:, :], mx[:, :], mnb[:, :], sem=s["vc"])
            self.wait("tensor", s["vc"], s["vc"].v)
            w_mr = self._rowmax(mx, mxc, tmp4, qrow[0:1, 1024:1536], bank=7)
            self.wait("vector", s["lds"], w_mr)
            isq = 1.0 / math.sqrt(E)
            self.op("vector", "tensor_scalar", qrow[0:1, 0:512], qrow[0:1, 1024:1536],
                    isq, QEPS, ALU.mult, ALU.max)
            self.op("vector", "tensor_scalar_mul", T["inv1"][0:1, g * 512:(g + 1) * 512],
                    qrow[0:1, 0:512], 1.0 / 127.0)
            self.op("vector", "reciprocal", qrow[0:1, 512:1024],
                    T["inv1"][0:1, g * 512:(g + 1) * 512])
            v_qs = self.op("vector", "tensor_scalar_mul", qrow[0:1, 512:1024],
                           qrow[0:1, 512:1024], isq, sem=s["vc"])
            self.bufw("tensor", "screp")
            sc_sc = self.bcast(qrow[0:1, 512:1024], screp, 512, 7, wait_vc=v_qs)
            self.wait("vector", s["sc"], sc_sc)
            self.bufw("vector", "xq1")
            zv = zb[:, 0:16 * 512].rearrange("p (c t) -> p c t", t=512)
            sv = screp[:, 0:512].rearrange("p (c t) -> p c t", c=1).broadcast_to((128, 16, 512))
            self.op("vector", "tensor_tensor", zv, zv, sv, ALU.mult)
            v_x1 = self.op("vector", "tensor_scalar", xq1[:, :], zb[:, 0:16 * 512],
                           MAGIC, MAGIC, ALU.add, ALU.subtract, sem=s["vc"])
            self.bufr("screp", s["vc"], v_x1)
            self.wait("scalar", s["vc"], v_x1)
            st_x1 = self.dma("scalar", d["xq1_sc"][g], xq1[:, :], s["st"])
            self.bufr("xq1", s["st"], st_x1)
        self.fence("epi")
        self.free("xq1", "screp", "tmp4", "mxc", "qrow", "mnb", "mx", "zb", "h2f", "epst",
                  "xg", "ust", "sqs3", "usb", "ofm", "obf", "wob", "gag3", "lv", "oa")

    def ffn(self):
        cfg, d, s, pb, T = self.cfg, self.d, self.s, self.pb, self.T
        Q, I8 = cfg.Q, cfg.I8
        OCG, OC16, ICG = cfg.OCG, cfg.OC16, cfg.ICG
        xq1s = self.sb("xq1s", [128, 16 * 512], BF16)
        yb = self.sb("yb", [128, OCG * 512], BF16)
        tg = [self.sb(f"tg{i}", [128, E], BF16) for i in range(2)]
        tu = [self.sb(f"tu{i}", [128, E], BF16) for i in range(2)]
        i1r = self.sb("i1r", [128, 512], F32)
        gf = self.sb("gf", [128, 512], F32)
        sf = self.sb("sf", [128, 512], F32)
        uf = self.sb("uf", [128, 512], F32)
        mx2 = self.sb("mx2", [128, 512], F32)
        mxc2 = self.sb("mxc2", [128, 4], F32)
        tmp42 = self.sb("tmp42", [4, 128], F32)
        q2row = self.sb("q2row", [1, 3 * 512], F32)
        scr2 = self.sb("scr2", [128, 512], F32)
        i2r = self.sb("i2r", [128, 512], F32)
        gbsd = self.sb("gbsd", [128, 16], F32)
        wqT = T["wq"]

        v_gb = self.op("vector", "tensor_scalar_mul", gbsd[:, :], T["gbpf"][:, :],
                       wqT[:, 5:6], sem=s["vc"])

        for fs in range(cfg.n_grp):
            self.bufw("sync", "xq1s")
            x1w = self.dma("sync", xq1s[:, :], d["xq1_sc"][fs], s["lds"])
            self.bufw("tensor", "i1r")
            sc_i1 = self.bcast(T["inv1"][0:1, fs * 512:(fs + 1) * 512], i1r, 512, 7)
            self.wait("vector", s["sc"], sc_i1)
            self.wait("tensor", s["lds"], x1w)
            self.bufw("vector", "yb")
            for oc in range(OCG):
                b = oc % 2
                self.bufw("sync", f"tg{b}")
                self.bufw("sync", f"tu{b}")
                tgw = self.dma("sync", tg[b][:, :], d["tg_sc"][oc], s["lds"])
                tuw = self.dma("sync", tu[b][:, :], d["tu_sc"][oc], s["lds"])
                self.wait("tensor", s["lds"], tgw)
                g_i = oc % 2
                self.bankw(g_i)
                for k in range(16):
                    self.op("tensor", "matmul", pb[g_i][:, :],
                            tg[b][:, k * 128:(k + 1) * 128],
                            xq1s[:, k * 512:(k + 1) * 512],
                            start=(k == 0), stop=(k == 15),
                            sem=(s["tn"] if k == 15 else None))
                t_gm = s["tn"].v
                self.wait("tensor", s["lds"], tuw)
                u_i = 2 + oc % 2
                self.bankw(u_i)
                for k in range(16):
                    self.op("tensor", "matmul", pb[u_i][:, :],
                            tu[b][:, k * 128:(k + 1) * 128],
                            xq1s[:, k * 512:(k + 1) * 512],
                            start=(k == 0), stop=(k == 15),
                            sem=(s["tn"] if k == 15 else None))
                t_um = s["tn"].v
                self.bufr(f"tg{b}", s["tn"], t_gm)
                self.bufr(f"tu{b}", s["tn"], t_um)
                self.wait("vector", s["tn"], t_um)
                self.bufw("vector", "gf")
                self.op("vector", "scalar_tensor_tensor", gf[:, :], pb[g_i][:, :],
                        wqT[:, 3:4], i1r[:, :], ALU.mult, ALU.mult, sem=s["vc"])
                v_gf = s["vc"].v
                self.bankr(g_i, s["vc"], v_gf)
                self.wait("scalar", s["vc"], v_gf)
                sc_sf = self.op("scalar", "activation", sf[:, :], gf[:, :], AF.Silu,
                                sem=s["sc"])
                self.bufw("vector", "uf")
                v_uf = self.op("vector", "scalar_tensor_tensor", uf[:, :], pb[u_i][:, :],
                               wqT[:, 4:5], i1r[:, :], ALU.mult, ALU.mult, sem=s["vc"])
                self.bankr(u_i, s["vc"], v_uf)
                self.wait("vector", s["sc"], sc_sf)
                v_y = self.op("vector", "tensor_mul", yb[:, oc * 512:(oc + 1) * 512],
                              sf[:, :], uf[:, :], sem=s["vc"])
                self.bufr("gf", s["sc"], sc_sf)
                self.bufr("uf", s["vc"], v_y)
                last_v_y = v_y
                last_t_um = t_um
            self.bufr("xq1s", s["tn"], last_t_um)
            self.bufr("i1r", s["vc"], last_v_y)
            if cfg.debug:
                self.wait("scalar", s["vc"], last_v_y)
                self.dma("scalar", d["dbg_y"][fs], yb[:, :], s["st"])
                self.wait("tensor", s["st"], s["st"].v)
            for oc in range(OCG):
                bidx = 4 + oc % 2
                self.bankw(bidx)
                self.op("tensor", "matmul", pb[bidx][:, :], T["h128s"][:, :],
                        yb[:, oc * 512:(oc + 1) * 512], start=True, stop=True, sem=s["tn"])
                self.wait("scalar", s["tn"], s["tn"].v)
                scz = self.op("scalar", "copy", yb[:, oc * 512:(oc + 1) * 512],
                              pb[bidx][:, :], sem=s["sc"])
                self.bankr(bidx, s["sc"], scz)
            sc_za = s["sc"].v
            self.wait("vector", s["sc"], sc_za)
            h = 1
            while h < OCG:
                v4 = yb[:, 0:OCG * 512].rearrange("p (g two u t) -> p g two u t",
                                                  two=2, u=h, t=512)
                av = v4[:, :, 0]
                bv = v4[:, :, 1]
                self.op("vector", "tensor_add", av, av, bv)
                self.op("vector", "scalar_tensor_tensor", bv, bv, -2.0, av,
                        ALU.mult, ALU.add)
                h *= 2
            self.op("vector", "tensor_max", mx2[:, :], yb[:, 0:512], yb[:, 512:1024])
            self.op("vector", "tensor_tensor", sf[:, :], yb[:, 0:512], yb[:, 512:1024],
                    ALU.min)
            for c in range(2, OCG):
                self.op("vector", "tensor_max", mx2[:, :], mx2[:, :],
                        yb[:, c * 512:(c + 1) * 512])
                self.op("vector", "tensor_tensor", sf[:, :], sf[:, :],
                        yb[:, c * 512:(c + 1) * 512], ALU.min)
            self.op("vector", "tensor_scalar_mul", sf[:, :], sf[:, :], -1.0)
            self.op("vector", "tensor_max", mx2[:, :], mx2[:, :], sf[:, :], sem=s["vc"])
            self.wait("tensor", s["vc"], s["vc"].v)
            w_mr = self._rowmax(mx2, mxc2, tmp42, q2row[0:1, 1024:1536], bank=7)
            self.wait("vector", s["lds"], w_mr)
            isq = 1.0 / math.sqrt(I8)
            self.op("vector", "tensor_scalar", q2row[0:1, 0:512], q2row[0:1, 1024:1536],
                    isq, QEPS, ALU.mult, ALU.max)
            self.op("vector", "tensor_scalar_mul", q2row[0:1, 512:1024],
                    q2row[0:1, 0:512], 1.0 / 127.0)
            self.op("vector", "reciprocal", q2row[0:1, 0:512], q2row[0:1, 512:1024])
            v_q2 = self.op("vector", "tensor_scalar_mul", q2row[0:1, 0:512],
                           q2row[0:1, 0:512], isq, sem=s["vc"])
            self.bufw("tensor", "scr2")
            self.bufw("tensor", "i2r")
            self.bcast(q2row[0:1, 0:512], scr2, 512, 6, wait_vc=v_q2)
            sc_q2 = self.bcast(q2row[0:1, 512:1024], i2r, 512, 7)
            self.wait("vector", s["sc"], sc_q2)
            for c0 in range(0, OCG, 2):
                self.op("vector", "tensor_mul", gf[:, :],
                        yb[:, c0 * 512:(c0 + 1) * 512], scr2[:, :])
                self.op("vector", "tensor_mul", uf[:, :],
                        yb[:, (c0 + 1) * 512:(c0 + 2) * 512], scr2[:, :])
                self.op("vector", "tensor_scalar", yb[:, c0 * 512:(c0 + 1) * 512],
                        gf[:, :], MAGIC, MAGIC, ALU.add, ALU.subtract)
                self.op("vector", "tensor_scalar", yb[:, (c0 + 1) * 512:(c0 + 2) * 512],
                        uf[:, :], MAGIC, MAGIC, ALU.add, ALU.subtract,
                        sem=(s["vc"] if c0 + 2 >= OCG else None))
            v_xq2 = s["vc"].v
            self.bufr("scr2", s["vc"], v_xq2)
            if cfg.debug:
                self.wait("scalar", s["vc"], v_xq2)
                self.dma("scalar", d["dbg_x2"][fs], yb[:, :], s["st"])
                self.dma("scalar", d["dbg_r"][fs][0:1, :], q2row[0:1, 0:512], s["st"])
                self.dma("scalar", d["dbg_r"][fs][1:2, :], q2row[0:1, 512:1024], s["st"])
                self.wait("tensor", s["st"], s["st"].v)
            self.wait("tensor", s["vc"], v_xq2)
            td = [self.T.get("td0") or self.sb("td0", [128, I8], BF16),
                  self.T.get("td1") or self.sb("td1", [128, I8], BF16)]
            dn = self.T.get("dn") or self.sb("dn", [128, OC16 * 512], BF16)
            self.bufw("vector", "dn")
            for oc in range(OC16):
                b = oc % 2
                self.bufw("sync", f"td{b}")
                tdw = self.dma("sync", td[b][:, :], d["td_sc"][oc], s["lds"])
                self.wait("tensor", s["lds"], tdw)
                d_i = 6 + oc % 2
                self.bankw(d_i)
                for k in range(ICG):
                    self.op("tensor", "matmul", pb[d_i][:, :],
                            td[b][:, k * 128:(k + 1) * 128],
                            yb[:, k * 512:(k + 1) * 512],
                            start=(k == 0), stop=(k == ICG - 1),
                            sem=(s["tn"] if k == ICG - 1 else None))
                t_d = s["tn"].v
                self.bufr(f"td{b}", s["tn"], t_d)
                self.wait("vector", s["tn"], t_d)
                if fs == 0 and oc == 0:
                    self.wait("vector", s["vc"], v_gb)
                v_d = self.op("vector", "scalar_tensor_tensor", dn[:, oc * 512:(oc + 1) * 512],
                              pb[d_i][:, :], gbsd[:, oc:oc + 1], i2r[:, :],
                              ALU.mult, ALU.mult, sem=s["vc"])
                self.bankr(d_i, s["vc"], v_d)
                last_t_d = t_d
            v_dn = s["vc"].v
            self.bufr("i2r", s["vc"], v_dn)
            self.bufr("yb", s["tn"], last_t_d)
            if cfg.debug:
                self.wait("scalar", s["vc"], v_dn)
                self.dma("scalar", d["dbg_dn"][fs], dn[:, :], s["st"])
                self.wait("tensor", s["st"], s["st"].v)
            xm = self.T.get("xm") or self.sb("xm", [128, E], F32)
            ob = self.T.get("ob") or self.sb("ob", [128, E], F32)
            self.wait("tensor", s["vc"], v_dn)
            for tgp in range(4):
                self.bufw("sync", "xm")
                xmw = self.dma("sync", xm[:, :], d["xmid_sc"][fs * 4 + tgp], s["lds"])
                for m in range(4):
                    bidx = m % 4
                    self.bankw(bidx)
                    for u in range(4):
                        oc = m * 4 + u
                        self.op("tensor", "transpose", self.pbb(bidx)[:, u * 128:(u + 1) * 128],
                                dn[:, oc * 512 + tgp * 128:oc * 512 + (tgp + 1) * 128],
                                T["s_idb"][:, :],
                                sem=(s["tn"] if u == 3 else None))
                t_tr = s["tn"].v
                self.bufr("dn", s["tn"], t_tr)
                self.wait("vector", s["tn"], t_tr)
                self.wait("vector", s["lds"], xmw)
                self.bufw("vector", "ob")
                for m in range(4):
                    self.op("vector", "tensor_add", ob[:, m * 512:(m + 1) * 512],
                            xm[:, m * 512:(m + 1) * 512], self.pbb(m)[:, 0:512],
                            sem=(s["vc"] if m == 3 else None))
                v_ob = s["vc"].v
                for m in range(4):
                    self.bankr(m, s["vc"], v_ob)
                self.bufr("xm", s["vc"], v_ob)
                self.wait("scalar", s["vc"], v_ob)
                st_o = self.dma("scalar", d["y"][(fs * 4 + tgp) * 128:(fs * 4 + tgp + 1) * 128, :],
                                ob[:, :], s["fin"])
                self.bufr("ob", s["fin"], st_o)
        self.wait("scalar", s["fin"], s["fin"].v)



def _prep_core_inputs(inputs, cfg, core):
    b, hf = core // 2, core % 2
    S, Q = cfg.S, cfg.Q
    x_full = np.asarray(inputs["L0_attn_h"][b], np.float32)
    own = x_full[hf * Q:(hf + 1) * Q]
    oth = x_full[(1 - hf) * Q:(2 - hf) * Q]
    x = np.ascontiguousarray(np.concatenate([own, oth], 0))
    pos = np.concatenate([np.arange(hf * Q, (hf + 1) * Q),
                          np.arange((1 - hf) * Q, (2 - hf) * Q)])
    cosd, sind = _rope_tables(pos)
    return x, cosd, sind


def _shared_arrays(inputs, cfg):
    f32 = lambda a: np.ascontiguousarray(np.asarray(a, np.float32))
    bf = lambda a: np.ascontiguousarray(np.asarray(a).astype(ml_dtypes.bfloat16))
    sh = {}
    sh["crow_all"] = f32(inputs["L0_c"])
    sh["g1r"] = f32(inputs["L0_attn_g1"])[None, :]
    sh["g2r"] = f32(inputs["L0_attn_g2"])[None, :]
    sh["g3r"] = f32(inputs["L0_attn_g3"])[None, :]
    sh["identf"] = np.eye(128, dtype=np.float32)
    sh["identb"] = np.eye(128).astype(ml_dtypes.bfloat16)
    sh["h128"] = bf(_hadamard(128))
    sh["wmod"] = f32(inputs["L0_wmod"])
    sh["wdkv"] = f32(inputs["L0_attn_w_d_kv"].T)
    sh["wdq"] = f32(inputs["L0_attn_w_d_q"].T)
    sh["wqr"] = f32(np.asarray(inputs["L0_attn_w_qr"])[ROPE_PERM].T)
    sh["wkr"] = f32(np.asarray(inputs["L0_attn_w_kr"])[ROPE_PERM].T)
    sh["wuq"] = f32(inputs["L0_attn_w_u_q"])
    sh["wuk"] = f32(inputs["L0_attn_w_u_k"])
    sh["wuv"] = f32(inputs["L0_attn_w_u_v"])
    sh["wot"] = f32(inputs["L0_attn_w_o"].T)
    sh["gtiled"] = _tile_oc(f32(inputs["L0_ffn_w_gate"]))
    sh["utiled"] = _tile_oc(f32(inputs["L0_ffn_w_up"]))
    sh["dtiled"] = _tile_oc(f32(inputs["L0_ffn_w_down"]))
    return sh


def kernel(**inputs):
    cfg = CFG()
    sh = _shared_arrays(inputs, cfg)
    nc = B(cfg).build()
    in_maps = []
    for core in range(8):
        b = core // 2
        x, cosd, sind = _prep_core_inputs(inputs, cfg, core)
        m = {k: sh[k] for k in sh if k != "crow_all"}
        m["crow"] = sh["crow_all"][b][None, :]
        m["x"] = x
        m["cosd"] = cosd
        m["sind"] = sind
        in_maps.append(m)
    res = run_bass_kernel_spmd(nc, in_maps, core_ids=list(range(8)))
    out = np.empty((4, 2 * cfg.Q, E), np.float32)
    for core in range(8):
        b, hf = core // 2, core % 2
        out[b, hf * cfg.Q:(hf + 1) * cfg.Q] = np.asarray(res.results[core]["y"])
    return out


# version 13 (final)
# speedup vs baseline: 1.2018x; 1.0145x over previous
import math
import contextlib
import numpy as np
import ml_dtypes

import concourse.bass as bass
import concourse.mybir as mybir
from concourse.bass_utils import run_bass_kernel_spmd


F32 = mybir.dt.float32
BF16 = mybir.dt.bfloat16
AF = mybir.ActivationFunctionType
ALU = mybir.AluOpType
AXX = mybir.AxisListType.X

E = 2048
CC = 512
NH = 16
DH = 128
LN_EPS = 1e-5
RMS_EPS = float(np.finfo(np.float32).eps)
QEPS = 1e-5
MAGIC = 12582912.0
INV12 = 1.0 / 12.0


class CFG:
    def __init__(self, S=4096, Q=2048, I8=8192, debug=False, stop=None,
                 dump_scratch=False):
        self.S, self.Q, self.I8 = S, Q, I8
        self.debug = debug
        self.stop = stop
        self.dump_scratch = dump_scratch
        self.n_k1 = S // 512
        self.n_k2 = Q // 512
        self.n_kt = S // 512
        self.n_qt = Q // 128
        self.n_grp = Q // 512
        self.OCG = I8 // 128
        self.ICG = I8 // 128
        self.OC16 = E // 128
        self.NSUB = I8 // 2048
        self.NQ = 2 * self.OCG + self.OC16 * self.NSUB


def _hadamard(n):
    h = np.array([[1.0]], np.float32)
    while h.shape[0] < n:
        h = np.block([[h, h], [h, -h]])
    return h


def _rope_perm():
    base = np.arange(E).reshape(NH, DH)
    ev, od = base[:, 0::2], base[:, 1::2]
    out = []
    for j in range(NH // 2):
        out.append(ev[2 * j]); out.append(ev[2 * j + 1])
        out.append(od[2 * j]); out.append(od[2 * j + 1])
    return np.concatenate(out)

ROPE_PERM = _rope_perm()


def _rope_tables(positions):
    freqs = 1.0 / (10000.0 ** (np.arange(0, DH, 2, dtype=np.float64) / DH))
    ang = positions[:, None].astype(np.float64) * freqs[None, :]
    cos, sin = np.cos(ang).T, np.sin(ang).T
    return (np.ascontiguousarray(np.concatenate([cos, cos], 0).astype(np.float32)),
            np.ascontiguousarray(np.concatenate([sin, sin], 0).astype(np.float32)))


def _tile_oc(w):
    O, K = w.shape
    t = w.reshape(O // 128, 128, K // 128, 128).transpose(0, 3, 2, 1)
    return np.ascontiguousarray(t.reshape(O // 128, 128, K))


class B:

    def __init__(self, cfg: CFG):
        self.cfg = cfg
        self.nc = bass.Bass("TRN2", target_bir_lowering=False, debug=False,
                            detect_race_conditions=False)
        self.es = contextlib.ExitStack()
        self.plan = {k: [] for k in ("sync", "scalar", "vector", "tensor", "gpsimd")}
        self.T = {}
        self._cm = {}
        self.bg = {}
        self.ug = {}
        self._haz = {}

    def em(self, eng, fn):
        self.plan[eng].append(fn)

    def wait(self, eng, sem, val):
        if val is None:
            return
        self.em(eng, lambda e, s=sem.h, v=val: e.wait_ge(s, v))

    _HAZ_ENGINES = ("vector", "scalar")

    @staticmethod
    def _ap_tname(a):
        t = getattr(a, "tensor", None)
        return getattr(t, "name", None) if t is not None else None

    @staticmethod
    def _ap_elems(a):
        try:
            sh = a.shape
            n = 1
            for d in list(sh)[1:]:
                n *= int(d)
            return n
        except Exception:
            return 1

    def op(self, eng, meth, *args, sem=None, inc=1, **kw):
        if eng in self._HAZ_ENGINES:
            outs = set()
            ins_n = set()
            aps = [a for a in args if self._ap_tname(a)]
            elems = 1
            if aps:
                outs.add(self._ap_tname(aps[0]))
                elems = self._ap_elems(aps[0])
                for a in aps[1:]:
                    ins_n.add(self._ap_tname(a))
            ao = kw.get("accum_out")
            if self._ap_tname(ao):
                outs.add(self._ap_tname(ao))
            for k in ("bias", "scale"):
                if self._ap_tname(kw.get(k)):
                    ins_n.add(self._ap_tname(kw.get(k)))
            hist = self._haz.setdefault(eng, [])
            need = False
            if hist and (ins_n & hist[-1][0]):
                need = True
            elif len(hist) >= 2 and (ins_n & hist[-2][0]) and hist[-1][1] < 256:
                need = True
            if need:
                self.em(eng, lambda e: e.drain())
                hist.clear()
            hist.append((outs, elems))
            del hist[:-2]
        tgt = None
        semh = None
        if sem is not None:
            tgt = sem.bump(inc)
            semh = sem.h
        drain_sem = semh is not None and eng in self._HAZ_ENGINES and meth != "dma_start"

        def fn(e, meth=meth, args=args, kw=kw, semh=semh, inc=inc, ds=drain_sem):
            ins = getattr(e, meth)(*args, **kw)
            if semh is not None:
                if ds:
                    e.drain().then_inc(semh, inc)
                else:
                    ins.then_inc(semh, inc)
        self.em(eng, fn)
        if drain_sem:
            self._haz[eng].clear()
        return tgt

    def dma(self, eng, out, in_, sem):
        return self.op(eng, "dma_start", sem=sem, inc=16, out=out, in_=in_)

    def dma_fm(self, eng, dst, src, k, sem):
        in3 = src.rearrange("(k p) c -> p k c", p=128)
        out3 = dst.rearrange("p (k c) -> p k c", k=k)
        return self.dma(eng, out3, in3, sem)

    _uid = 0

    def sb(self, name, shape, dt):
        B._uid += 1
        cm = self.nc.sbuf_tensor(f"{name}__{B._uid}", list(shape), dt)
        t = cm.__enter__()
        assert name not in self._cm, name
        self._cm[name] = cm
        self.T[name] = t
        return t

    def ps(self, name, shape=(128, 512), dt=F32):
        cm = self.nc.psum_tensor(name, list(shape), dt)
        t = cm.__enter__()
        self._cm[name] = cm
        self.T[name] = t
        return t

    def free(self, *names):
        for n in names:
            self._cm.pop(n).__exit__(None, None, None)
            self.T.pop(n)

    def dram(self, name, shape, dt, kind="Internal"):
        if kind == "Internal" and self.cfg.dump_scratch:
            kind = "ExternalOutput"
        return self.nc.dram_tensor(name, list(shape), dt, kind=kind).ap()

    def sem(self, name):
        class S:
            pass
        s = S()
        s.h = self.es.enter_context(self.nc.semaphore(name))
        s.v = 0
        s.bump = lambda n, s=s: (setattr(s, "v", s.v + n), s.v)[1]
        return s

    def fence(self, tag):
        s = self.s
        vals = [(s[n], s[n].v) for n in
                ("ldg", "lds", "st", "tn", "vc", "sc", "gp", "fin")]
        for eng in ("sync", "scalar", "vector", "tensor", "gpsimd"):
            for sem, v in vals:
                if v:
                    self.wait(eng, sem, v)
        self.bg.clear()
        self.ug.clear()

    def pbb(self, idx):
        return self.pb[idx][:, :].bitcast(BF16)

    def bankw(self, idx):
        g = self.bg.get(idx)
        if g:
            self.wait("tensor", g[0], g[1])

    def bankr(self, idx, sem, val):
        self.bg[idx] = (sem, val)

    def bufw(self, eng, name):
        g = self.ug.get(name)
        if g:
            self.wait(eng, g[0], g[1])

    def bufr(self, name, sem, val):
        self.ug[name] = (sem, val)

    def build(self):
        cfg = self.cfg
        S, Q, I8 = cfg.S, cfg.Q, cfg.I8
        d = self.d = {}
        di = lambda n, sh, dt=F32: d.__setitem__(n, self.dram(n, sh, dt, "ExternalInput"))
        di("x", (S, E)); di("crow", (1, E)); di("g1r", (1, CC)); di("g2r", (1, CC))
        di("g3r", (1, E)); di("identf", (128, 128)); di("identb", (128, 128), BF16)
        di("h128", (128, 128), BF16); di("cosd", (128, S)); di("sind", (128, S))
        di("wmod", (E, 6 * E)); di("wdkv", (E, CC)); di("wdq", (E, CC))
        di("wqr", (CC, E)); di("wkr", (E, E)); di("wuq", (E, CC)); di("wuk", (E, CC))
        di("wuv", (E, CC)); di("wot", (E, E))
        di("gtiled", (cfg.OCG, 128, E)); di("utiled", (cfg.OCG, 128, E))
        di("dtiled", (cfg.OC16, 128, I8))
        d["y"] = self.dram("y", (Q, E), F32, "ExternalOutput")
        d["h_sc"] = self.dram("h_sc", (cfg.n_k1, 128, 16 * 512), BF16)
        d["kr_sc"] = self.dram("kr_sc", (cfg.n_k1, 128, 16 * 512), BF16)
        d["ckv_sc"] = self.dram("ckv_sc", (cfg.n_k1, 128, 4 * 512), BF16)
        d["xmid_sc"] = self.dram("xmid_sc", (Q // 128, 128, E), F32)
        d["xq1_sc"] = self.dram("xq1_sc", (cfg.n_grp, 128, 16 * 512), BF16)
        d["wo_sc"] = self.dram("wo_sc", (128, 4 * E), BF16)
        d["qr_sc"] = self.dram("qr_sc", (cfg.n_k2, 128, 16 * 512), BF16)
        d["aq_sc"] = self.dram("aq_sc", (cfg.n_k2, 128, 4 * 512), BF16)
        d["oa_sc"] = self.dram("oa_sc", (128, cfg.n_qt * 512), F32)
        d["l_sc"] = self.dram("l_sc", (128, cfg.n_qt), F32)
        if cfg.debug:
            d["dbg_p0"] = self.dram("dbg_p0", (128, 256), F32)
            d["dbg_m512"] = self.dram("dbg_m512", (128, 4 * CC), BF16)
            d["dbg_y"] = self.dram("dbg_y", (cfg.n_grp, 128, cfg.OCG * 512), BF16)
            d["dbg_x2"] = self.dram("dbg_x2", (cfg.n_grp, 128, cfg.OCG * 512), BF16)
            d["dbg_dn"] = self.dram("dbg_dn", (cfg.n_grp, 128, cfg.OC16 * 512), BF16)
            d["dbg_r"] = self.dram("dbg_r", (cfg.n_grp, 2, 512), F32)
        d["tg_sc"] = self.dram("tg_sc", (cfg.OCG, 128, E), BF16)
        d["tu_sc"] = self.dram("tu_sc", (cfg.OCG, 128, E), BF16)
        d["td_sc"] = self.dram("td_sc", (cfg.OC16, 128, I8), BF16)

        self.sb("s_idf", [128, 128], F32)
        self.sb("s_idb", [128, 128], BF16)
        self.sb("h128s", [128, 128], BF16)
        self.sb("mod_sb", [128, 96], F32)
        self.sb("gvec", [128, 8], F32)
        self.sb("ca1", [128, 32], F32)
        self.sb("gag3r", [1, E], F32)
        self.sb("wq", [128, 8], F32)
        self.sb("gbpf", [128, 16], F32)
        self.sb("wsum", [128, cfg.NQ], F32)
        self.sb("m512", [128, 4 * CC], BF16)
        self.sb("epsc", [128, 1], F32)
        self.sb("onesr", [1, 128], F32)
        self.sb("onesc", [128, 1], F32)
        self.sb("wrow", [1, 8], F32)
        self.sb("ropeA", [128, 512], F32)
        self.sb("ropeB", [128, 512], F32)
        self.sb("inv1", [1, Q], F32)
        for i in range(8):
            self.ps(f"pb{i}")
        self.pb = [self.T[f"pb{i}"] for i in range(8)]

        self.s = {n: self.sem(n) for n in
                  ("ldg", "lds", "st", "tn", "vc", "sc", "gp", "fin")}

        phases = [("p0", self.p0), ("k1a", self.k1a), ("k1b", self.k1b),
                  ("k2", self.k2), ("attn", self.attn), ("epi", self.epi),
                  ("ffn", self.ffn)]
        for name, fn in phases:
            fn()
            if cfg.stop == name:
                break

        with self.nc.Block() as blk:
            blk.sync(lambda e: [f(e) for f in self.plan["sync"]])
            blk.scalar(lambda e: [f(e) for f in self.plan["scalar"]])
            blk.vector(lambda e: [f(e) for f in self.plan["vector"]])
            blk.tensor(lambda e: [f(e) for f in self.plan["tensor"]])
            blk.gpsimd(lambda e: [f(e) for f in self.plan["gpsimd"]])
        self.es.close()
        return self.nc

    def p0(self):
        cfg, d, s, pb, T = self.cfg, self.d, self.s, self.pb, self.T

        self.dma("sync", T["s_idf"][:, :], d["identf"][:, :], s["lds"])
        self.dma("sync", T["s_idb"][:, :], d["identb"][:, :], s["lds"])
        self.dma("sync", T["h128s"][:, :], d["h128"][:, :], s["lds"])
        self.op("vector", "memset", self.T["epsc"][:, :], RMS_EPS)
        self.op("vector", "memset", self.T["onesr"][:, :], 1.0)
        self.op("vector", "memset", self.T["onesc"][:, :], 1.0)
        g123 = self.sb("g123", [1, 2 * CC + E], F32)
        csf = self.sb("csf", [128, 16], BF16)
        crow = self.sb("s_crow", [1, E], F32)
        csrow = self.sb("csrow", [1, E], F32)
        self.dma("sync", crow[:, :], d["crow"][:, :], s["lds"])
        self.dma("sync", g123[:, 0:CC], d["g1r"][:, :], s["lds"])
        self.dma("sync", g123[:, CC:2 * CC], d["g2r"][:, :], s["lds"])
        w_const = self.dma("sync", g123[:, 2 * CC:], d["g3r"][:, :], s["lds"])

        self.wait("scalar", s["lds"], w_const)
        sc0 = self.op("scalar", "activation", csrow[:, :], crow[:, :], AF.Silu, sem=s["sc"])
        self.wait("tensor", s["sc"], sc0)
        for j in range(16):
            self.op("tensor", "transpose", pb[0][:, j:j + 1],
                    csrow[0:1, j * 128:(j + 1) * 128], T["s_idf"][0:1, 0:1],
                    sem=(s["tn"] if j == 15 else None))
        t_csf = s["tn"].v
        self.wait("vector", s["tn"], t_csf)
        v_csf = self.op("vector", "tensor_copy", csf[:, :], pb[0][:, 0:16], sem=s["vc"])
        self.bankr(0, s["vc"], v_csf)

        self.wait("tensor", s["vc"], v_csf)
        for j in range(4):
            self.op("tensor", "transpose", pb[1][:, j:j + 1],
                    g123[0:1, j * 128:(j + 1) * 128], T["s_idf"][0:1, 0:1])
        for j in range(4):
            self.op("tensor", "transpose", pb[1][:, 4 + j:5 + j],
                    g123[0:1, CC + j * 128:CC + (j + 1) * 128], T["s_idf"][0:1, 0:1],
                    sem=(s["tn"] if j == 3 else None))
        t_g = s["tn"].v
        self.wait("vector", s["tn"], t_g)
        self.op("vector", "tensor_copy", T["gvec"][:, 0:4], pb[1][:, 0:4])
        v_g = self.op("vector", "tensor_scalar_mul", T["gvec"][:, 4:8], pb[1][:, 4:8],
                      INV12, sem=s["vc"])
        self.bankr(1, s["vc"], v_g)

        self.free("csrow", "s_crow")
        wuq = self.sb("s_wuq", [128, 16 * CC], BF16)
        wuk = self.sb("s_wuk", [128, 16 * CC], BF16)
        self.wait("gpsimd", s["tn"], t_g)
        self.dma_fm("gpsimd", wuq[:, :], d["wuq"][:, :], 16, s["ldg"])
        g_uk = self.dma_fm("gpsimd", wuk[:, :], d["wuk"][:, :], 16, s["ldg"])
        m512 = T["m512"]
        self.wait("tensor", s["ldg"], g_uk)
        for m in range(4):
            self.bankw(4)
            for k in range(16):
                self.op("tensor", "matmul", pb[4][:, :],
                        wuq[:, k * CC + m * 128:k * CC + (m + 1) * 128],
                        wuk[:, k * CC:(k + 1) * CC],
                        start=(k == 0), stop=(k == 15),
                        sem=(s["tn"] if k == 15 else None))
            self.wait("scalar", s["tn"], s["tn"].v)
            scm = self.op("scalar", "copy", m512[:, m * CC:(m + 1) * CC], pb[4][:, :],
                          sem=s["sc"])
            self.bankr(4, s["sc"], scm)
            t_m512_done = s["tn"].v
        self.free("s_wuk", "s_wuq")

        wuv = self.sb("s_wuv", [128, 16 * CC], BF16)
        wot = self.sb("s_wot", [128, 16 * E], BF16)
        self.wait("gpsimd", s["tn"], t_m512_done)
        self.dma_fm("gpsimd", wuv[:, :], d["wuv"][:, :], 16, s["ldg"])
        g_wo = self.dma_fm("gpsimd", wot[:, :], d["wot"][:, :], 16, s["ldg"])
        g2raw = self.sb("g2raw", [128, 4], F32)
        v_g2 = self.op("vector", "tensor_scalar_mul", g2raw[:, :], T["gvec"][:, 4:8], 12.0,
                       sem=s["vc"])
        self.wait("scalar", s["vc"], v_g2)
        wos = self.sb("wos", [128, 2 * 512], BF16)
        self.wait("tensor", s["ldg"], g_wo)
        st_wo = None
        for mn in range(16):
            m, n = divmod(mn, 4)
            bidx = 5 + mn % 2
            self.bankw(bidx)
            for k in range(16):
                self.op("tensor", "matmul", pb[bidx][:, :],
                        wuv[:, k * CC + m * 128:k * CC + (m + 1) * 128],
                        wot[:, k * E + n * 512:k * E + (n + 1) * 512],
                        start=(k == 0), stop=(k == 15),
                        sem=(s["tn"] if k == 15 else None))
            self.wait("scalar", s["tn"], s["tn"].v)
            self.bufw("scalar", f"wos{mn % 2}")
            scv = self.op("scalar", "activation", wos[:, (mn % 2) * 512:(mn % 2 + 1) * 512],
                          pb[bidx][:, :], AF.Copy, bias=0.0, scale=g2raw[:, m:m + 1],
                          sem=s["sc"])
            self.bankr(bidx, s["sc"], scv)
            self.wait("scalar", s["sc"], scv)
            st_wo = self.dma("scalar", d["wo_sc"][:, mn * 512:(mn + 1) * 512],
                             wos[:, (mn % 2) * 512:(mn % 2 + 1) * 512], s["st"])
            self.bufr(f"wos{mn % 2}", s["st"], st_wo)
        self.st_wo = st_wo
        t_wo_done = s["tn"].v
        self.free("wos", "g2raw", "s_wot", "s_wuv")

        wmodb = [self.sb(f"wmodb{i}", [128, 6 * E], BF16) for i in range(2)]
        self.wait("gpsimd", s["tn"], t_wo_done)
        self.wait("gpsimd", s["st"], st_wo)
        self.op("vector", "memset", T["mod_sb"][:, :], 0.0)
        for k in range(16):
            self.bufw("gpsimd", f"wmodb{k % 2}")
            wv = self.dma("gpsimd", wmodb[k % 2][:, :], d["wmod"][k * 128:(k + 1) * 128, :],
                          s["ldg"])
            self.wait("tensor", s["ldg"], wv)
            bidx = 2 + k % 2
            self.bankw(bidx)
            for j in range(96):
                self.op("tensor", "matmul", pb[bidx][:, j:j + 1],
                        wmodb[k % 2][:, j * 128:(j + 1) * 128], csf[:, k:k + 1],
                        start=True, stop=True,
                        sem=(s["tn"] if j == 95 else None))
            tk = s["tn"].v
            self.bufr(f"wmodb{k % 2}", s["tn"], tk)
            self.wait("vector", s["tn"], tk)
            v_acc = self.op("vector", "tensor_add", T["mod_sb"][:, :], T["mod_sb"][:, :],
                            pb[bidx][:, 0:96], sem=s["vc"])
            self.bankr(bidx, s["vc"], v_acc)
        v_mod = s["vc"].v

        self.op("vector", "tensor_scalar_add", T["ca1"][:, 0:16], T["mod_sb"][:, 16:32], 1.0)
        self.op("vector", "tensor_scalar_add", T["ca1"][:, 16:32], T["mod_sb"][:, 64:80], 1.0)
        gsc = self.sb("gsc", [128, 16], F32)
        self.op("vector", "tensor_single_scalar", gsc[:, :], T["mod_sb"][:, 80:96], 0.0, ALU.is_ge)
        v_sg = self.op("vector", "tensor_scalar", gsc[:, :], gsc[:, :], 2.0, -1.0,
                       ALU.mult, ALU.add, sem=s["vc"])
        self.wait("scalar", s["vc"], v_sg)
        sc_ab = self.op("scalar", "activation", T["gbpf"][:, :], T["mod_sb"][:, 80:96],
                        AF.Abs, sem=s["sc"])
        self.wait("vector", s["sc"], sc_ab)
        self.op("vector", "tensor_scalar_max", T["gbpf"][:, :], T["gbpf"][:, :], 0.25)
        self.op("vector", "tensor_mul", T["gbpf"][:, :], T["gbpf"][:, :], gsc[:, :])

        gab = self.sb("gab", [16, 128], F32)
        rowb = self.sb("rowb", [1, 2 * E], F32)
        self.wait("tensor", s["vc"], v_mod)
        self.bankw(3)
        t_ga = self.op("tensor", "transpose", pb[3][0:16, 0:128], T["mod_sb"][:, 32:48],
                       T["s_idf"][:, :], sem=s["tn"])
        self.wait("vector", s["tn"], t_ga)
        v_gab = self.op("vector", "tensor_copy", gab[:, :], pb[3][0:16, 0:128], sem=s["vc"])
        self.bankr(3, s["vc"], v_gab)
        self.wait("sync", s["vc"], v_gab)
        for j in range(16):
            w_gar = self.dma("sync", rowb[0:1, j * 128:(j + 1) * 128],
                             gab[j:j + 1, :], s["lds"])
        scr = rowb[0:1, E:2 * E]
        self.wait("vector", s["lds"], w_gar)
        self.op("vector", "tensor_single_scalar", scr, rowb[0:1, 0:E], 0.0, ALU.is_ge)
        v_s2 = self.op("vector", "tensor_scalar", scr, scr, 2.0, -1.0, ALU.mult, ALU.add,
                       sem=s["vc"])
        self.wait("scalar", s["vc"], v_s2)
        sc_a2 = self.op("scalar", "activation", rowb[0:1, 0:E], rowb[0:1, 0:E], AF.Abs,
                        sem=s["sc"])
        self.wait("vector", s["sc"], sc_a2)
        self.op("vector", "tensor_scalar_max", rowb[0:1, 0:E], rowb[0:1, 0:E], 0.25)
        self.op("vector", "tensor_mul", rowb[0:1, 0:E], rowb[0:1, 0:E], scr)
        self.op("vector", "tensor_mul", rowb[0:1, 0:E], rowb[0:1, 0:E], g123[:, 2 * CC:])
        self.v_gag3 = self.op("vector", "tensor_copy", T["gag3r"][0:1, :], rowb[0:1, 0:E],
                              sem=s["vc"])
        self.free("rowb", "gab", "gsc", "wmodb1", "wmodb0", "csf", "g123")

        if cfg.debug:
            self.wait("scalar", s["vc"], s["vc"].v)
            self.wait("scalar", s["sc"], s["sc"].v)
            self.dma("scalar", d["dbg_p0"][:, 0:96], T["mod_sb"][:, :], s["st"])
            self.dma("scalar", d["dbg_p0"][:, 96:128], T["ca1"][:, :], s["st"])
            self.dma("scalar", d["dbg_p0"][:, 128:136], T["gvec"][:, :], s["st"])
            self.dma("scalar", d["dbg_p0"][:, 140:156], T["gbpf"][:, :], s["st"])
            self.dma("scalar", d["dbg_m512"][:, :], T["m512"][:, :], s["st"])
        self.fence("p0")

        chunks = []
        for oc in range(cfg.OCG):
            chunks.append((d["gtiled"][oc], d["tg_sc"][oc], 0, oc))
        for oc in range(cfg.OCG):
            chunks.append((d["utiled"][oc], d["tu_sc"][oc], 1, cfg.OCG + oc))
        for oc in range(cfg.OC16):
            for j in range(cfg.NSUB):
                chunks.append((d["dtiled"][oc][:, j * 2048:(j + 1) * 2048],
                               d["td_sc"][oc][:, j * 2048:(j + 1) * 2048],
                               2, 2 * cfg.OCG + oc * cfg.NSUB + j))
        self.q_chunks = chunks
        mchunks = []
        for oc in range(0, cfg.OCG, 2):
            mchunks.append((d["gtiled"][oc:oc + 2].rearrange("a p e -> p a e"), len(mchunks)))
        for oc in range(0, cfg.OCG, 2):
            mchunks.append((d["utiled"][oc:oc + 2].rearrange("a p e -> p a e"), len(mchunks)))
        for oc in range(cfg.OC16):
            if cfg.NSUB >= 2:
                for j in range(0, cfg.NSUB, 2):
                    mchunks.append((d["dtiled"][oc][:, j * 2048:(j + 2) * 2048], len(mchunks)))
            elif oc % 2 == 0:
                mchunks.append((d["dtiled"][oc:oc + 2].rearrange("a p e -> p a e"),
                                len(mchunks)))
        self.m_chunks = mchunks
        self.m_cols = {"g": (0, cfg.OCG // 2),
                       "u": (cfg.OCG // 2, cfg.OCG),
                       "d": (cfg.OCG, len(mchunks))}
        self.m_idx = 0

    def _mean(self, n):
        cfg, d, s, T = self.cfg, self.d, self.s, self.T
        for _ in range(n):
            if self.m_idx >= len(self.m_chunks):
                return
            idx = self.m_idx
            self.m_idx += 1
            ap, col = self.m_chunks[idx]
            b = idx % 2
            self.bufw("gpsimd", f"wmb{b}")
            dst = self.T[f"wmb{b}"][:, :]
            if len(ap.shape) == 3:
                dst = dst.rearrange("p (a e) -> p a e", a=2)
            wv = self.dma("gpsimd", dst, ap, s["ldg"])
            if idx % 2 == 0:
                self.wait("vector", s["ldg"], wv)
                gv = self.op("vector", "tensor_reduce", T["wsum"][:, col:col + 1],
                             self.T[f"wmb{b}"][:, :], AXX, ALU.add, True, sem=s["vc"])
                self.bufr(f"wmb{b}", s["vc"], gv)
            else:
                self.wait("scalar", s["ldg"], wv)
                sv = self.op("scalar", "activation", T["mabs"][:, :],
                             self.T[f"wmb{b}"][:, :], AF.Abs,
                             accum_out=T["wsum"][:, col:col + 1], sem=s["sc"])
                self.bufr(f"wmb{b}", s["sc"], sv)

    def _ln4(self, xt, st_, sqs, xw_scalar=None):
        s = self.s
        for sub in range(4):
            self.op("vector", "reduce_sum", st_[:, sub:sub + 1],
                    xt[:, sub * E:(sub + 1) * E], AXX)
        if xw_scalar is not None:
            self.wait("scalar", s["lds"], xw_scalar)
        for sub in range(4):
            for bq in range(4):
                self.op("scalar", "activation", sqs[:, :],
                        xt[:, sub * E + bq * 512:sub * E + (bq + 1) * 512],
                        AF.Square, accum_out=st_[:, 4 + sub * 4 + bq:5 + sub * 4 + bq],
                        sem=(s["sc"] if sub == 3 and bq == 3 else None))
        sq_done = s["sc"].v
        self.wait("vector", s["sc"], sq_done)
        qv = st_[:, 4:20].rearrange("p (a b) -> p a b", b=4)
        self.op("vector", "tensor_reduce", st_[:, 20:24], qv, AXX, ALU.add)
        self.op("vector", "tensor_scalar_mul", st_[:, 24:28], st_[:, 0:4], 1.0 / E)
        self.op("vector", "tensor_scalar_mul", st_[:, 28:32], st_[:, 20:24], 1.0 / E)
        self.op("vector", "tensor_mul", st_[:, 32:36], st_[:, 24:28], st_[:, 24:28])
        self.op("vector", "tensor_sub", st_[:, 28:32], st_[:, 28:32], st_[:, 32:36])
        self.op("vector", "tensor_scalar_add", st_[:, 28:32], st_[:, 28:32], LN_EPS)
        v_rc = self.op("vector", "reciprocal", st_[:, 28:32], st_[:, 28:32], sem=s["vc"])
        self.wait("scalar", s["vc"], v_rc)
        s_sq = self.op("scalar", "activation", st_[:, 32:36], st_[:, 28:32], AF.Sqrt,
                       sem=s["sc"])
        self.wait("vector", s["sc"], s_sq)
        for sub in range(4):
            v = self.op("vector", "tensor_scalar", xt[:, sub * E:(sub + 1) * E],
                        xt[:, sub * E:(sub + 1) * E], st_[:, 24 + sub:25 + sub],
                        st_[:, 32 + sub:33 + sub], ALU.subtract, ALU.mult,
                        sem=(s["vc"] if sub == 3 else None))
        return v

    def k1a(self):
        cfg, d, s, pb, T = self.cfg, self.d, self.s, self.pb, self.T
        self.sb("wmb0", [128, 4096], BF16)
        self.sb("wmb1", [128, 4096], BF16)
        self.sb("mabs", [128, 4096], BF16)
        wdkv = self.sb("s_wdkv", [128, 16 * CC], BF16)
        g_kv = self.dma_fm("gpsimd", wdkv[:, :], d["wdkv"][:, :], 16, s["ldg"])
        xa = [self.sb(f"xa{i}", [128, 4 * E], F32) for i in range(2)]
        hf = [self.sb(f"hf{i}", [128, 16 * 512], BF16) for i in range(2)]
        cvsb = self.sb("cvsb", [128, 4 * 512], BF16)
        ckvo = self.sb("ckvo", [128, 4 * 512], BF16)
        st_ = self.sb("k1st", [128, 40], F32)
        sqs = self.sb("sqs", [128, 512], F32)

        for i in range(cfg.n_k1):
            bi = i % 2
            self.bufw("sync", f"xa{bi}")
            xw = self.dma("sync", xa[bi][:, :].rearrange("p (s e) -> p s e", s=4),
                          d["x"][i * 512:(i + 1) * 512, :].rearrange("(s p) e -> p s e", p=128),
                          s["lds"])
            self.wait("vector", s["lds"], xw)
            if i == 0:
                self.wait("vector", s["vc"], self.v_gag3)
            v_ln = self._ln4(xa[bi][:, :], st_, sqs, xw_scalar=xw)
            self.wait("tensor", s["vc"], v_ln)
            if i == 0:
                self.wait("tensor", s["ldg"], g_kv)
            self.bufw("tensor", f"hf{bi}")
            for c in range(16):
                bidx = c % 2
                self.bankw(bidx)
                for sub in range(4):
                    self.op("tensor", "transpose", pb[bidx][:, sub * 128:(sub + 1) * 128],
                            xa[bi][:, sub * E + c * 128:sub * E + (c + 1) * 128],
                            T["s_idf"][:, :], sem=(s["tn"] if sub == 3 else None))
                self.wait("vector", s["tn"], s["tn"].v)
                scv = self.op("vector", "tensor_scalar", hf[bi][:, c * 512:(c + 1) * 512],
                              pb[bidx][:, :], T["ca1"][:, c:c + 1], T["mod_sb"][:, c:c + 1],
                              ALU.mult, ALU.add, sem=s["vc"])
                self.bankr(bidx, s["vc"], scv)
            sc_h = s["vc"].v
            self.wait("scalar", s["vc"], sc_h)
            sth = self.dma("scalar", d["h_sc"][i], hf[bi][:, :], s["st"])
            self.wait("tensor", s["vc"], sc_h)
            for m in range(4):
                bidx = 2 + m % 2
                self.bankw(bidx)
                for k in range(16):
                    self.op("tensor", "matmul", pb[bidx][:, :],
                            wdkv[:, k * CC + m * 128:k * CC + (m + 1) * 128],
                            hf[bi][:, k * 512:(k + 1) * 512],
                            start=(k == 0), stop=(k == 15),
                            sem=(s["tn"] if k == 15 else None))
                self.wait("scalar", s["tn"], s["tn"].v)
                scv = self.op("scalar", "copy", cvsb[:, m * 512:(m + 1) * 512],
                              pb[bidx][:, :], sem=s["sc"])
                self.bankr(bidx, s["sc"], scv)
            sc_cv = s["sc"].v
            self.wait("tensor", s["sc"], sc_cv)
            self.bufw("vector", "ckvo")
            for sub in range(4):
                bidx = 4 + sub % 2
                self.bankw(bidx)
                for m in range(4):
                    self.op("tensor", "transpose", self.pbb(bidx)[:, m * 128:(m + 1) * 128],
                            cvsb[:, m * 512 + sub * 128:m * 512 + (sub + 1) * 128],
                            T["s_idb"][:, :], sem=(s["tn"] if m == 3 else None))
                t_tm = s["tn"].v
                self.wait("scalar", s["tn"], t_tm)
                s1 = self.op("scalar", "activation", sqs[:, :], self.pbb(bidx)[:, 0:512],
                             AF.Square, accum_out=st_[:, 29:30], sem=s["sc"])
                self.wait("scalar", s["sc"], s1)
                s2 = self.op("scalar", "activation", st_[:, 30:31], st_[:, 29:30], AF.Sqrt,
                             bias=self.T["epsc"][:, 0:1], scale=1.0 / CC, sem=s["sc"])
                self.wait("vector", s["sc"], s2)
                self.op("vector", "reciprocal", st_[:, 31:32], st_[:, 30:31])
                vv = self.op("vector", "tensor_scalar_mul", ckvo[:, sub * 512:(sub + 1) * 512],
                             self.pbb(bidx)[:, 0:512], st_[:, 31:32], sem=s["vc"])
                self.bankr(bidx, s["vc"], vv)
            v_cv = s["vc"].v
            self.wait("scalar", s["vc"], v_cv)
            stc = self.dma("scalar", d["ckv_sc"][i], ckvo[:, :], s["st"])
            tm = s["tn"].v
            self.bufr(f"xa{bi}", s["tn"], tm)
            self.bufr(f"hf{bi}", s["st"], sth)
            self.bufr("ckvo", s["st"], stc)
            self._mean(-(-len(self.m_chunks) // cfg.n_k1))
        self._mean(len(self.m_chunks))
        if cfg.debug:
            self.wait("scalar", s["gp"], s["gp"].v)
            self.wait("scalar", s["sc"], s["sc"].v)
            self.dma("scalar", d["dbg_p0"][:, 156:156 + min(64, len(self.m_chunks))],
                     T["wsum"][:, 0:min(64, len(self.m_chunks))], s["st"])
        self.fence("k1a")
        self.free("sqs", "k1st", "ckvo", "cvsb", "hf1", "hf0", "xa1", "xa0",
                  "s_wdkv", "mabs", "wmb1", "wmb0")

    def k1b(self):
        cfg, d, s, pb, T = self.cfg, self.d, self.s, self.pb, self.T
        wq = T["wq"]
        for j, mat in enumerate(("g", "u", "d")):
            lo, hi = self.m_cols[mat]
            v_r = self.op("vector", "tensor_reduce", wq[:, 3 + j:4 + j],
                          T["wsum"][:, lo:hi], AXX, ALU.add,
                          sem=(s["vc"] if j == 2 else None))
        wrow = T["wrow"]
        self.wait("tensor", s["vc"], v_r)
        self.bankw(4)
        t_ws = self.op("tensor", "matmul", pb[4][0:1, 0:3], T["onesc"][:, 0:1],
                       wq[:, 3:6], start=True, stop=True, sem=s["tn"])
        self.wait("scalar", s["tn"], t_ws)
        sc_w = self.op("scalar", "copy", wrow[0:1, 0:3], pb[4][0:1, 0:3], sem=s["sc"])
        self.bankr(4, s["sc"], sc_w)
        self.wait("vector", s["sc"], sc_w)
        nel = float(cfg.I8 * E)
        self.op("vector", "tensor_scalar", wrow[0:1, 0:3], wrow[0:1, 0:3],
                1.0 / nel, QEPS, ALU.mult, ALU.max)
        v_w2 = self.op("vector", "reciprocal", wrow[0:1, 3:6], wrow[0:1, 0:3], sem=s["vc"])
        self.wait("tensor", s["vc"], v_w2)
        self.bankw(4)
        t_wb = self.op("tensor", "matmul", pb[4][:, 0:6], T["onesr"][0:1, :],
                       wrow[0:1, 0:6], start=True, stop=True, sem=s["tn"])
        self.wait("scalar", s["tn"], t_wb)
        self.op("scalar", "copy", wq[:, 3:6], pb[4][:, 0:3])
        sc_wb = self.op("scalar", "copy", wq[:, 0:3], pb[4][:, 3:6], sem=s["sc"])
        self.bankr(4, s["sc"], sc_wb)
        self.wq_ready = sc_wb
        self.q_idx = 0

        cosd = self.sb("s_cos", [128, cfg.S], F32)
        sind = self.sb("s_sin", [128, cfg.S], F32)
        hb = self.sb("hb0", [128, 16 * 512], BF16)
        wkr = self.sb("s_wkr", [128, 16 * E], BF16)
        krf = [self.sb(f"krf{i}", [128, 16 * 512], BF16) for i in range(2)]
        self.qf = [self.sb(f"qf{i}", [128, 2048], F32) for i in range(2)]
        self.qt = [self.sb(f"qt{i}", [128, 2048], BF16) for i in range(2)]

        self.dma("sync", cosd[:, :], d["cosd"][:, :], s["lds"])
        w_tb = self.dma("sync", sind[:, :], d["sind"][:, :], s["lds"])
        g_kr = self.dma_fm("gpsimd", wkr[:, :], d["wkr"][:, :], 16, s["ldg"])

        nq_per = max(1, cfg.NQ // (2 * cfg.n_k1))
        for i in range(cfg.n_k1):
            bi = i % 2
            self.bufw("sync", "hb0")
            hw = self.dma("sync", hb[:, :], d["h_sc"][i], s["lds"])
            self.wait("tensor", s["lds"], hw)
            if i == 0:
                self.wait("tensor", s["ldg"], g_kr)
                self.wait("vector", s["lds"], w_tb)
            self.bufw("vector", f"krf{bi}")
            for grp in range(4):
                for u in range(4):
                    m = grp * 4 + u
                    bidx = m % 4
                    self.bankw(bidx)
                    for k in range(16):
                        self.op("tensor", "matmul", pb[bidx][:, :],
                                wkr[:, k * E + m * 128:k * E + (m + 1) * 128],
                                hb[:, k * 512:(k + 1) * 512],
                                start=(k == 0), stop=(k == 15),
                                sem=(s["tn"] if k == 15 else None))
                t_grp = s["tn"].v
                self.wait("vector", s["tn"], t_grp)
                v_rp = self._rope4(grp * 4, [pb[0], pb[1], pb[2], pb[3]], krf[bi], 512,
                                   cosd[:, i * 512:(i + 1) * 512],
                                   sind[:, i * 512:(i + 1) * 512])
                for bidx in range(4):
                    self.bankr(bidx, s["vc"], v_rp)
            v_kr = s["vc"].v
            self.bufr("hb0", s["tn"], s["tn"].v)
            self.wait("scalar", s["vc"], v_kr)
            stk = self.dma("scalar", d["kr_sc"][i], krf[bi][:, :], s["st"])
            self.bufr(f"krf{bi}", s["st"], stk)
            self._quant(nq_per)
        self._quant((cfg.NQ + 1) // 2 - self.q_idx)
        self.fence("k1b")
        self.free("qt1", "qt0", "qf1", "qf0", "krf1", "krf0", "s_wkr", "hb0",
                  "s_sin", "s_cos")

    def _rope4(self, base, banks, out, outw, cosap, sinap):
        T = self.T
        ret = None
        for u in range(2):
            x1 = banks[2 * u][:, :]
            x2 = banks[2 * u + 1][:, :]
            o1 = out[:, (base + 2 * u) * outw:(base + 2 * u + 1) * outw]
            o2 = out[:, (base + 2 * u + 1) * outw:(base + 2 * u + 2) * outw]
            t1 = T["ropeA"][:, :]
            t2 = T["ropeB"][:, :]
            self.op("vector", "tensor_mul", t1, x1, cosap)
            self.op("vector", "tensor_mul", t2, x2, sinap)
            self.op("vector", "tensor_sub", o1, t1, t2)
            self.op("vector", "tensor_mul", t1, x1, sinap)
            self.op("vector", "tensor_mul", t2, x2, cosap)
            ret = self.op("vector", "tensor_add", o2, t1, t2,
                          sem=(self.s["vc"] if u == 1 else None))
        return ret

    def bcast(self, row, out_sb, n, bank, wait_vc=None):
        s, pb, T = self.s, self.pb, self.T
        if wait_vc is not None:
            self.wait("tensor", s["vc"], wait_vc)
        sc = None
        for j0 in range(0, n, 512):
            w = min(512, n - j0)
            self.bankw(bank)
            t = self.op("tensor", "matmul", pb[bank][:, 0:w], T["onesr"][0:1, :],
                        row[0:1, j0:j0 + w], start=True, stop=True, sem=s["tn"])
            self.wait("scalar", s["tn"], t)
            sc = self.op("scalar", "copy", out_sb[:, j0:j0 + w], pb[bank][:, 0:w],
                         sem=s["sc"])
            self.bankr(bank, s["sc"], sc)
        return sc

    def _rowmax(self, mx, mxc, tmp4, outrow, bank=7):
        s, pb, T = self.s, self.pb, self.T
        self.bankw(bank)
        for j in range(4):
            self.op("tensor", "transpose", pb[bank][:, j * 128:(j + 1) * 128],
                    mx[:, j * 128:(j + 1) * 128], T["s_idf"][:, :],
                    sem=(s["tn"] if j == 3 else None))
        t_tr = s["tn"].v
        self.wait("vector", s["tn"], t_tr)
        for j in range(4):
            self.op("vector", "reduce_max", mxc[:, j:j + 1],
                    pb[bank][:, j * 128:(j + 1) * 128], AXX,
                    sem=(s["vc"] if j == 3 else None))
        v_mc = s["vc"].v
        self.bankr(bank, s["vc"], v_mc)
        self.wait("tensor", s["vc"], v_mc)
        self.bankw(bank)
        t2 = self.op("tensor", "transpose", pb[bank][0:4, 0:128], mxc[:, 0:4],
                     T["s_idf"][:, :], sem=s["tn"])
        self.wait("vector", s["tn"], t2)
        v_t4 = self.op("vector", "tensor_copy", tmp4[:, :], pb[bank][0:4, 0:128],
                       sem=s["vc"])
        self.bankr(bank, s["vc"], v_t4)
        self.wait("sync", s["vc"], v_t4)
        w = None
        for j in range(4):
            w = self.dma("sync", outrow[0:1, j * 128:(j + 1) * 128], tmp4[j:j + 1, :],
                         s["lds"])
        return w

    def _quant(self, n):
        cfg, d, s = self.cfg, self.d, self.s
        wq = self.T["wq"]
        while n > 0:
            if self.q_idx >= len(self.q_chunks):
                return
            pair = []
            for _ in range(min(2, n, len(self.q_chunks) - self.q_idx)):
                idx = self.q_idx
                self.q_idx += 1
                n -= 1
                srcap, dst, j, _ = self.q_chunks[idx]
                b = idx % 2
                if idx == 0:
                    self.wait("scalar", s["sc"], self.wq_ready)
                self.bufw("sync", f"qf{b}")
                wv = self.dma("sync", self.qf[b][:, :], srcap, s["lds"])
                pair.append((idx, b, dst, j, wv))
            for _, b, _, j, wv in pair:
                self.wait("scalar", s["lds"], wv)
                sv = self.op("scalar", "activation", self.qf[b][:, :],
                             self.qf[b][:, :], AF.Copy, scale=wq[:, j:j + 1],
                             sem=s["sc"])
                pairsc = sv
                self.wait("vector", s["sc"], sv)
                self.bufw("vector", f"qt{b}")
            for _, b, _, _, _ in pair:
                f = self.qf[b]
                self.op("vector", "tensor_scalar", f[:, :], f[:, :], MAGIC,
                        MAGIC, ALU.add, ALU.subtract)
            for _, b, _, _, _ in pair:
                f, t = self.qf[b], self.qt[b]
                qv = self.op("vector", "tensor_scalar", t[:, :], f[:, :], 1.0,
                             -1.0, ALU.min, ALU.max, sem=s["vc"])
                self.bufr(f"qf{b}", s["vc"], qv)
                self.wait("scalar", s["vc"], qv)
                stq = self.dma("scalar",
                               [p for p in pair if p[1] == b][0][2], t[:, :],
                               s["st"])
                self.bufr(f"qt{b}", s["st"], stq)

    def k2(self):
        cfg, d, s, pb, T = self.cfg, self.d, self.s, self.pb, self.T
        cosd = self.sb("s_cos", [128, cfg.Q], F32)
        sind = self.sb("s_sin", [128, cfg.Q], F32)
        hb = self.sb("hb0", [128, 16 * 512], BF16)
        wdq = self.sb("s_wdq", [128, 16 * CC], BF16)
        wqr = self.sb("s_wqr", [128, 4 * E], BF16)
        qrf = self.sb("qrf", [128, 16 * 512], BF16)
        aqf = self.sb("aqf", [128, 4 * 512], BF16)
        tbf = self.sb("tbf", [128, 4 * 512], BF16)
        cqb = self.sb("cqb", [128, 4 * 512], BF16)
        rrep = self.sb("rrep", [128, 512], F32)
        sqs2 = self.sb("sqs2", [128, 512], F32)
        rows = self.sb("rows", [1, 1024], F32)
        ones = self.sb("ones", [128, 1], F32)

        self.dma("sync", cosd[:, :], d["cosd"][:, 0:cfg.Q], s["lds"])
        w_tb = self.dma("sync", sind[:, :], d["sind"][:, 0:cfg.Q], s["lds"])
        g_dq = self.dma_fm("gpsimd", wdq[:, :], d["wdq"][:, :], 16, s["ldg"])
        g_qr = self.dma_fm("gpsimd", wqr[:, :], d["wqr"][:, :], 4, s["ldg"])
        self.wait("vector", s["ldg"], g_qr)
        for k in range(4):
            self.op("vector", "tensor_scalar_mul", wqr[:, k * E:(k + 1) * E],
                    wqr[:, k * E:(k + 1) * E], T["gvec"][:, k:k + 1],
                    sem=(s["vc"] if k == 3 else None))
        v_wqr = s["vc"].v
        gp_o = None
        v_ones = self.op("vector", "memset", ones[:, :], 1.0, sem=s["vc"])
        m512 = T["m512"]

        for i in range(cfg.n_k2):
            self.bufw("sync", "hb0")
            hw = self.dma("sync", hb[:, :], d["h_sc"][i], s["lds"])
            self.wait("tensor", s["lds"], hw)
            if i == 0:
                self.wait("tensor", s["ldg"], g_dq)
                self.wait("tensor", s["vc"], v_ones)
                self.wait("tensor", s["vc"], v_wqr)
                self.wait("vector", s["lds"], w_tb)
            for m in range(4):
                self.bankw(m)
                for k in range(16):
                    self.op("tensor", "matmul", pb[m][:, :],
                            wdq[:, k * CC + m * 128:k * CC + (m + 1) * 128],
                            hb[:, k * 512:(k + 1) * 512],
                            start=(k == 0), stop=(k == 15),
                            sem=(s["tn"] if (k == 15 and m == 3) else None))
            t_t = s["tn"].v
            self.wait("scalar", s["tn"], t_t)
            self.bufw("scalar", "tbf")
            for m in range(4):
                self.op("scalar", "copy", tbf[:, m * 512:(m + 1) * 512], pb[m][:, :],
                        sem=(s["sc"] if m == 3 else None))
            sc_t = s["sc"].v
            for m in range(4):
                if m > 0:
                    self.wait("scalar", s["tn"], t_sqm)
                sq = self.op("scalar", "activation", sqs2[:, :], pb[m][:, :], AF.Square,
                             sem=s["sc"])
                self.wait("tensor", s["sc"], sq)
                if m == 0:
                    self.bankw(4)
                t_sqm = self.op("tensor", "matmul", pb[4][0:1, :], ones[:, 0:1], sqs2[:, :],
                                start=(m == 0), stop=(m == 3), sem=s["tn"])
            t_sq = s["tn"].v
            self.wait("scalar", s["tn"], t_sq)
            s_den = self.op("scalar", "activation", rows[0:1, 0:512], pb[4][0:1, :], AF.Sqrt,
                            bias=self.T["epsc"][0:1, 0:1], scale=1.0 / CC, sem=s["sc"])
            self.bankr(4, s["sc"], s_den)
            self.wait("vector", s["sc"], s_den)
            self.op("vector", "reciprocal", rows[0:1, 512:1024], rows[0:1, 0:512])
            v_rr = self.op("vector", "tensor_scalar_mul", rows[0:1, 512:1024],
                           rows[0:1, 512:1024], INV12, sem=s["vc"])
            sc_b = self.bcast(rows[0:1, 512:1024], rrep, 512, 4, wait_vc=v_rr)
            self.wait("vector", s["sc"], sc_b)
            self.bufw("vector", "cqb")
            for m in range(4):
                self.op("vector", "tensor_mul", cqb[:, m * 512:(m + 1) * 512],
                        pb[m][:, :], rrep[:, :], sem=(s["vc"] if m == 3 else None))
            v_cq = s["vc"].v
            for m in range(4):
                self.bankr(m, s["vc"], v_cq)
            self.wait("tensor", s["sc"], sc_t)
            self.bufw("scalar", "aqf")
            for m in range(4):
                self.bankw(5)
                for k in range(4):
                    self.op("tensor", "matmul", pb[5][:, :],
                            m512[:, k * CC + m * 128:k * CC + (m + 1) * 128],
                            tbf[:, k * 512:(k + 1) * 512],
                            start=(k == 0), stop=(k == 3),
                            sem=(s["tn"] if k == 3 else None))
                self.wait("scalar", s["tn"], s["tn"].v)
                sca = self.op("scalar", "activation", aqf[:, m * 512:(m + 1) * 512],
                              pb[5][:, :], AF.Copy, bias=0.0, scale=T["gvec"][:, 4 + m:5 + m],
                              sem=s["sc"])
                self.bankr(5, s["sc"], sca)
            sc_aq = s["sc"].v
            self.wait("scalar", s["sc"], sc_aq)
            st_aq = self.dma("scalar", d["aq_sc"][i], aqf[:, :], s["st"])
            self.bufr("aqf", s["st"], st_aq)
            self.bufr("tbf", s["tn"], s["tn"].v)
            self.wait("tensor", s["vc"], v_cq)
            self.bufw("vector", "qrf")
            for grp in range(4):
                for u in range(4):
                    m = grp * 4 + u
                    bidx = m % 4
                    self.bankw(bidx)
                    for k in range(4):
                        self.op("tensor", "matmul", pb[bidx][:, :],
                                wqr[:, k * E + m * 128:k * E + (m + 1) * 128],
                                cqb[:, k * 512:(k + 1) * 512],
                                start=(k == 0), stop=(k == 3),
                                sem=(s["tn"] if k == 3 else None))
                t_q = s["tn"].v
                self.wait("vector", s["tn"], t_q)
                v_rp = self._rope4(grp * 4, [pb[0], pb[1], pb[2], pb[3]], qrf, 512,
                                   cosd[:, i * 512:(i + 1) * 512],
                                   sind[:, i * 512:(i + 1) * 512])
                for bb in range(4):
                    self.bankr(bb, s["vc"], v_rp)
            v_qr = s["vc"].v
            self.wait("scalar", s["vc"], v_qr)
            st_qr = self.dma("scalar", d["qr_sc"][i], qrf[:, :], s["st"])
            self.bufr("qrf", s["st"], st_qr)
            self.bufr("hb0", s["tn"], s["tn"].v)
        self.fence("k2")
        self.free("ones", "rows", "sqs2", "rrep", "cqb", "tbf", "aqf", "qrf",
                  "s_wqr", "s_wdq", "hb0", "s_sin", "s_cos")

    def attn(self):
        cfg, d, s, pb, T = self.cfg, self.d, self.s, self.pb, self.T
        Q, n_qt = cfg.Q, cfg.n_qt
        qr = self.sb("qr", [128, 16 * Q], BF16)
        aq = self.sb("aq", [128, 4 * Q], BF16)
        oa = self.sb("oa", [128, n_qt * 512], F32)
        ml = self.sb("ml", [128, 4 * n_qt], F32)
        krb = self.sb("krb0", [128, 16 * 512], BF16)
        ckvb = [self.sb(f"ckvb{i}", [128, 4 * 512], BF16) for i in range(2)]
        cfm = self.sb("cfm", [128, 4 * 512], BF16)
        pbf = self.sb("pbf", [128, 1024], BF16)
        ptb = self.sb("ptb", [128, 1024], BF16)
        prow = self.sb("prow", [128, 4], F32)
        self.qf = [self.sb(f"qf{i}", [128, 2048], F32) for i in range(2)]
        self.qt = [self.sb(f"qt{i}", [128, 2048], BF16) for i in range(2)]
        identb = T["s_idb"]

        qr3 = qr[:, :].rearrange("p (m q) -> p m q", m=16)
        for i in range(cfg.n_k2):
            self.dma("sync", qr3[:, :, i * 512:(i + 1) * 512],
                     d["qr_sc"][i].rearrange("p (m t) -> p m t", m=16), s["lds"])
        aq3 = aq[:, :].rearrange("p (m q) -> p m q", m=4)
        for i in range(cfg.n_k2):
            self.dma("sync", aq3[:, :, i * 512:(i + 1) * 512],
                     d["aq_sc"][i].rearrange("p (m t) -> p m t", m=4), s["lds"])
        w_qa = s["lds"].v

        self.op("vector", "memset", ml[:, 0:n_qt], -30000.0)
        self.op("vector", "memset", ml[:, n_qt:2 * n_qt], 0.0)
        v_i = self.op("vector", "memset", oa[:, :], 0.0, sem=s["vc"])
        self.wait("tensor", s["vc"], v_i)
        self.wait("tensor", s["lds"], w_qa)

        def emit_S(qts, sbk):
            t_s = [None, None]
            for ci, qt in enumerate(qts):
                self.bankw(sbk[ci])
                for m in range(4):
                    self.op("tensor", "matmul", pb[sbk[ci]][:, :],
                            aq[:, m * Q + qt * 128:m * Q + (qt + 1) * 128],
                            cfm[:, m * 512:(m + 1) * 512],
                            start=(m == 0), stop=False)
                for m in range(16):
                    self.op("tensor", "matmul", pb[sbk[ci]][:, :],
                            qr[:, m * Q + qt * 128:m * Q + (qt + 1) * 128],
                            krb[:, m * 512:(m + 1) * 512],
                            start=False, stop=(m == 15),
                            sem=(s["tn"] if m == 15 else None))
                t_s[ci] = s["tn"].v
            return t_s

        def emit_epi(qts, sbk, t_s, bb):
            v_m = [None, None]
            sc_p = [None, None]
            v_l = [None, None]
            t_pt = [None, None]
            sc_pt = [None, None]
            t_o = [None, None]
            for ci, qt in enumerate(qts):
                mo = ml[:, qt:qt + 1]
                mn = ml[:, 2 * n_qt + qt:2 * n_qt + qt + 1]
                self.wait("vector", s["tn"], t_s[ci])
                self.bufw("vector", f"prow{ci}")
                self.op("vector", "reduce_max", prow[:, 2 * ci:2 * ci + 1],
                        pb[sbk[ci]][:, :], AXX)
                self.op("vector", "tensor_max", mn, prow[:, 2 * ci:2 * ci + 1], mo)
                v_m[ci] = self.op("vector", "tensor_scalar_mul",
                                  prow[:, 2 * ci + 1:2 * ci + 2], mn, -1.0,
                                  sem=s["vc"])
            for ci, qt in enumerate(qts):
                mo = ml[:, qt:qt + 1]
                co = ml[:, 3 * n_qt + qt:3 * n_qt + qt + 1]
                self.wait("scalar", s["vc"], v_m[ci])
                self.op("scalar", "activation", co, mo, AF.Exp,
                        bias=prow[:, 2 * ci + 1:2 * ci + 2], scale=1.0)
                self.bufw("scalar", f"pbf{ci}")
                sc_p[ci] = self.op("scalar", "activation",
                                   pbf[:, ci * 512:(ci + 1) * 512], pb[sbk[ci]][:, :],
                                   AF.Exp, bias=prow[:, 2 * ci + 1:2 * ci + 2],
                                   scale=1.0, accum_out=prow[:, 2 * ci:2 * ci + 1],
                                   sem=s["sc"])
                self.bankr(sbk[ci], s["sc"], sc_p[ci])
                self.bufr(f"prow{ci}", s["sc"], sc_p[ci])
            for ci, qt in enumerate(qts):
                mo = ml[:, qt:qt + 1]
                lo = ml[:, n_qt + qt:n_qt + qt + 1]
                mn = ml[:, 2 * n_qt + qt:2 * n_qt + qt + 1]
                co = ml[:, 3 * n_qt + qt:3 * n_qt + qt + 1]
                self.wait("vector", s["sc"], sc_p[ci])
                self.op("vector", "tensor_copy", mo, mn)
                v_l[ci] = self.op("vector", "scalar_tensor_tensor", lo, lo, co,
                                  prow[:, 2 * ci:2 * ci + 1], ALU.mult, ALU.add,
                                  sem=s["vc"])
            for ci, qt in enumerate(qts):
                pt_i = 2 + ci
                self.wait("tensor", s["sc"], sc_p[ci])
                self.bankw(pt_i)
                for sub in range(4):
                    self.op("tensor", "transpose",
                            self.pbb(pt_i)[:, sub * 128:(sub + 1) * 128],
                            pbf[:, ci * 512 + sub * 128:ci * 512 + (sub + 1) * 128],
                            identb[:, :], sem=(s["tn"] if sub == 3 else None))
                t_pt[ci] = s["tn"].v
                self.bufr(f"pbf{ci}", s["tn"], t_pt[ci])
            for ci, qt in enumerate(qts):
                pt_i = 2 + ci
                self.wait("scalar", s["tn"], t_pt[ci])
                self.bufw("scalar", f"ptb{ci}")
                sc_pt[ci] = self.op("scalar", "copy", ptb[:, ci * 512:(ci + 1) * 512],
                                    self.pbb(pt_i)[:, 0:512], sem=s["sc"])
                self.bankr(pt_i, s["sc"], sc_pt[ci])
            for ci, qt in enumerate(qts):
                o_i = 4 + ci
                self.wait("tensor", s["sc"], sc_pt[ci])
                self.bankw(o_i)
                for sub in range(4):
                    self.op("tensor", "matmul", pb[o_i][:, :],
                            ptb[:, ci * 512 + sub * 128:ci * 512 + (sub + 1) * 128],
                            ckvb[bb][:, sub * 512:(sub + 1) * 512],
                            start=(sub == 0), stop=(sub == 3),
                            sem=(s["tn"] if sub == 3 else None))
                t_o[ci] = s["tn"].v
                self.bufr(f"ptb{ci}", s["tn"], t_o[ci])
            self.bufr(f"ckvb{bb}", s["tn"], t_o[1])
            for ci, qt in enumerate(qts):
                co = ml[:, 3 * n_qt + qt:3 * n_qt + qt + 1]
                self.wait("vector", s["tn"], t_o[ci])
                self.wait("vector", s["vc"], v_l[ci])
                v_o = self.op("vector", "scalar_tensor_tensor",
                              oa[:, qt * 512:(qt + 1) * 512],
                              oa[:, qt * 512:(qt + 1) * 512], co,
                              pb[4 + ci][:, :], ALU.mult, ALU.add, sem=s["vc"])
                self.bankr(4 + ci, s["vc"], v_o)

        pend = None
        gp = 0
        q_start = self.q_idx
        q_left0 = cfg.NQ - q_start
        pairs_total = cfg.n_kt * (n_qt // 2)
        for kt in range(cfg.n_kt):
            b = kt % 2
            self.bufw("sync", "krb0")
            self.bufw("sync", f"ckvb{b}")
            krw = self.dma("sync", krb[:, :], d["kr_sc"][kt], s["lds"])
            cvw = self.dma("sync", ckvb[b][:, :], d["ckv_sc"][kt], s["lds"])
            self.wait("tensor", s["lds"], cvw)
            self.bufw("tensor", "cfm")
            for m in range(4):
                bidx = 2 + m % 2
                self.bankw(bidx)
                for sub in range(4):
                    self.op("tensor", "transpose", self.pbb(bidx)[:, sub * 128:(sub + 1) * 128],
                            ckvb[b][:, sub * 512 + m * 128:sub * 512 + (m + 1) * 128],
                            identb[:, :], sem=(s["tn"] if sub == 3 else None))
                self.wait("scalar", s["tn"], s["tn"].v)
                scv = self.op("scalar", "copy", cfm[:, m * 512:(m + 1) * 512],
                              self.pbb(bidx)[:, 0:512], sem=s["sc"])
                self.bankr(bidx, s["sc"], scv)
            sc_cfm = s["sc"].v
            self.wait("tensor", s["sc"], sc_cfm)
            self.wait("tensor", s["lds"], krw)

            for qp in range(n_qt // 2):
                qts = (2 * qp, 2 * qp + 1)
                sbk = (0, 1) if gp % 2 == 0 else (6, 7)
                gp += 1
                t_s = emit_S(qts, sbk)
                if pend is not None:
                    emit_epi(*pend)
                pend = (qts, sbk, t_s, b)
                tgt = q_start + q_left0 * gp // pairs_total
                self._quant(tgt - self.q_idx)
            self.bufr("krb0", s["tn"], t_s[1])
            self.bufr("cfm", s["tn"], t_s[1])
        emit_epi(*pend)
        self._quant(cfg.NQ - self.q_idx)
        v_last = s["vc"].v
        self.wait("scalar", s["vc"], v_last)
        self.dma("scalar", d["oa_sc"][:, :], oa[:, :], s["st"])
        self.dma("scalar", d["l_sc"][:, :], ml[:, n_qt:2 * n_qt], s["st"])
        self.fence("attn")
        self.free("qt1", "qt0", "qf1", "qf0", "prow", "ptb", "pbf", "cfm",
                  "ckvb1", "ckvb0", "krb0", "ml", "oa", "aq", "qr")

    def epi(self):
        cfg, d, s, pb, T = self.cfg, self.d, self.s, self.pb, self.T
        Q, n_qt = cfg.Q, cfg.n_qt
        identb, identf = T["s_idb"], T["s_idf"]

        oa = self.sb("oa", [128, n_qt * 512], F32)
        lv = self.sb("lv", [128, 2 * n_qt], F32)
        gag3 = self.sb("gag3", [128, E], F32)
        wob = self.sb("wob", [128, 4 * E], BF16)
        obf = self.sb("obf", [128, 512], BF16)
        ofm = self.sb("ofm", [128, 512], BF16)
        usb = self.sb("usb", [128, E], F32)
        sqs = self.sb("sqs3", [128, 512], F32)
        ust = self.sb("ust", [128, 8], F32)
        xg = self.sb("xg", [128, 4 * E], F32)
        st_ = self.sb("epst", [128, 40], F32)
        h2f = self.sb("h2f", [128, 16 * 512], BF16)
        zb = self.sb("zb", [128, 16 * 512], F32)
        mx = self.sb("mx", [128, 512], F32)
        mnb = self.sb("mnb", [128, 512], F32)
        qrow = self.sb("qrow", [1, 3 * 512], F32)
        mxc = self.sb("mxc", [128, 4], F32)
        tmp4 = self.sb("tmp4", [4, 128], F32)
        screp = self.sb("screp", [128, 512], F32)
        xq1 = self.sb("xq1", [128, 16 * 512], BF16)

        w_oa = self.dma("sync", oa[:, :], d["oa_sc"][:, :], s["lds"])
        w_l = self.dma("sync", lv[:, 0:n_qt], d["l_sc"][:, :], s["lds"])
        sc_g = self.bcast(T["gag3r"][0:1, :], gag3, E, 7, wait_vc=self.v_gag3)
        self.wait("sync", s["st"], self.st_wo)
        wo_w = self.dma("sync", wob[:, :], d["wo_sc"][:, :], s["lds"])
        self.wait("vector", s["lds"], w_l)
        self.wait("vector", s["lds"], w_oa)
        for qt in range(n_qt):
            self.op("vector", "reciprocal", lv[:, n_qt + qt:n_qt + qt + 1],
                    lv[:, qt:qt + 1], sem=(s["vc"] if qt == n_qt - 1 else None))

        for g in range(cfg.n_grp):
            self.bufw("sync", "xg")
            xw = self.dma("sync", xg[:, :].rearrange("p (s e) -> p s e", s=4),
                          d["x"][g * 512:(g + 1) * 512, :].rearrange("(s p) e -> p s e", p=128),
                          s["lds"])
            for j in range(4):
                qt = g * 4 + j
                self.bufw("vector", "obf")
                v_ob = self.op("vector", "tensor_scalar_mul", obf[:, :],
                               oa[:, qt * 512:(qt + 1) * 512],
                               lv[:, n_qt + qt:n_qt + qt + 1], sem=s["vc"])
                self.wait("tensor", s["vc"], v_ob)
                self.bankw(0)
                for m in range(4):
                    self.op("tensor", "transpose", self.pbb(0)[:, m * 128:(m + 1) * 128],
                            obf[:, m * 128:(m + 1) * 128], identb[:, :],
                            sem=(s["tn"] if m == 3 else None))
                t_of = s["tn"].v
                self.bufr("obf", s["tn"], t_of)
                self.wait("scalar", s["tn"], t_of)
                self.bufw("scalar", "ofm")
                sc_of = self.op("scalar", "copy", ofm[:, :], self.pbb(0)[:, 0:512], sem=s["sc"])
                self.bankr(0, s["sc"], sc_of)
                self.wait("tensor", s["sc"], sc_of)
                if g == 0 and j == 0:
                    self.wait("tensor", s["lds"], wo_w)
                for n in range(4):
                    bidx = 1 + n % 2
                    self.bankw(bidx)
                    for m in range(4):
                        self.op("tensor", "matmul", pb[bidx][:, :],
                                ofm[:, m * 128:(m + 1) * 128],
                                wob[:, m * E + n * 512:m * E + (n + 1) * 512],
                                start=(m == 0), stop=(m == 3),
                                sem=(s["tn"] if m == 3 else None))
                    t_u = s["tn"].v
                    self.wait("scalar", s["tn"], t_u)
                    self.op("scalar", "copy", usb[:, n * 512:(n + 1) * 512], pb[bidx][:, :])
                    sc_u = self.op("scalar", "activation", sqs[:, :], pb[bidx][:, :],
                                   AF.Square, accum_out=ust[:, n:n + 1], sem=s["sc"])
                    self.bankr(bidx, s["sc"], sc_u)
                sc_un = s["sc"].v
                self.bufr("ofm", s["tn"], s["tn"].v)
                self.wait("vector", s["sc"], sc_un)
                self.op("vector", "tensor_add", ust[:, 4:5], ust[:, 0:1], ust[:, 1:2])
                self.op("vector", "tensor_add", ust[:, 5:6], ust[:, 2:3], ust[:, 3:4])
                v_sq = self.op("vector", "tensor_add", ust[:, 4:5], ust[:, 4:5], ust[:, 5:6],
                               sem=s["vc"])
                self.wait("scalar", s["vc"], v_sq)
                sden = self.op("scalar", "activation", ust[:, 6:7], ust[:, 4:5], AF.Sqrt,
                               bias=self.T["epsc"][:, 0:1], scale=1.0 / E, sem=s["sc"])
                self.wait("vector", s["sc"], sden)
                self.op("vector", "reciprocal", ust[:, 7:8], ust[:, 6:7])
                self.wait("vector", s["lds"], xw)
                if g == 0 and j == 0:
                    self.wait("vector", s["sc"], sc_g)
                xs = xg[:, j * E:(j + 1) * E]
                self.op("vector", "tensor_scalar_mul", usb[:, :], usb[:, :], ust[:, 7:8])
                self.op("vector", "tensor_mul", usb[:, :], usb[:, :], gag3[:, :])
                v_xm = self.op("vector", "tensor_add", xs, xs, usb[:, :], sem=s["vc"])
                self.wait("scalar", s["vc"], v_xm)
                st_xm = self.dma("scalar", d["xmid_sc"][g * 4 + j], xs, s["st"])
            self.wait("vector", s["st"], st_xm)
            self.wait("scalar", s["st"], st_xm)
            v_ln = self._ln4(xg[:, :], st_, sqs)
            self.wait("tensor", s["vc"], v_ln)
            for c in range(16):
                bidx = 3 + c % 2
                self.bankw(bidx)
                for sub in range(4):
                    self.op("tensor", "transpose", pb[bidx][:, sub * 128:(sub + 1) * 128],
                            xg[:, sub * E + c * 128:sub * E + (c + 1) * 128],
                            identf[:, :], sem=(s["tn"] if sub == 3 else None))
                self.wait("vector", s["tn"], s["tn"].v)
                scv = self.op("vector", "tensor_scalar", h2f[:, c * 512:(c + 1) * 512],
                              pb[bidx][:, :], T["ca1"][:, 16 + c:17 + c],
                              T["mod_sb"][:, 48 + c:49 + c],
                              ALU.mult, ALU.add, sem=s["vc"])
                self.bankr(bidx, s["vc"], scv)
            sc_h2 = s["vc"].v
            self.bufr("xg", s["tn"], s["tn"].v)
            self.wait("tensor", s["vc"], sc_h2)
            for c in range(16):
                bidx = 5 + c % 2
                self.bankw(bidx)
                self.op("tensor", "matmul", pb[bidx][:, :], T["h128s"][:, :],
                        h2f[:, c * 512:(c + 1) * 512], start=True, stop=True, sem=s["tn"])
                self.wait("scalar", s["tn"], s["tn"].v)
                scz = self.op("scalar", "copy", zb[:, c * 512:(c + 1) * 512], pb[bidx][:, :],
                              sem=s["sc"])
                self.bankr(bidx, s["sc"], scz)
            sc_z = s["sc"].v
            self.wait("vector", s["sc"], sc_z)
            h = 1
            while h < 16:
                v4 = zb[:, 0:16 * 512].rearrange("p (g two u t) -> p g two u t",
                                                 two=2, u=h, t=512)
                av = v4[:, :, 0]
                bv = v4[:, :, 1]
                self.op("vector", "tensor_add", av, av, bv)
                self.op("vector", "scalar_tensor_tensor", bv, bv, -2.0, av,
                        ALU.mult, ALU.add)
                h *= 2
            zv3 = zb[:, 0:16 * 512].rearrange("p (c t) -> p t c", t=512)
            self.op("vector", "tensor_reduce", mx[:, :], zv3, AXX, ALU.max,
                    apply_absolute_value=True, sem=s["vc"])
            self.wait("tensor", s["vc"], s["vc"].v)
            w_mr = self._rowmax(mx, mxc, tmp4, qrow[0:1, 1024:1536], bank=7)
            self.wait("vector", s["lds"], w_mr)
            isq = 1.0 / math.sqrt(E)
            self.op("vector", "tensor_scalar", qrow[0:1, 0:512], qrow[0:1, 1024:1536],
                    isq, QEPS, ALU.mult, ALU.max)
            self.op("vector", "tensor_scalar_mul", T["inv1"][0:1, g * 512:(g + 1) * 512],
                    qrow[0:1, 0:512], 1.0 / 127.0)
            self.op("vector", "reciprocal", qrow[0:1, 512:1024],
                    T["inv1"][0:1, g * 512:(g + 1) * 512])
            v_qs = self.op("vector", "tensor_scalar_mul", qrow[0:1, 512:1024],
                           qrow[0:1, 512:1024], isq, sem=s["vc"])
            self.bufw("tensor", "screp")
            sc_sc = self.bcast(qrow[0:1, 512:1024], screp, 512, 7, wait_vc=v_qs)
            self.wait("vector", s["sc"], sc_sc)
            self.bufw("vector", "xq1")
            zv = zb[:, 0:16 * 512].rearrange("p (c t) -> p c t", t=512)
            sv = screp[:, 0:512].rearrange("p (c t) -> p c t", c=1).broadcast_to((128, 16, 512))
            self.op("vector", "tensor_tensor", zv, zv, sv, ALU.mult)
            v_x1 = self.op("vector", "tensor_scalar", xq1[:, :], zb[:, 0:16 * 512],
                           MAGIC, MAGIC, ALU.add, ALU.subtract, sem=s["vc"])
            self.bufr("screp", s["vc"], v_x1)
            self.wait("scalar", s["vc"], v_x1)
            st_x1 = self.dma("scalar", d["xq1_sc"][g], xq1[:, :], s["st"])
            self.bufr("xq1", s["st"], st_x1)
        self.fence("epi")
        self.free("xq1", "screp", "tmp4", "mxc", "qrow", "mnb", "mx", "zb", "h2f", "epst",
                  "xg", "ust", "sqs3", "usb", "ofm", "obf", "wob", "gag3", "lv", "oa")

    def ffn(self):
        cfg, d, s, pb, T = self.cfg, self.d, self.s, self.pb, self.T
        Q, I8 = cfg.Q, cfg.I8
        OCG, OC16, ICG = cfg.OCG, cfg.OC16, cfg.ICG
        xq1s = self.sb("xq1s", [128, 16 * 512], BF16)
        yb = self.sb("yb", [128, OCG * 512], BF16)
        tg = [self.sb(f"tg{i}", [128, E], BF16) for i in range(2)]
        tu = [self.sb(f"tu{i}", [128, E], BF16) for i in range(2)]
        i1r = self.sb("i1r", [128, 512], F32)
        gf = self.sb("gf", [128, 512], F32)
        sf = self.sb("sf", [128, 512], F32)
        uf = self.sb("uf", [128, 512], F32)
        mx2 = self.sb("mx2", [128, 512], F32)
        mxc2 = self.sb("mxc2", [128, 4], F32)
        tmp42 = self.sb("tmp42", [4, 128], F32)
        q2row = self.sb("q2row", [1, 3 * 512], F32)
        scr2 = self.sb("scr2", [128, 512], F32)
        i2r = self.sb("i2r", [128, 512], F32)
        gbsd = self.sb("gbsd", [128, 16], F32)
        wqT = T["wq"]

        v_gb = self.op("vector", "tensor_scalar_mul", gbsd[:, :], T["gbpf"][:, :],
                       wqT[:, 5:6], sem=s["vc"])

        for fs in range(cfg.n_grp):
            self.bufw("sync", "xq1s")
            x1w = self.dma("sync", xq1s[:, :], d["xq1_sc"][fs], s["lds"])
            self.bufw("tensor", "i1r")
            sc_i1 = self.bcast(T["inv1"][0:1, fs * 512:(fs + 1) * 512], i1r, 512, 7)
            self.wait("vector", s["sc"], sc_i1)
            self.wait("tensor", s["lds"], x1w)
            self.bufw("vector", "yb")
            for oc in range(OCG):
                b = oc % 2
                self.bufw("sync", f"tg{b}")
                self.bufw("sync", f"tu{b}")
                tgw = self.dma("sync", tg[b][:, :], d["tg_sc"][oc], s["lds"])
                tuw = self.dma("sync", tu[b][:, :], d["tu_sc"][oc], s["lds"])
                self.wait("tensor", s["lds"], tgw)
                g_i = oc % 2
                self.bankw(g_i)
                for k in range(16):
                    self.op("tensor", "matmul", pb[g_i][:, :],
                            tg[b][:, k * 128:(k + 1) * 128],
                            xq1s[:, k * 512:(k + 1) * 512],
                            start=(k == 0), stop=(k == 15),
                            sem=(s["tn"] if k == 15 else None))
                t_gm = s["tn"].v
                self.wait("tensor", s["lds"], tuw)
                u_i = 2 + oc % 2
                self.bankw(u_i)
                for k in range(16):
                    self.op("tensor", "matmul", pb[u_i][:, :],
                            tu[b][:, k * 128:(k + 1) * 128],
                            xq1s[:, k * 512:(k + 1) * 512],
                            start=(k == 0), stop=(k == 15),
                            sem=(s["tn"] if k == 15 else None))
                t_um = s["tn"].v
                self.bufr(f"tg{b}", s["tn"], t_gm)
                self.bufr(f"tu{b}", s["tn"], t_um)
                self.wait("vector", s["tn"], t_um)
                self.bufw("vector", "gf")
                self.op("vector", "scalar_tensor_tensor", gf[:, :], pb[g_i][:, :],
                        wqT[:, 3:4], i1r[:, :], ALU.mult, ALU.mult, sem=s["vc"])
                v_gf = s["vc"].v
                self.bankr(g_i, s["vc"], v_gf)
                self.wait("scalar", s["vc"], v_gf)
                sc_sf = self.op("scalar", "activation", sf[:, :], gf[:, :], AF.Silu,
                                sem=s["sc"])
                self.bufw("vector", "uf")
                v_uf = self.op("vector", "scalar_tensor_tensor", uf[:, :], pb[u_i][:, :],
                               wqT[:, 4:5], i1r[:, :], ALU.mult, ALU.mult, sem=s["vc"])
                self.bankr(u_i, s["vc"], v_uf)
                self.wait("vector", s["sc"], sc_sf)
                v_y = self.op("vector", "tensor_mul", yb[:, oc * 512:(oc + 1) * 512],
                              sf[:, :], uf[:, :], sem=s["vc"])
                self.bufr("gf", s["sc"], sc_sf)
                self.bufr("uf", s["vc"], v_y)
                last_v_y = v_y
                last_t_um = t_um
            self.bufr("xq1s", s["tn"], last_t_um)
            self.bufr("i1r", s["vc"], last_v_y)
            if cfg.debug:
                self.wait("scalar", s["vc"], last_v_y)
                self.dma("scalar", d["dbg_y"][fs], yb[:, :], s["st"])
                self.wait("tensor", s["st"], s["st"].v)
            for oc in range(OCG):
                bidx = 4 + oc % 2
                self.bankw(bidx)
                self.op("tensor", "matmul", pb[bidx][:, :], T["h128s"][:, :],
                        yb[:, oc * 512:(oc + 1) * 512], start=True, stop=True, sem=s["tn"])
                self.wait("scalar", s["tn"], s["tn"].v)
                scz = self.op("scalar", "copy", yb[:, oc * 512:(oc + 1) * 512],
                              pb[bidx][:, :], sem=s["sc"])
                self.bankr(bidx, s["sc"], scz)
            sc_za = s["sc"].v
            self.wait("vector", s["sc"], sc_za)
            h = 1
            while h < OCG:
                v4 = yb[:, 0:OCG * 512].rearrange("p (g two u t) -> p g two u t",
                                                  two=2, u=h, t=512)
                av = v4[:, :, 0]
                bv = v4[:, :, 1]
                self.op("vector", "tensor_add", av, av, bv)
                self.op("vector", "scalar_tensor_tensor", bv, bv, -2.0, av,
                        ALU.mult, ALU.add)
                h *= 2
            yv3 = yb[:, 0:OCG * 512].rearrange("p (c t) -> p t c", t=512)
            self.op("vector", "tensor_reduce", mx2[:, :], yv3, AXX, ALU.max,
                    apply_absolute_value=True, sem=s["vc"])
            self.wait("tensor", s["vc"], s["vc"].v)
            w_mr = self._rowmax(mx2, mxc2, tmp42, q2row[0:1, 1024:1536], bank=7)
            self.wait("vector", s["lds"], w_mr)
            isq = 1.0 / math.sqrt(I8)
            self.op("vector", "tensor_scalar", q2row[0:1, 0:512], q2row[0:1, 1024:1536],
                    isq, QEPS, ALU.mult, ALU.max)
            self.op("vector", "tensor_scalar_mul", q2row[0:1, 512:1024],
                    q2row[0:1, 0:512], 1.0 / 127.0)
            self.op("vector", "reciprocal", q2row[0:1, 0:512], q2row[0:1, 512:1024])
            v_q2 = self.op("vector", "tensor_scalar_mul", q2row[0:1, 0:512],
                           q2row[0:1, 0:512], isq, sem=s["vc"])
            self.bufw("tensor", "scr2")
            self.bufw("tensor", "i2r")
            self.bcast(q2row[0:1, 0:512], scr2, 512, 6, wait_vc=v_q2)
            sc_q2 = self.bcast(q2row[0:1, 512:1024], i2r, 512, 7)
            self.wait("vector", s["sc"], sc_q2)
            for c0 in range(0, OCG, 2):
                self.op("vector", "tensor_mul", gf[:, :],
                        yb[:, c0 * 512:(c0 + 1) * 512], scr2[:, :])
                self.op("vector", "tensor_mul", uf[:, :],
                        yb[:, (c0 + 1) * 512:(c0 + 2) * 512], scr2[:, :])
                self.op("vector", "tensor_scalar", yb[:, c0 * 512:(c0 + 1) * 512],
                        gf[:, :], MAGIC, MAGIC, ALU.add, ALU.subtract)
                self.op("vector", "tensor_scalar", yb[:, (c0 + 1) * 512:(c0 + 2) * 512],
                        uf[:, :], MAGIC, MAGIC, ALU.add, ALU.subtract,
                        sem=(s["vc"] if c0 + 2 >= OCG else None))
            v_xq2 = s["vc"].v
            self.bufr("scr2", s["vc"], v_xq2)
            if cfg.debug:
                self.wait("scalar", s["vc"], v_xq2)
                self.dma("scalar", d["dbg_x2"][fs], yb[:, :], s["st"])
                self.dma("scalar", d["dbg_r"][fs][0:1, :], q2row[0:1, 0:512], s["st"])
                self.dma("scalar", d["dbg_r"][fs][1:2, :], q2row[0:1, 512:1024], s["st"])
                self.wait("tensor", s["st"], s["st"].v)
            self.wait("tensor", s["vc"], v_xq2)
            td = [self.T.get("td0") or self.sb("td0", [128, I8], BF16),
                  self.T.get("td1") or self.sb("td1", [128, I8], BF16)]
            dn = self.T.get("dn") or self.sb("dn", [128, OC16 * 512], BF16)
            self.bufw("vector", "dn")
            for oc in range(OC16):
                b = oc % 2
                self.bufw("sync", f"td{b}")
                tdw = self.dma("sync", td[b][:, :], d["td_sc"][oc], s["lds"])
                self.wait("tensor", s["lds"], tdw)
                d_i = 6 + oc % 2
                self.bankw(d_i)
                for k in range(ICG):
                    self.op("tensor", "matmul", pb[d_i][:, :],
                            td[b][:, k * 128:(k + 1) * 128],
                            yb[:, k * 512:(k + 1) * 512],
                            start=(k == 0), stop=(k == ICG - 1),
                            sem=(s["tn"] if k == ICG - 1 else None))
                t_d = s["tn"].v
                self.bufr(f"td{b}", s["tn"], t_d)
                self.wait("vector", s["tn"], t_d)
                if fs == 0 and oc == 0:
                    self.wait("vector", s["vc"], v_gb)
                v_d = self.op("vector", "scalar_tensor_tensor", dn[:, oc * 512:(oc + 1) * 512],
                              pb[d_i][:, :], gbsd[:, oc:oc + 1], i2r[:, :],
                              ALU.mult, ALU.mult, sem=s["vc"])
                self.bankr(d_i, s["vc"], v_d)
                last_t_d = t_d
            v_dn = s["vc"].v
            self.bufr("i2r", s["vc"], v_dn)
            self.bufr("yb", s["tn"], last_t_d)
            if cfg.debug:
                self.wait("scalar", s["vc"], v_dn)
                self.dma("scalar", d["dbg_dn"][fs], dn[:, :], s["st"])
                self.wait("tensor", s["st"], s["st"].v)
            xm = self.T.get("xm") or self.sb("xm", [128, E], F32)
            ob = self.T.get("ob") or self.sb("ob", [128, E], F32)
            self.wait("tensor", s["vc"], v_dn)
            for tgp in range(4):
                self.bufw("sync", "xm")
                xmw = self.dma("sync", xm[:, :], d["xmid_sc"][fs * 4 + tgp], s["lds"])
                for m in range(4):
                    bidx = m % 4
                    self.bankw(bidx)
                    for u in range(4):
                        oc = m * 4 + u
                        self.op("tensor", "transpose", self.pbb(bidx)[:, u * 128:(u + 1) * 128],
                                dn[:, oc * 512 + tgp * 128:oc * 512 + (tgp + 1) * 128],
                                T["s_idb"][:, :],
                                sem=(s["tn"] if u == 3 else None))
                t_tr = s["tn"].v
                self.bufr("dn", s["tn"], t_tr)
                self.wait("vector", s["tn"], t_tr)
                self.wait("vector", s["lds"], xmw)
                self.bufw("vector", "ob")
                for m in range(4):
                    self.op("vector", "tensor_add", ob[:, m * 512:(m + 1) * 512],
                            xm[:, m * 512:(m + 1) * 512], self.pbb(m)[:, 0:512],
                            sem=(s["vc"] if m == 3 else None))
                v_ob = s["vc"].v
                for m in range(4):
                    self.bankr(m, s["vc"], v_ob)
                self.bufr("xm", s["vc"], v_ob)
                self.wait("scalar", s["vc"], v_ob)
                st_o = self.dma("scalar", d["y"][(fs * 4 + tgp) * 128:(fs * 4 + tgp + 1) * 128, :],
                                ob[:, :], s["fin"])
                self.bufr("ob", s["fin"], st_o)
        self.wait("scalar", s["fin"], s["fin"].v)



def _prep_core_inputs(inputs, cfg, core):
    b, hf = core // 2, core % 2
    S, Q = cfg.S, cfg.Q
    x_full = np.asarray(inputs["L0_attn_h"][b], np.float32)
    own = x_full[hf * Q:(hf + 1) * Q]
    oth = x_full[(1 - hf) * Q:(2 - hf) * Q]
    x = np.ascontiguousarray(np.concatenate([own, oth], 0))
    pos = np.concatenate([np.arange(hf * Q, (hf + 1) * Q),
                          np.arange((1 - hf) * Q, (2 - hf) * Q)])
    cosd, sind = _rope_tables(pos)
    return x, cosd, sind


def _shared_arrays(inputs, cfg):
    f32 = lambda a: np.ascontiguousarray(np.asarray(a, np.float32))
    bf = lambda a: np.ascontiguousarray(np.asarray(a).astype(ml_dtypes.bfloat16))
    sh = {}
    sh["crow_all"] = f32(inputs["L0_c"])
    sh["g1r"] = f32(inputs["L0_attn_g1"])[None, :]
    sh["g2r"] = f32(inputs["L0_attn_g2"])[None, :]
    sh["g3r"] = f32(inputs["L0_attn_g3"])[None, :]
    sh["identf"] = np.eye(128, dtype=np.float32)
    sh["identb"] = np.eye(128).astype(ml_dtypes.bfloat16)
    sh["h128"] = bf(_hadamard(128))
    sh["wmod"] = f32(inputs["L0_wmod"])
    sh["wdkv"] = f32(inputs["L0_attn_w_d_kv"].T)
    sh["wdq"] = f32(inputs["L0_attn_w_d_q"].T)
    sh["wqr"] = f32(np.asarray(inputs["L0_attn_w_qr"])[ROPE_PERM].T)
    sh["wkr"] = f32(np.asarray(inputs["L0_attn_w_kr"])[ROPE_PERM].T)
    sh["wuq"] = f32(inputs["L0_attn_w_u_q"])
    sh["wuk"] = f32(inputs["L0_attn_w_u_k"])
    sh["wuv"] = f32(inputs["L0_attn_w_u_v"])
    sh["wot"] = f32(inputs["L0_attn_w_o"].T)
    sh["gtiled"] = _tile_oc(f32(inputs["L0_ffn_w_gate"]))
    sh["utiled"] = _tile_oc(f32(inputs["L0_ffn_w_up"]))
    sh["dtiled"] = _tile_oc(f32(inputs["L0_ffn_w_down"]))
    return sh


def kernel(**inputs):
    cfg = CFG()
    sh = _shared_arrays(inputs, cfg)
    nc = B(cfg).build()
    in_maps = []
    for core in range(8):
        b = core // 2
        x, cosd, sind = _prep_core_inputs(inputs, cfg, core)
        m = {k: sh[k] for k in sh if k != "crow_all"}
        m["crow"] = sh["crow_all"][b][None, :]
        m["x"] = x
        m["cosd"] = cosd
        m["sind"] = sind
        in_maps.append(m)
    res = run_bass_kernel_spmd(nc, in_maps, core_ids=list(range(8)))
    out = np.empty((4, 2 * cfg.Q, E), np.float32)
    for core in range(8):
        b, hf = core // 2, core % 2
        out[b, hf * cfg.Q:(hf + 1) * cfg.Q] = np.asarray(res.results[core]["y"])
    return out
